# version 01
# baseline (speedup 1.0000x reference)
import math
from contextlib import ExitStack

import numpy as np
import concourse.bass as bass
import concourse.mybir as mybir
from concourse.bass_utils import run_bass_kernel_spmd

F32 = mybir.dt.float32
BF16 = mybir.dt.bfloat16
I32 = mybir.dt.int32
AF = mybir.ActivationFunctionType
ALU = mybir.AluOpType
AX = mybir.AxisListType

B, T, D = 8, 4096, 2048
NCH = D // 128
H0, HK0, HD0, WIN = 16, 4, 128, 512
NWIN = T // WIN
FF = 8192
H1, S1, DH1 = 24, 64, 64
DI = H1 * DH1
PD = H1 * (1 + 2 * S1)
NE, FE = 8, 1024

ENGINES = ("sync", "tensor", "vector", "scalar", "gpsimd")
ARENA_F32 = 47616
NEG = -1e30
F32EPS = 1.1920928955078125e-07


class Prog:

    def __init__(self, name, fused_scratch=False):
        self.nc = bass.Bass("TRN2", target_bir_lowering=False,
                            detect_race_conditions=False)
        self.name = name
        self.fused_scratch = fused_scratch
        self.tape = {e: [] for e in ENGINES}
        self.cnt = {}
        self.sem_names = (["pe", "dve", "act", "pool", "fence"]
                          + [f"ld{i}" for i in range(8)]
                          + [f"st{i}" for i in range(8)])
        self._ld_rr = 0
        self._st_rr = 0
        self.sems = {}
        self.ctx = ExitStack()
        self.hbm = {}
        self.hbm_kind = {}
        self._arena = None
        self._arena_off = 0
        self._phase_mark = 0
        self.psum = []
        self.bank_gate = {}
        for s in self.sem_names:
            self.cnt[s] = 0

    def inp(self, name, shape, dt=F32):
        if name not in self.hbm:
            self.hbm[name] = self.nc.dram_tensor(name, list(shape), dt,
                                                 kind="ExternalInput").ap()
            self.hbm_kind[name] = "in"
        return self.hbm[name]

    def out(self, name, shape, dt=F32):
        if name not in self.hbm:
            self.hbm[name] = self.nc.dram_tensor(name, list(shape), dt,
                                                 kind="ExternalOutput").ap()
            self.hbm_kind[name] = "out"
        return self.hbm[name]

    def scratch(self, name, shape, dt=F32, role="rw"):
        if name in self.hbm:
            return self.hbm[name]
        if self.fused_scratch:
            ap = self.nc.dram_tensor(name, list(shape), dt, kind="Internal").ap()
            self.hbm_kind[name] = "internal"
        elif role == "r":
            ap = self.nc.dram_tensor(name, list(shape), dt, kind="ExternalInput").ap()
            self.hbm_kind[name] = "in"
        else:
            ap = self.nc.dram_tensor(name, list(shape), dt, kind="ExternalOutput").ap()
            self.hbm_kind[name] = "out"
        self.hbm[name] = ap
        return ap

    def _ensure_arena(self):
        if self._arena is None:
            self._arena = self.ctx.enter_context(
                self.nc.sbuf_tensor("arena", [128, ARENA_F32], F32))

    def alloc(self, n, dt=F32):
        self._ensure_arena()
        nf = n if dt in (F32, I32) else (n + 1) // 2
        off = (self._arena_off + 127) & ~127
        self._arena_off = off + nf
        assert self._arena_off <= ARENA_F32, (
            f"{self.name}: arena overflow {self._arena_off} > {ARENA_F32}")
        sl = self._arena[:, off:off + nf]
        if dt == F32:
            return sl
        if dt == I32:
            return sl.bitcast(I32)
        return sl.bitcast(BF16)[:, 0:n]

    def phase_begin(self):
        self._phase_mark = self._arena_off

    def phase_end(self):
        self._arena_off = self._phase_mark

    def psum_banks(self):
        if not self.psum:
            for i in range(8):
                self.psum.append(self.ctx.enter_context(
                    self.nc.psum_tensor(f"pb{i}", [128, 512], F32)))
        return self.psum

    def use_bank(self, eng, idx):
        g = self.bank_gate.get(idx)
        if g:
            self.wait(eng, g[0], g[1])

    def set_bank_gate(self, idx, sem, count):
        self.bank_gate[idx] = (sem, count)

    def emit(self, eng, fn):
        self.tape[eng].append(fn)

    def bump(self, sem, k=1):
        self.cnt[sem] += k
        return self.cnt[sem]

    def ldgroup(self):
        self._ld_rr = (self._ld_rr + 1) % 8
        return [f"ld{self._ld_rr}", self.cnt[f"ld{self._ld_rr}"]]

    def stgroup(self):
        self._st_rr = (self._st_rr + 1) % 8
        return [f"st{self._st_rr}", self.cnt[f"st{self._st_rr}"]]

    def dma_g(self, eng, out_ap, in_ap, grp):
        sem = grp[0]
        c = self.bump(sem, 16)
        grp[1] = c
        self.emit(eng, lambda h, o=out_ap, i=in_ap, s=sem: h.dma_start(
            out=o, in_=i).then_inc(self.sems[s], 16))
        return c

    def wait_t(self, eng, grp):
        self.wait(eng, grp[0], grp[1])

    def dma(self, eng, out_ap, in_ap, grp):
        g = self.ldgroup() if grp == "ld" or grp == "ld2" else self.stgroup()
        self.dma_g(eng, out_ap, in_ap, g)
        return g

    def wait(self, eng, sem, count):
        if count <= 0:
            return
        self.emit(eng, lambda h, s=sem, c=count: h.wait_ge(self.sems[s], c))

    def op(self, eng, sem, fn):
        c = self.bump(sem, 1)
        self.emit(eng, lambda h, f=fn, s=sem: f(h).then_inc(self.sems[s], 1))
        return c

    def opq(self, eng, fn):
        self.emit(eng, lambda h, f=fn: f(h))

    def fence(self):
        for e in ENGINES:
            self.bump("fence", 1)
            self.emit(e, lambda h: h.nop().then_inc(self.sems["fence"], 1))
        tgt = self.cnt["fence"]
        for e in ENGINES:
            self.wait(e, "fence", tgt)

    def finalize(self):
        nc = self.nc
        with self.ctx:
            for s in self.sem_names:
                self.sems[s] = self.ctx.enter_context(nc.semaphore(s))
            self.psum_banks()
            with nc.Block() as block:
                @block.sync
                def _(h):
                    for fn in self.tape["sync"]:
                        fn(h)

                @block.tensor
                def _(h):
                    for fn in self.tape["tensor"]:
                        fn(h)

                @block.vector
                def _(h):
                    for fn in self.tape["vector"]:
                        fn(h)

                @block.scalar
                def _(h):
                    for fn in self.tape["scalar"]:
                        fn(h)

                @block.gpsimd
                def _(h):
                    for fn in self.tape["gpsimd"]:
                        fn(h)
        return nc



def host_constants():
    c = {}
    c["IDENT"] = np.eye(128, dtype=np.float32)
    r = np.arange(WIN)[:, None]
    col = np.arange(2 * WIN)[None, :]
    allowed = np.where(col < WIN, col > r, col - WIN <= r)
    c["MASKG"] = np.where(allowed, 0.0, NEG).astype(np.float32)
    c["MASK0"] = np.where((col >= WIN) & (col - WIN <= r), 0.0, NEG).astype(np.float32)
    tau = np.arange(128)[:, None]
    tt = np.arange(128)[None, :]
    c["LTNEG"] = np.where(tau <= tt, 0.0, NEG).astype(np.float32)
    c["EXP32"] = (np.arange(128)[None, :] // 32 == np.arange(4)[:, None]).astype(np.float32)
    c["IOTA8"] = np.tile(np.arange(8, dtype=np.float32)[None, :], (128, 1))
    return c



def ph_ln(p, x_name, xn_name, want_rms=None):
    eps = 1e-5
    p.phase_begin()
    x = p.inp("x", [T, D]) if x_name == "x" else p.scratch(x_name, [T, D], role="r")
    xn = p.scratch(xn_name, [NCH, 128, T], role="w")
    xr = p.scratch(want_rms, [NCH, 128, T], role="w") if want_rms else None
    ident = p.inp("IDENT", [128, 128])

    xt = p.alloc(D)
    sq = p.alloc(D)
    stats = p.alloc(16)
    xnt = p.alloc(D)
    xrt = p.alloc(D) if want_rms else None
    idt = p.alloc(128)
    evac = p.alloc(256)
    pb = p.psum_banks()

    g_id = p.dma("sync", idt, ident[:, :], "ld")
    p.wait_t("tensor", g_id)

    g_store = None
    ntile = T // 128
    for i in range(ntile):
        if g_store:
            p.wait_t("sync", g_store)
        g_x = p.dma("sync", xt, x[i * 128:(i + 1) * 128, :], "ld")
        p.wait_t("vector", g_x)
        c0 = p.op("vector", "dve", lambda h, o=stats[:, 0:1], a=xt: h.reduce_sum(o, a, axis=AX.X))
        p.wait("vector", "dve", c0)
        c0b = p.op("vector", "dve", lambda h, o=stats[:, 1:2], a=stats[:, 0:1]: h.tensor_scalar(
            o, a, 1.0 / D, None, ALU.mult))
        p.wait("vector", "dve", c0b)
        c1 = p.op("vector", "dve", lambda h, o=xnt, a=xt, m=stats[:, 1:2]: h.tensor_scalar(
            o, a, m, None, ALU.subtract))
        p.wait("scalar", "dve", c1)
        c2 = p.op("scalar", "act", lambda h, o=sq, a=xnt: h.activation(o, a, AF.Square))
        p.wait("vector", "act", c2)
        c2b = p.op("vector", "dve", lambda h, o=stats[:, 2:3], a=sq: h.reduce_sum(o, a, axis=AX.X))
        p.wait("vector", "dve", c2b)
        c2c = p.op("vector", "dve", lambda h, o=stats[:, 3:4], a=stats[:, 2:3]: h.tensor_scalar(
            o, a, 1.0 / D, eps, ALU.mult, ALU.add))
        p.wait("vector", "dve", c2c)
        c3 = p.op("vector", "dve", lambda h, o=stats[:, 4:5], a=stats[:, 3:4]: h.reciprocal(o, a))
        p.wait("scalar", "dve", c3)
        c4 = p.op("scalar", "act", lambda h, o=stats[:, 5:6], a=stats[:, 4:5]: h.activation(
            o, a, AF.Sqrt))
        p.wait("vector", "act", c4)
        cln = p.op("vector", "dve", lambda h, o=xnt, a=xnt, m=stats[:, 5:6]: h.tensor_scalar(
            o, a, m, None, ALU.mult))
        clast = cln
        if want_rms:
            p.wait("scalar", "dve", cln)
            c6 = p.op("scalar", "act", lambda h, o=sq, a=xnt: h.activation(o, a, AF.Square))
            p.wait("vector", "act", c6)
            c6b = p.op("vector", "dve", lambda h, o=stats[:, 6:7], a=sq: h.reduce_sum(o, a, axis=AX.X))
            p.wait("vector", "dve", c6b)
            c6c = p.op("vector", "dve", lambda h, o=stats[:, 7:8], a=stats[:, 6:7]: h.tensor_scalar(
                o, a, 1.0 / D, F32EPS, ALU.mult, ALU.add))
            p.wait("vector", "dve", c6c)
            c7 = p.op("vector", "dve", lambda h, o=stats[:, 8:9], a=stats[:, 7:8]: h.reciprocal(o, a))
            p.wait("scalar", "dve", c7)
            c8 = p.op("scalar", "act", lambda h, o=stats[:, 9:10], a=stats[:, 8:9]: h.activation(
                o, a, AF.Sqrt))
            p.wait("vector", "act", c8)
            clast = p.op("vector", "dve", lambda h, o=xrt, a=xnt, m=stats[:, 9:10]:
                         h.tensor_scalar(o, a, m, None, ALU.mult))
        p.wait("tensor", "dve", clast)
        g_store = p.stgroup()
        for c in range(NCH):
            bank = pb[c % 2]
            p.use_bank("tensor", c % 2)
            cm = p.op("tensor", "pe", lambda h, o=bank[0:128, 0:128],
                      a=xnt[:, c * 128:(c + 1) * 128], i=idt: h.transpose(o, a, i))
            p.wait("scalar", "pe", cm)
            ce = p.op("scalar", "act", lambda h, o=evac[:, 0:128], a=bank[0:128, 0:128]:
                      h.copy(o, a))
            if not want_rms:
                p.set_bank_gate(c % 2, "act", ce)
            gs = p.dma("scalar", xn[c, :, i * 128:(i + 1) * 128], evac[:, 0:128], "st")
            p.wait_t("scalar", gs)
            if want_rms:
                cm2 = p.op("tensor", "pe", lambda h, o=bank[0:128, 128:256],
                           a=xrt[:, c * 128:(c + 1) * 128], i=idt: h.transpose(o, a, i))
                p.wait("scalar", "pe", cm2)
                ce2 = p.op("scalar", "act", lambda h, o=evac[:, 128:256], a=bank[0:128, 128:256]:
                           h.copy(o, a))
                p.set_bank_gate(c % 2, "act", ce2)
                gs = p.dma("scalar", xr[c, :, i * 128:(i + 1) * 128], evac[:, 128:256], "st")
                p.wait_t("scalar", gs)
        g_store = gs
    p.fence()
    p.phase_end()


def ph_matmul_fm(p, w_name, w_shape, in_name, out_name, n_out,
                 transpose_w=False, bias_name=None, tb=512, out_scale=None,
                 rhs_scale_name=None, mslice=None):
    K = w_shape[0] if not transpose_w else w_shape[1]
    nm_all = (n_out + 127) // 128
    m0, M = (0, n_out) if mslice is None else mslice
    assert m0 % 128 == 0
    nk = K // 128
    nm = (M + 127) // 128
    cm0 = m0 // 128
    p.phase_begin()
    w = p.inp(w_name, list(w_shape))
    xin = p.scratch(in_name, [nk, 128, T], role="r")
    xout = p.scratch(out_name, [nm_all, 128, T], role="w")
    bias = p.inp(bias_name, [n_out]) if bias_name else None
    rsv = p.inp(rhs_scale_name, [K]) if rhs_scale_name else None

    wt = p.alloc(nk * M)
    xt = p.alloc(nk * tb)
    ev = p.alloc(2 * tb)
    bt = p.alloc(nm) if bias_name else None
    rst = p.alloc(nk) if rhs_scale_name else None
    pb = p.psum_banks()

    c_dve_w = 0
    g_w = p.ldgroup()
    if not transpose_w:
        for k in range(nk):
            p.dma_g("sync", wt[:, k * M:(k + 1) * M],
                    w[k * 128:(k + 1) * 128, m0:m0 + M], g_w)
    else:
        ident = p.inp("IDENT", [128, 128])
        idt = p.alloc(128)
        wtmp = p.alloc(K)
        g_id = p.dma("sync", idt, ident[:, :], "ld")
        p.wait_t("tensor", g_id)
        for m in range(nm):
            mrows = min(128, M - m * 128)
            g_wt = p.dma("sync", wtmp[0:mrows, :],
                         w[m0 + m * 128:m0 + m * 128 + mrows, :], "ld")
            p.wait_t("tensor", g_wt)
            for k in range(nk):
                bidx = 6 + (k % 2)
                bank = pb[bidx]
                p.use_bank("tensor", bidx)
                cm = p.op("tensor", "pe", lambda h, o=bank[0:128, 0:128],
                          a=wtmp[0:mrows, k * 128:(k + 1) * 128], i=idt, mr=mrows:
                          h.transpose(o[0:128, 0:mr], a, i[0:mr, 0:mr]))
                p.wait("vector", "pe", cm)
                c_dve_w = p.op("vector", "dve",
                               lambda h, o=wt[:, k * M + m * 128:k * M + m * 128 + mrows],
                               a=bank[0:128, 0:mrows]: h.tensor_copy(o, a))
                p.set_bank_gate(bidx, "dve", c_dve_w)
            p.wait("sync", "dve", c_dve_w)
    if bias_name:
        for m in range(nm):
            mrows = min(128, M - m * 128)
            p.dma_g("sync", bt[0:mrows, m:m + 1],
                    bias[m0 + m * 128:m0 + m * 128 + mrows].unsqueeze(1), g_w)
    if rhs_scale_name:
        for k in range(nk):
            p.dma_g("sync", rst[0:128, k:k + 1],
                    rsv[k * 128:(k + 1) * 128].unsqueeze(1), g_w)

    ntb = T // tb
    p.wait_t("scalar", g_w)
    p.wait_t("vector", g_w)
    p.wait_t("tensor", g_w)
    last_mm = 0
    gs_slot = [None, None]
    for it in range(ntb):
        if last_mm:
            p.wait("sync", "pe", last_mm)
        g_x = p.ldgroup()
        for k in range(nk):
            p.dma_g("sync", xt[:, k * tb:(k + 1) * tb],
                    xin[k, :, it * tb:(it + 1) * tb], g_x)
        if rhs_scale_name:
            p.wait_t("vector", g_x)
            cvs = 0
            for k in range(nk):
                cvs = p.op("vector", "dve", lambda h, o=xt[:, k * tb:(k + 1) * tb],
                           a=xt[:, k * tb:(k + 1) * tb], m=rst[0:128, k:k + 1]:
                           h.tensor_scalar(o, a, m, None, ALU.mult))
            p.wait("tensor", "dve", cvs)
        p.wait_t("tensor", g_x)
        if transpose_w:
            p.wait("tensor", "dve", c_dve_w)
        for m in range(nm):
            mrows = min(128, M - m * 128)
            bank = pb[m % 2]
            p.use_bank("tensor", m % 2)
            for k in range(nk):
                cm = p.op("tensor", "pe", lambda h, o=bank[0:mrows, 0:tb],
                          a=wt[:, k * M + m * 128:k * M + m * 128 + mrows],
                          b=xt[:, k * tb:(k + 1) * tb], s=(k == 0), e=(k == nk - 1):
                          h.matmul(o, a, b, start=s, stop=e))
            last_mm = cm
            eslot = ev[:, (m % 2) * tb:(m % 2) * tb + tb]
            if bias_name:
                p.wait("vector", "pe", cm)
                if gs_slot[m % 2]:
                    p.wait_t("vector", gs_slot[m % 2])
                ce = p.op("vector", "dve", lambda h, o=eslot, a=bank[0:mrows, 0:tb],
                          bb=bt[0:mrows, m:m + 1], mr=mrows:
                          h.tensor_scalar(o[0:mr, :], a, bb, None, ALU.add))
                p.set_bank_gate(m % 2, "dve", ce)
                p.wait("scalar", "dve", ce)
            else:
                p.wait("scalar", "pe", cm)
                if out_scale is not None:
                    ce = p.op("scalar", "act", lambda h, o=eslot, a=bank[0:mrows, 0:tb],
                              mr=mrows: h.activation(o[0:mr, :], a, AF.Copy, scale=out_scale))
                else:
                    ce = p.op("scalar", "act", lambda h, o=eslot, a=bank[0:mrows, 0:tb],
                              mr=mrows: h.copy(o[0:mr, :], a))
                p.set_bank_gate(m % 2, "act", ce)
            gs = p.dma("scalar", xout[cm0 + m, 0:mrows, it * tb:(it + 1) * tb],
                       eslot[0:mrows, :], "st")
            p.wait_t("scalar", gs)
            gs_slot[m % 2] = gs
    p.fence()
    p.phase_end()


def ph_attn(p):
    nwin = T // WIN
    scale = 1.0 / math.sqrt(HD0)
    p.phase_begin()
    q_fm = p.scratch("q_fm", [H0, 128, T], role="r")
    k_fm = p.scratch("k_fm", [HK0, 128, T], role="r")
    v_fm = p.scratch("v_fm", [HK0, 128, T], role="r")
    attn_o = p.scratch("attn_o", [H0, 128, T], role="w")
    maskg = p.inp("MASKG", [WIN, 2 * WIN])
    mask0 = p.inp("MASK0", [WIN, 2 * WIN])
    ident = p.inp("IDENT", [128, 128])

    kc = [p.alloc(1024) for _ in range(HK0)]
    vt = [p.alloc(1024) for _ in range(HK0)]
    vstg = p.alloc(512)
    qh = p.alloc(512)
    sc = p.alloc(1024)
    pp = p.alloc(1024)
    pt = p.alloc(8 * 512)
    oe = p.alloc(512)
    mskg = p.alloc(4 * 1024)
    msk0 = p.alloc(4 * 1024)
    stats = p.alloc(8)
    idt = p.alloc(128)
    pb = p.psum_banks()

    g_const = p.ldgroup()
    p.dma_g("sync", idt, ident[:, :], g_const)
    for sub in range(4):
        p.dma_g("sync", mskg[:, sub * 1024:(sub + 1) * 1024],
                maskg[sub * 128:(sub + 1) * 128, :], g_const)
        p.dma_g("sync", msk0[:, sub * 1024:(sub + 1) * 1024],
                mask0[sub * 128:(sub + 1) * 128, :], g_const)
    p.wait_t("tensor", g_const)
    p.wait_t("vector", g_const)
    cz = 0
    for g in range(HK0):
        p.op("vector", "dve", lambda h, o=kc[g][:, 0:512]: h.memset(o, 0.0))
        cz = p.op("vector", "dve", lambda h, o=vt[g][:, 0:512]: h.memset(o, 0.0))
    p.wait("tensor", "dve", cz)

    last_pv = 0
    for n in range(nwin):
        msk = msk0 if n == 0 else mskg
        for g in range(HK0):
            if last_pv:
                p.wait("sync", "pe", last_pv)
            g_k = p.ldgroup()
            if n > 0:
                p.dma_g("sync", kc[g][:, 0:512], k_fm[g, :, (n - 1) * WIN:n * WIN], g_k)
            p.dma_g("sync", kc[g][:, 512:1024], k_fm[g, :, n * WIN:(n + 1) * WIN], g_k)
            halves = ([1] if n == 0 else [0, 1])
            c_vt = 0
            last_tr = 0
            for hv in halves:
                wv = n - 1 + hv
                if last_tr:
                    p.wait("sync", "pe", last_tr)
                g_v = p.dma("sync", vstg, v_fm[g, :, wv * WIN:(wv + 1) * WIN], "ld")
                p.wait_t("tensor", g_v)
                for j in range(4):
                    bidx = 5 + (j % 2)
                    bank = pb[bidx]
                    p.use_bank("tensor", bidx)
                    cm = p.op("tensor", "pe", lambda h, o=bank[0:128, 0:128],
                              a=vstg[:, j * 128:(j + 1) * 128], i=idt: h.transpose(o, a, i))
                    p.wait("scalar", "pe", cm)
                    c_vt = p.op("scalar", "act", lambda h,
                                o=vt[g][:, (hv * 4 + j) * 128:(hv * 4 + j + 1) * 128],
                                a=bank[0:128, 0:128]: h.copy(o, a))
                    p.set_bank_gate(bidx, "act", c_vt)
                last_tr = cm
            p.wait_t("tensor", g_k)
            p.wait("tensor", "act", c_vt)
            for r in range(4):
                hq = g * 4 + r
                if last_pv:
                    p.wait("sync", "pe", last_pv)
                g_q = p.dma("sync", qh, q_fm[hq, :, n * WIN:(n + 1) * WIN], "ld")
                p.wait_t("tensor", g_q)
                cpt = 0
                for sub in range(4):
                    p.use_bank("tensor", 0)
                    p.use_bank("tensor", 1)
                    p.op("tensor", "pe", lambda h, o=pb[0][0:128, 0:512],
                         a=qh[:, sub * 128:(sub + 1) * 128], b=kc[g][:, 0:512]:
                         h.matmul(o, a, b, start=True, stop=True))
                    cm1 = p.op("tensor", "pe", lambda h, o=pb[1][0:128, 0:512],
                               a=qh[:, sub * 128:(sub + 1) * 128], b=kc[g][:, 512:1024]:
                               h.matmul(o, a, b, start=True, stop=True))
                    p.wait("vector", "pe", cm1)
                    cd0 = p.op("vector", "dve", lambda h, o=sc[:, 0:512],
                               a=pb[0][0:128, 0:512], m=msk[:, sub * 1024:sub * 1024 + 512]:
                               h.scalar_tensor_tensor(o, a, scale, m, ALU.mult, ALU.add))
                    p.set_bank_gate(0, "dve", cd0)
                    cd1 = p.op("vector", "dve", lambda h, o=sc[:, 512:1024],
                               a=pb[1][0:128, 0:512], m=msk[:, sub * 1024 + 512:sub * 1024 + 1024]:
                               h.scalar_tensor_tensor(o, a, scale, m, ALU.mult, ALU.add))
                    p.set_bank_gate(1, "dve", cd1)
                    cd2 = p.op("vector", "dve", lambda h, o=stats[:, 0:1], a=sc:
                               h.reduce_max(o, a, axis=AX.X, negate=True))
                    p.wait("scalar", "dve", cd2)
                    ce = p.op("scalar", "act", lambda h, o=pp, a=sc, b=stats[:, 0:1]:
                              h.activation(o, a, AF.Exp, bias=b))
                    p.wait("vector", "act", ce)
                    cd3 = p.op("vector", "dve", lambda h, o=stats[:, 1:2], a=pp:
                               h.reduce_sum(o, a, axis=AX.X))
                    p.wait("vector", "dve", cd3)
                    cd4 = p.op("vector", "dve", lambda h, o=stats[:, 2:3], a=stats[:, 1:2]:
                               h.reciprocal(o, a))
                    p.wait("vector", "dve", cd4)
                    cd5 = p.op("vector", "dve", lambda h, o=pp, a=pp, m=stats[:, 2:3]:
                               h.tensor_scalar(o, a, m, None, ALU.mult))
                    p.wait("tensor", "dve", cd5)
                    for j in range(8):
                        bidx = 2 + (j % 2)
                        bank = pb[bidx]
                        p.use_bank("tensor", bidx)
                        cm = p.op("tensor", "pe", lambda h, o=bank[0:128, 0:128],
                                  a=pp[:, j * 128:(j + 1) * 128], i=idt: h.transpose(o, a, i))
                        p.wait("scalar", "pe", cm)
                        cpt = p.op("scalar", "act", lambda h,
                                   o=pt[:, j * 512 + sub * 128:j * 512 + sub * 128 + 128],
                                   a=bank[0:128, 0:128]: h.copy(o, a))
                        p.set_bank_gate(bidx, "act", cpt)
                p.use_bank("tensor", 4)
                p.wait("tensor", "act", cpt)
                for j in range(8):
                    cm = p.op("tensor", "pe", lambda h, o=pb[4][0:128, 0:512],
                              a=vt[g][:, j * 128:(j + 1) * 128], b=pt[:, j * 512:(j + 1) * 512],
                              s=(j == 0), e=(j == 7): h.matmul(o, a, b, start=s, stop=e))
                last_pv = cm
                p.wait("scalar", "pe", cm)
                co = p.op("scalar", "act", lambda h, o=oe, a=pb[4][0:128, 0:512]: h.copy(o, a))
                p.set_bank_gate(4, "act", co)
                gs = p.dma("scalar", attn_o[hq, :, n * WIN:(n + 1) * WIN], oe, "st")
                p.wait_t("scalar", gs)
    p.fence()
    p.phase_end()




def ph_merge(p, out_name, base_name, terms, base_is_input=False):
    p.phase_begin()
    base = (p.inp(base_name, [T, D]) if base_is_input
            else p.scratch(base_name, [T, D], role="r"))
    outt = p.scratch(out_name, [T, D], role="w")
    fms = [p.scratch(nm, [NCH, 128, T], role="r") for nm, _ in terms]
    ident = p.inp("IDENT", [128, 128])
    need_rg = any(k == "rg" for _, k in terms)

    xb = p.alloc(D)
    ft = [p.alloc(D) for _ in terms]
    idt = p.alloc(128)
    rgc = p.alloc(4) if need_rg else None
    onec = p.alloc(128) if need_rg else None
    pb = p.psum_banks()

    g_c = p.ldgroup()
    p.dma_g("sync", idt, ident[:, :], g_c)
    if need_rg:
        rg = p.inp("L1_attn_res_gate", [1])
        p.dma_g("sync", rgc[0:1, 0:1], rg[0:1].unsqueeze(1), g_c)
    p.wait_t("tensor", g_c)
    p.wait_t("vector", g_c)
    if need_rg:
        cz = p.op("vector", "dve", lambda h, o=onec[0:1, 0:128]: h.memset(o, 1.0))
        p.wait("tensor", "dve", cz)
        cmb = p.op("tensor", "pe", lambda h, o=pb[3][0:128, 0:1], a=onec[0:1, 0:128],
                   b=rgc[0:1, 0:1]: h.matmul(o, a, b, start=True, stop=True))
        p.wait("vector", "pe", cmb)
        crg = p.op("vector", "dve", lambda h, o=rgc[:, 1:2], a=pb[3][0:128, 0:1]:
                   h.tensor_copy(o, a))
        p.set_bank_gate(3, "dve", crg)

    ntile = T // 128
    last_gs = None
    for i in range(ntile):
        if last_gs:
            p.wait_t("sync", last_gs)
        g_x = p.ldgroup()
        p.dma_g("sync", xb, base[i * 128:(i + 1) * 128, :], g_x)
        for t_i, (nm, _) in enumerate(terms):
            p.dma_g("sync", ft[t_i].rearrange("p (c t) -> p c t", t=128),
                    fms[t_i][:, :, i * 128:(i + 1) * 128].rearrange("c p t -> p c t"), g_x)
        p.wait_t("tensor", g_x)
        p.wait_t("vector", g_x)
        for c in range(NCH):
            for t_i, (nm, kind) in enumerate(terms):
                bidx = t_i % 2
                bank = pb[bidx]
                p.use_bank("tensor", bidx)
                cm = p.op("tensor", "pe", lambda h, o=bank[0:128, 0:128],
                          a=ft[t_i][:, c * 128:(c + 1) * 128], i=idt: h.transpose(o, a, i))
                p.wait("vector", "pe", cm)
                if kind == "rg":
                    cd = p.op("vector", "dve", lambda h, o=xb[:, c * 128:(c + 1) * 128],
                              a=bank[0:128, 0:128], m=rgc[:, 1:2], x2=xb[:, c * 128:(c + 1) * 128]:
                              h.scalar_tensor_tensor(o, a, m, x2, ALU.mult, ALU.add))
                else:
                    cd = p.op("vector", "dve", lambda h, o=xb[:, c * 128:(c + 1) * 128],
                              a=bank[0:128, 0:128], x2=xb[:, c * 128:(c + 1) * 128]:
                              h.tensor_tensor(o, a, x2, ALU.add))
                p.set_bank_gate(bidx, "dve", cd)
        p.wait("scalar", "dve", cd)
        gs = p.dma("scalar", outt[i * 128:(i + 1) * 128, :], xb, "st")
        p.wait_t("scalar", gs)
        last_gs = gs
    p.fence()
    p.phase_end()


def _deq_weight(p, w_name, Kd, Nd, out_name, sc_name):
    nkc = Kd // 128
    nkb = Kd // 32
    nnc = Nd // 128
    w = p.inp(w_name, [Kd, Nd])
    wout = p.scratch(out_name, [nkc, 128, Nd], BF16, role="w")
    schbm = p.scratch(sc_name, [nkb, Nd], role="w")
    sihbm = p.scratch(sc_name + "_inv", [nkb, Nd], role="w")
    ident = p.inp("IDENT", [128, 128])
    exp32 = p.inp("EXP32", [4, 128])
    pb = p.psum_banks()

    wblk = p.alloc(128)
    wT = p.alloc(128)
    scT = p.alloc(nkc * 4)
    scTi = p.alloc(nkc * 4, I32)
    msk_m = p.alloc(nkc * 4, I32)
    idt = p.alloc(128)
    e32 = p.alloc(128)
    ev = p.alloc(128)

    g_c = p.ldgroup()
    p.dma_g("sync", idt, ident[:, :], g_c)
    p.dma_g("sync", e32[0:4, :], exp32[:, :], g_c)
    p.wait_t("tensor", g_c)
    p.wait_t("vector", g_c)
    cmsk = p.op("vector", "dve", lambda h, o=msk_m: h.memset(o, -8388608))

    last_sc_store = None
    last_ce = 0
    for nn in range(nnc):
        if last_sc_store:
            p.wait_t("sync", last_sc_store)
        if last_ce:
            p.wait("vector", "act", last_ce)
        prev_evt = 0
        for kk in range(nkc):
            if prev_evt:
                p.wait("sync", "dve", prev_evt)
            g_wb = p.dma("sync", wblk, w[kk * 128:(kk + 1) * 128,
                                         nn * 128:(nn + 1) * 128], "ld")
            p.wait_t("tensor", g_wb)
            bidx = 0 if kk % 2 == 0 else 1
            bank = pb[bidx]
            p.use_bank("tensor", bidx)
            cm = p.op("tensor", "pe", lambda h, o=bank[0:128, 0:128], a=wblk, i=idt:
                      h.transpose(o, a, i))
            p.wait("vector", "pe", cm)
            cd = p.op("vector", "dve", lambda h, o=wT, a=bank[0:128, 0:128]:
                      h.tensor_copy(o, a))
            p.set_bank_gate(bidx, "dve", cd)
            p.wait("vector", "dve", cd)
            prev_evt = p.op("vector", "dve", lambda h, o=scT[:, kk * 4:(kk + 1) * 4],
                            a=wT.rearrange("p (g k) -> p g k", k=32):
                            h.reduce_max(o, a, axis=AX.X, apply_absolute_value=True))
        p.wait("vector", "dve", prev_evt)
        ci1 = p.op("vector", "dve", lambda h, o=scTi, a=scT.bitcast(I32), m=msk_m:
                   h.tensor_tensor(o, a, m, ALU.bitwise_and))
        p.wait("vector", "dve", ci1)
        ci2 = p.op("vector", "dve", lambda h, o=scT, a=scTi.bitcast(F32):
                   h.tensor_scalar(o, a, 0.25, None, ALU.mult))
        p.wait("vector", "dve", ci2)
        ci3 = p.op("vector", "dve", lambda h, o=scTi, a=scT.bitcast(I32):
                   h.tensor_scalar(o, a, -1.0, 2130706432.0, ALU.mult, ALU.add))
        p.wait("tensor", "dve", ci3)
        nkb_cols = nkc * 4
        for src, dsthbm in ((scT, schbm), (scTi.bitcast(F32), sihbm)):
            for q0 in range(0, nkb_cols, 128):
                qn = min(128, nkb_cols - q0)
                bank = pb[2]
                p.use_bank("tensor", 2)
                cm = p.op("tensor", "pe", lambda h, o=bank[0:128, 0:128], a=src[:, q0:q0 + qn],
                          i=idt, q=qn: h.transpose(o[0:q, 0:128], a, i))
                p.wait("scalar", "pe", cm)
                ce = p.op("scalar", "act", lambda h, o=ev[0:128, 0:128],
                          a=bank[0:128, 0:128], q=qn: h.copy(o[0:q, :], a[0:q, :]))
                p.set_bank_gate(2, "act", ce)
                last_ce = ce
                last_sc_store = p.dma("scalar", dsthbm[q0:q0 + qn, nn * 128:(nn + 1) * 128],
                                      ev[0:qn, 0:128], "st")
                p.wait_t("scalar", last_sc_store)

    p.wait_t("sync", last_sc_store)
    CW = min(Nd, 4096)
    nch_col = Nd // CW
    wsl = p.alloc(CW)
    scn = p.alloc(CW)
    inv = p.alloc(CW)
    aa = p.alloc(CW)
    t1 = p.alloc(CW, I32)
    low = p.alloc(CW)
    lowb = p.alloc(CW)
    hmk = p.alloc(CW, I32)
    sgn = p.alloc(CW)
    obf = p.alloc(CW, BF16)
    last_b = None
    for kk in range(nkc):
        for cc in range(nch_col):
            if last_b:
                p.wait_t("sync", last_b)
            g_w = p.ldgroup()
            p.dma_g("sync", wsl, w[kk * 128:(kk + 1) * 128, cc * CW:(cc + 1) * CW], g_w)
            g_s = p.ldgroup()
            p.dma_g("sync", lowb[0:4, 0:CW],
                    schbm[kk * 4:(kk + 1) * 4, cc * CW:(cc + 1) * CW], g_s)
            p.dma_g("sync", hmk.bitcast(F32)[0:4, 0:CW],
                    sihbm[kk * 4:(kk + 1) * 4, cc * CW:(cc + 1) * CW], g_s)
            p.wait_t("tensor", g_s)
            nslice = CW // 512
            cdx = 0
            for src4, dstfull in ((lowb[0:4, 0:CW], scn), (hmk.bitcast(F32)[0:4, 0:CW], inv)):
                for jj in range(nslice):
                    bidx = 3 + (jj % 2)
                    bank = pb[bidx]
                    p.use_bank("tensor", bidx)
                    cm = p.op("tensor", "pe", lambda h, o=bank[0:128, 0:512],
                              a=e32[0:4, :], b=src4[:, jj * 512:(jj + 1) * 512]:
                              h.matmul(o, a, b, start=True, stop=True))
                    p.wait("vector", "pe", cm)
                    cdx = p.op("vector", "dve", lambda h, o=dstfull[:, jj * 512:(jj + 1) * 512],
                               a=bank[0:128, 0:512]: h.tensor_copy(o, a))
                    p.set_bank_gate(bidx, "dve", cdx)
            p.wait("vector", "dve", cdx)
            p.wait_t("vector", g_w)
            c2 = cdx
            p.wait_t("scalar", g_w)
            c3 = p.op("scalar", "act", lambda h, o=aa, a=wsl: h.activation(o, a, AF.Abs))
            c3b = p.op("scalar", "act", lambda h, o=sgn, a=wsl: h.activation(o, a, AF.Sign))
            p.wait("vector", "act", c3b)
            c4 = p.op("vector", "dve", lambda h, o=aa, a=aa, b=inv: h.tensor_mul(o, a, b))
            p.wait("vector", "dve", c4)
            cch = p.op("vector", "dve", lambda h, o=low, a=aa: h.tensor_scalar(
                o, a, 0.25, None, ALU.is_ge))
            for thr in (0.75, 1.25, 1.75, 2.5, 2.5, 3.5, 3.5, 5.0, 5.0, 5.0, 5.0):
                p.wait("vector", "dve", cch)
                cch = p.op("vector", "dve", lambda h, o=low, a=aa, b=low, t=thr:
                           h.scalar_tensor_tensor(o, a, t, b, ALU.is_ge, ALU.add))
            p.wait("vector", "dve", cch)
            c13 = p.op("vector", "dve", lambda h, o=aa, a=low, b=sgn: h.tensor_mul(o, a, b))
            p.wait("vector", "dve", c13)
            c14 = p.op("vector", "dve", lambda h, o=obf, a=aa, b=scn:
                       h.scalar_tensor_tensor(o, a, 0.5, b, ALU.mult, ALU.mult))
            p.wait("scalar", "dve", c14)
            last_b = p.dma("scalar", wout[kk, :, cc * CW:(cc + 1) * CW], obf, "st")
            p.wait_t("scalar", last_b)


def ph_deq(p):
    p.phase_begin()
    _deq_weight(p, "L0_ffn_up_kernel", D, FF, "wup_bf", "sc_up")
    p.fence()
    p.phase_end()
    p.phase_begin()
    _deq_weight(p, "L0_ffn_down_kernel", FF, D, "wdn_bf", "sc_dn")
    p.fence()
    p.phase_end()


def ph_ffn(p):
    tb = 512
    p.phase_begin()
    xn = p.scratch("xn0", [NCH, 128, T], role="r")
    wup = p.scratch("wup_bf", [NCH, 128, FF], BF16, role="r")
    wdn = p.scratch("wdn_bf", [FF // 128, 128, D], BF16, role="r")
    bup = p.inp("L0_ffn_up_bias", [FF])
    bdn = p.inp("L0_ffn_down_bias", [D])
    out = p.scratch("ffn_out", [NCH, 128, T], role="w")

    NQ = 4
    QH = FF // NQ
    nmq = QH // 128

    wbuf = p.alloc(16 * QH, BF16)
    xhi = p.alloc(16 * tb, BF16)
    xlo = p.alloc(16 * tb, BF16)
    hhi = p.alloc(nmq * tb, BF16)
    hlo = p.alloc(nmq * tb, BF16)
    acc = p.alloc(16 * tb)
    ght = p.alloc(tb)
    ev = p.alloc(tb)
    bupt = p.alloc(FF // 128)
    bdnt = p.alloc(16)
    pb = p.psum_banks()

    g_b = p.ldgroup()
    for m in range(FF // 128):
        p.dma_g("sync", bupt[:, m:m + 1], bup[m * 128:(m + 1) * 128].unsqueeze(1), g_b)
    for m in range(16):
        p.dma_g("sync", bdnt[:, m:m + 1], bdn[m * 128:(m + 1) * 128].unsqueeze(1), g_b)
    p.wait_t("scalar", g_b)
    p.wait_t("vector", g_b)

    last_up_mm = 0
    last_dn_mm = 0
    for it in range(T // tb):
        if last_up_mm:
            p.wait("sync", "pe", last_up_mm)
        cspl = 0
        for k in range(NCH):
            if cspl:
                p.wait("sync", "dve", cspl)
            g_k = p.dma("sync", ght, xn[k, :, it * tb:(it + 1) * tb], "ld")
            p.wait_t("vector", g_k)
            ch = p.op("vector", "dve", lambda h, o=xhi[:, k * tb:(k + 1) * tb], a=ght:
                      h.tensor_copy(o, a))
            p.wait("vector", "dve", ch)
            cspl = p.op("vector", "dve", lambda h, o=xlo[:, k * tb:(k + 1) * tb], a=ght,
                        b=xhi[:, k * tb:(k + 1) * tb]: h.tensor_sub(o, a, b))
        p.wait("tensor", "dve", cspl)
        for q in range(NQ):
            if last_up_mm:
                p.wait("sync", "pe", max(last_up_mm, last_dn_mm))
            g_w = p.ldgroup()
            for k in range(NCH):
                p.dma_g("sync", wbuf[:, k * QH:(k + 1) * QH],
                        wup[k, :, q * QH:(q + 1) * QH], g_w)
            p.wait_t("tensor", g_w)
            chl = 0
            for m in range(nmq):
                bidx = m % 2
                bank = pb[bidx]
                p.use_bank("tensor", bidx)
                for k in range(NCH):
                    cm = p.op("tensor", "pe", lambda h, o=bank[0:128, 0:tb],
                              a=wbuf[:, k * QH + m * 128:k * QH + m * 128 + 128],
                              b=xhi[:, k * tb:(k + 1) * tb], s=(k == 0):
                              h.matmul(o, a, b, start=s, stop=False))
                    cm = p.op("tensor", "pe", lambda h, o=bank[0:128, 0:tb],
                              a=wbuf[:, k * QH + m * 128:k * QH + m * 128 + 128],
                              b=xlo[:, k * tb:(k + 1) * tb], e=(k == NCH - 1):
                              h.matmul(o, a, b, start=False, stop=e))
                last_up_mm = cm
                p.wait("scalar", "pe", cm)
                cg = p.op("scalar", "act", lambda h, o=ght, a=bank[0:128, 0:tb],
                          bb=bupt[:, q * nmq + m:q * nmq + m + 1]:
                          h.activation(o, a, AF.Gelu_apprx_tanh, bias=bb))
                p.set_bank_gate(bidx, "act", cg)
                p.wait("vector", "act", cg)
                ch = p.op("vector", "dve", lambda h, o=hhi[:, m * tb:(m + 1) * tb], a=ght:
                          h.tensor_copy(o, a))
                p.wait("vector", "dve", ch)
                chl = p.op("vector", "dve", lambda h, o=hlo[:, m * tb:(m + 1) * tb], a=ght,
                           b=hhi[:, m * tb:(m + 1) * tb]: h.tensor_sub(o, a, b))
            p.wait("sync", "pe", last_up_mm)
            g_w2 = p.ldgroup()
            for k in range(nmq):
                p.dma_g("sync", wbuf[:, k * D:(k + 1) * D],
                        wdn[q * nmq + k, :, :], g_w2)
            p.wait_t("tensor", g_w2)
            p.wait("tensor", "dve", chl)
            for md in range(16):
                bidx = 2 + (md % 2)
                bank = pb[bidx]
                p.use_bank("tensor", bidx)
                for k in range(nmq):
                    cm = p.op("tensor", "pe", lambda h, o=bank[0:128, 0:tb],
                              a=wbuf[:, k * D + md * 128:k * D + md * 128 + 128],
                              b=hhi[:, k * tb:(k + 1) * tb], s=(k == 0):
                              h.matmul(o, a, b, start=s, stop=False))
                    cm = p.op("tensor", "pe", lambda h, o=bank[0:128, 0:tb],
                              a=wbuf[:, k * D + md * 128:k * D + md * 128 + 128],
                              b=hlo[:, k * tb:(k + 1) * tb], e=(k == nmq - 1):
                              h.matmul(o, a, b, start=False, stop=e))
                last_dn_mm = cm
                p.wait("vector", "pe", cm)
                if q == 0:
                    cacc = p.op("vector", "dve", lambda h, o=acc[:, md * tb:(md + 1) * tb],
                                a=bank[0:128, 0:tb]: h.tensor_copy(o, a))
                else:
                    cacc = p.op("vector", "dve", lambda h, o=acc[:, md * tb:(md + 1) * tb],
                                a=bank[0:128, 0:tb], b=acc[:, md * tb:(md + 1) * tb]:
                                h.tensor_add(o, a, b))
                p.set_bank_gate(bidx, "dve", cacc)
        for md in range(16):
            cf = p.op("vector", "dve", lambda h, o=ev, a=acc[:, md * tb:(md + 1) * tb],
                      bb=bdnt[:, md:md + 1]: h.tensor_scalar(o, a, bb, None, ALU.add))
            p.wait("scalar", "dve", cf)
            gs = p.dma("scalar", out[md, :, it * tb:(it + 1) * tb], ev, "st")
            p.wait_t("scalar", gs)
            p.wait_t("vector", gs)
    p.fence()
    p.phase_end()





def ph_conv(p):
    p.phase_begin()
    xz = p.scratch("xz_fm", [24, 128, T], role="r")
    cw = p.inp("L1_attn_conv_w", [DI, 1, 4])
    cb = p.inp("L1_attn_conv_b", [DI])
    out = p.scratch("xconv", [12, 128, T], role="w")

    xt = p.alloc(T + 4)
    acc = p.alloc(T)
    wc = p.alloc(48)
    bc = p.alloc(12)
    g_c = p.ldgroup()
    for c in range(12):
        p.dma_g("sync", wc[:, c * 4:(c + 1) * 4],
                cw[c * 128:(c + 1) * 128, 0, :], g_c)
        p.dma_g("sync", bc[:, c:c + 1], cb[c * 128:(c + 1) * 128].unsqueeze(1), g_c)
    p.wait_t("vector", g_c)
    p.wait_t("scalar", g_c)

    last_gs = None
    for c in range(12):
        if last_gs:
            p.wait_t("sync", last_gs)
        cz = p.op("vector", "dve", lambda h, o=xt[:, 0:3]: h.memset(o, 0.0))
        g_x = p.dma("sync", xt[:, 3:T + 3], xz[c, :, :], "ld")
        p.wait_t("vector", g_x)
        p.wait("vector", "dve", cz)
        c0 = p.op("vector", "dve", lambda h, o=acc, a=xt[:, 0:T], m=wc[:, c * 4:c * 4 + 1]:
                  h.tensor_scalar(o, a, m, None, ALU.mult))
        for j in range(1, 4):
            p.wait("vector", "dve", c0)
            c0 = p.op("vector", "dve", lambda h, o=acc, a=xt[:, j:T + j],
                      m=wc[:, c * 4 + j:c * 4 + j + 1], b=acc:
                      h.scalar_tensor_tensor(o, a, m, b, ALU.mult, ALU.add))
        p.wait("scalar", "dve", c0)
        cs = p.op("scalar", "act", lambda h, o=acc, a=acc, bb=bc[:, c:c + 1]:
                  h.activation(o, a, AF.Silu, bias=bb))
        last_gs = p.dma("scalar", out[c, :, :], acc, "st")
        p.wait_t("scalar", last_gs)
    p.fence()
    p.phase_end()


def ph_scanprep1(p):
    p.phase_begin()
    xp = p.scratch("xp_fm", [25, 128, T], role="r").rearrange("c p t -> (c p) t")
    xpn = p.scratch("xpn", [25 * 128, T], role="w")
    xconv = p.scratch("xconv", [12, 128, T], role="r")
    snbh = p.scratch("snb_hbm", [H1, T], role="w")
    mallh = p.scratch("mall_hbm", [H1, T], role="w")

    W = 512
    bt = p.alloc(W)
    sq = p.alloc(W)
    row = p.alloc(4 * W)
    ones = p.alloc(128)
    pb = p.psum_banks()
    cz = p.op("vector", "dve", lambda h, o=ones: h.memset(o, 1.0))
    p.wait("tensor", "dve", cz)

    last_gs = None
    last_cm = 0
    for hd in range(H1):
        for part in range(2):
            r0 = hd * 129 + 1 + part * S1
            for it in range(T // W):
                if last_gs:
                    p.wait_t("sync", last_gs)
                g_b = p.dma("sync", bt[0:S1, :], xp[r0:r0 + S1, it * W:(it + 1) * W], "ld")
                p.wait_t("scalar", g_b)
                if last_cm:
                    p.wait("scalar", "pe", last_cm)
                c1 = p.op("scalar", "act", lambda h, o=sq[0:S1, :], a=bt[0:S1, :]:
                          h.activation(o, a, AF.Square))
                p.wait("tensor", "act", c1)
                p.use_bank("tensor", 0)
                cm = p.op("tensor", "pe", lambda h, o=pb[0][0:1, 0:W], a=ones[0:S1, 0:1],
                          b=sq[0:S1, :]: h.matmul(o, a, b, start=True, stop=True))
                last_cm = cm
                p.wait("vector", "pe", cm)
                c2 = p.op("vector", "dve", lambda h, o=row[0:1, 0:W], a=pb[0][0:1, 0:W]:
                          h.tensor_copy(o, a))
                p.set_bank_gate(0, "dve", c2)
                p.wait("scalar", "dve", c2)
                c3 = p.op("scalar", "act", lambda h, o=row[0:1, W:2 * W], a=row[0:1, 0:W]:
                          h.activation(o, a, AF.Sqrt))
                p.wait("vector", "act", c3)
                c4 = p.op("vector", "dve", lambda h, o=row[0:1, W:2 * W], a=row[0:1, W:2 * W]:
                          h.tensor_scalar(o, a, 1e-12, None, ALU.max))
                p.wait("vector", "dve", c4)
                c5 = p.op("vector", "dve", lambda h, o=row[0:1, 2 * W:3 * W],
                          a=row[0:1, W:2 * W]: h.reciprocal(o, a))
                p.wait("vector", "dve", c5)
                if part == 0:
                    c6 = p.op("vector", "dve", lambda h, o=row[0:1, 3 * W:4 * W],
                              a=row[0:1, 2 * W:3 * W], b=row[0:1, 2 * W:3 * W]:
                              h.tensor_mul(o, a, b))
                    p.wait("vector", "dve", c6)
                    c7 = p.op("vector", "dve", lambda h, o=row[0:1, 3 * W:4 * W],
                              a=row[0:1, 3 * W:4 * W], b=row[0:1, 0:W]:
                              h.tensor_mul(o, a, b))
                    p.wait("scalar", "dve", c7)
                    gs0 = p.dma("scalar", snbh[hd:hd + 1, it * W:(it + 1) * W],
                                row[0:1, 3 * W:4 * W], "st")
                    p.wait_t("scalar", gs0)
                p.use_bank("tensor", 1)
                p.wait("tensor", "dve", c5)
                cmb = p.op("tensor", "pe", lambda h, o=pb[1][0:S1, 0:W], a=ones[0:1, 0:S1],
                           b=row[0:1, 2 * W:3 * W]: h.matmul(o, a, b, start=True, stop=True))
                p.wait("vector", "pe", cmb)
                c8 = p.op("vector", "dve", lambda h, o=bt[0:S1, :], a=bt[0:S1, :],
                          b=pb[1][0:S1, 0:W]: h.tensor_mul(o, a, b))
                p.set_bank_gate(1, "dve", c8)
                p.wait("scalar", "dve", c8)
                last_gs = p.dma("scalar", xpn[r0:r0 + S1, it * W:(it + 1) * W],
                                bt[0:S1, :], "st")
                p.wait_t("scalar", last_gs)
    for hd in range(H1):
        hp, off = hd // 2, 64 * (hd % 2)
        for it in range(T // W):
            if last_gs:
                p.wait_t("sync", last_gs)
            g_b = p.dma("sync", bt[0:S1, :], xconv[hp, off:off + 64, it * W:(it + 1) * W], "ld")
            p.wait_t("tensor", g_b)
            p.use_bank("tensor", 0)
            cm = p.op("tensor", "pe", lambda h, o=pb[0][0:1, 0:W], a=ones[0:64, 0:1],
                      b=bt[0:64, :]: h.matmul(o, a, b, start=True, stop=True))
            p.wait("vector", "pe", cm)
            c2 = p.op("vector", "dve", lambda h, o=row[0:1, 0:W], a=pb[0][0:1, 0:W]:
                      h.tensor_scalar(o, a, 1.0 / 64.0, None, ALU.mult))
            p.set_bank_gate(0, "dve", c2)
            p.wait("scalar", "dve", c2)
            last_gs = p.dma("scalar", mallh[hd:hd + 1, it * W:(it + 1) * W],
                            row[0:1, 0:W], "st")
            p.wait_t("scalar", last_gs)
    p.fence()
    p.phase_end()


def ph_scanprep2(p):
    p.phase_begin()
    xp = p.scratch("xp_fm", [25, 128, T], role="r").rearrange("c p t -> (c p) t")
    snbh = p.scratch("snb_hbm", [H1, T], role="r")
    mallh = p.scratch("mall_hbm", [H1, T], role="r")
    alog = p.inp("L1_attn_A_log", [H1])
    l2ab = p.inp("L1_attn_log2_alpha_base", [H1])
    l2b = p.inp("L1_attn_log2_beta", [H1])
    sema = p.inp("L1_attn_surprise_ema", [H1])
    ident = p.inp("IDENT", [128, 128])
    nck = T // 128
    eT = p.scratch("e_T", [nck, 128, H1], role="w")
    omT = p.scratch("om_T", [nck, 128, H1], role="w")
    exT = p.scratch("ex_T", [nck, 128, H1], role="w")

    HH = H1
    LN2 = 0.6931471805599453
    dt = p.alloc(T)
    mm = p.alloc(T)
    sb = p.alloc(T)
    w1 = p.alloc(T)
    w2 = p.alloc(T)
    al = p.alloc(T)
    om = p.alloc(T)
    ec = p.alloc(T)
    colv = p.alloc(16)
    onesd = p.alloc(128)
    idt = p.alloc(128)
    ev = p.alloc(3 * 32)
    pb = p.psum_banks()

    g_c = p.ldgroup()
    p.dma_g("sync", idt, ident[:, :], g_c)
    p.dma_g("sync", colv[0:HH, 0:1], alog[:].unsqueeze(1), g_c)
    p.dma_g("sync", colv[0:HH, 1:2], l2ab[:].unsqueeze(1), g_c)
    p.dma_g("sync", colv[0:HH, 2:3], l2b[:].unsqueeze(1), g_c)
    p.dma_g("sync", colv[0:HH, 3:4], sema[:].unsqueeze(1), g_c)
    g_d = p.ldgroup()
    p.dma_g("sync", dt[0:HH, :], xp[0:24 * 129:129, :], g_d)
    p.dma_g("sync", mm[0:HH, :], mallh[:, :], g_d)
    p.dma_g("sync", sb[0:HH, :], snbh[:, :], g_d)
    for e in ("vector", "scalar", "tensor"):
        p.wait_t(e, g_c)
        p.wait_t(e, g_d)

    def vop(fn):
        c = p.op("vector", "dve", fn)
        p.wait("vector", "dve", c)
        return c

    def sop(fn):
        c = p.op("scalar", "act", fn)
        p.wait("scalar", "act", c)
        return c

    c = vop(lambda h, o=colv[0:HH, 4:5], a=colv[0:HH, 0:1]: h.tensor_scalar(
        o, a, -2.3, -0.01, ALU.max, ALU.min))
    c = vop(lambda h, o=colv[0:HH, 5:6], a=colv[0:HH, 1:2]: h.tensor_scalar(
        o, a, -3.32, -0.015, ALU.max, ALU.min))
    p.wait("scalar", "dve", c)
    c = sop(lambda h, o=colv[0:HH, 5:6], a=colv[0:HH, 5:6]: h.activation(
        o, a, AF.Exp, scale=LN2))
    p.wait("vector", "act", c)
    c = vop(lambda h, o=colv[0:HH, 5:6], a=colv[0:HH, 5:6]: h.tensor_scalar(
        o, a, -1.0, 1.0, ALU.mult, ALU.add))
    c = vop(lambda h, o=colv[0:HH, 6:7], a=colv[0:HH, 5:6]: h.tensor_scalar(
        o, a, -1.0, 1.0, ALU.mult, ALU.add))
    c = vop(lambda h, o=colv[0:HH, 7:8], a=colv[0:HH, 2:3]: h.tensor_scalar(
        o, a, -2.0, 2.0, ALU.max, ALU.min))
    p.wait("scalar", "dve", c)
    c = sop(lambda h, o=colv[0:HH, 7:8], a=colv[0:HH, 7:8]: h.activation(
        o, a, AF.Exp, scale=LN2))
    p.wait("vector", "act", c)
    c = vop(lambda h, o=colv[0:HH, 9:10], a=colv[0:HH, 3:4]: h.tensor_scalar(
        o, a, 1e-6, None, ALU.add))
    c = vop(lambda h, o=colv[0:HH, 9:10], a=colv[0:HH, 9:10]: h.reciprocal(o, a))
    cz = vop(lambda h, o=onesd[0:HH, :]: h.memset(o, 1.0))
    p.wait("scalar", "dve", cz)

    c = sop(lambda h, o=w1[0:HH, :], a=dt[0:HH, :]: h.activation(o, a, AF.Abs))
    c = sop(lambda h, o=w1[0:HH, :], a=w1[0:HH, :]: h.activation(o, a, AF.Exp, scale=-1.0))
    c = sop(lambda h, o=w1[0:HH, :], a=w1[0:HH, :], bb=onesd[0:HH, 0:1]:
            h.activation(o, a, AF.Ln, bias=bb))
    p.wait("vector", "act", c)
    c = vop(lambda h, o=w2[0:HH, :], a=dt[0:HH, :]: h.tensor_scalar(o, a, 0.0, None, ALU.max))
    c = vop(lambda h, o=w1[0:HH, :], a=w1[0:HH, :], b=w2[0:HH, :]: h.tensor_add(o, a, b))
    c = vop(lambda h, o=w1[0:HH, :], a=w1[0:HH, :], m=colv[0:HH, 4:5]:
            h.tensor_scalar(o, a, m, None, ALU.mult))
    p.wait("scalar", "dve", c)
    c = sop(lambda h, o=w2[0:HH, :], a=w1[0:HH, :]: h.activation(o, a, AF.Exp))
    p.wait("vector", "act", c)
    c = vop(lambda h, o=w2[0:HH, :], a=w2[0:HH, :]: h.tensor_scalar(
        o, a, -1.0, 1.0, ALU.mult, ALU.add))
    c = vop(lambda h, o=w2[0:HH, :], a=w2[0:HH, :], b=mm[0:HH, :]: h.tensor_mul(o, a, b))
    p.wait("scalar", "dve", c)
    c = sop(lambda h, o=w2[0:HH, :], a=w2[0:HH, :]: h.activation(o, a, AF.Square))
    p.wait("vector", "act", c)
    c = vop(lambda h, o=w2[0:HH, :], a=w2[0:HH, :], b=sb[0:HH, :]: h.tensor_mul(o, a, b))
    c = vop(lambda h, o=w2[0:HH, :], a=w2[0:HH, :], m=colv[0:HH, 9:10]:
            h.tensor_scalar(o, a, m, 1.0 / S1, ALU.mult, ALU.mult))
    c = vop(lambda h, o=w2[0:HH, :], a=w2[0:HH, :], m=colv[0:HH, 7:8]:
            h.tensor_scalar(o, a, m, None, ALU.mult))
    p.wait("scalar", "dve", c)
    c = sop(lambda h, o=w2[0:HH, :], a=w2[0:HH, :]: h.activation(o, a, AF.Tanh))
    p.wait("vector", "act", c)
    c = vop(lambda h, o=w2[0:HH, :], a=w2[0:HH, :]: h.tensor_scalar(o, a, 0.0, None, ALU.max))
    c = vop(lambda h, o=al[0:HH, :], a=w2[0:HH, :], m=colv[0:HH, 6:7]:
            h.tensor_scalar(o, a, m, None, ALU.mult))
    c = vop(lambda h, o=al[0:HH, :], a=al[0:HH, :], m=colv[0:HH, 5:6]:
            h.tensor_scalar(o, a, m, None, ALU.add))
    c = vop(lambda h, o=al[0:HH, :], a=al[0:HH, :]: h.tensor_scalar(
        o, a, 0.01, 0.999, ALU.max, ALU.min))
    c = vop(lambda h, o=om[0:HH, :], a=al[0:HH, :]: h.tensor_scalar(
        o, a, -1.0, 1.0, ALU.mult, ALU.add))
    p.wait("scalar", "dve", c)
    c = sop(lambda h, o=w1[0:HH, :], a=al[0:HH, :]: h.activation(o, a, AF.Ln))
    p.wait("vector", "act", c)
    for cc in range(nck):
        c = vop(lambda h, o=ec[0:HH, cc * 128:(cc + 1) * 128], d0=onesd[0:HH, :],
                d1=w1[0:HH, cc * 128:(cc + 1) * 128]:
                h.tensor_tensor_scan(o, d0, d1, 0.0, ALU.mult, ALU.add))
    p.wait("scalar", "dve", c)
    c = sop(lambda h, o=w2[0:HH, :], a=ec[0:HH, :]: h.activation(o, a, AF.Exp))
    p.wait("tensor", "act", c)
    p.wait("tensor", "dve", c)
    last_gs = None
    for cc in range(nck):
        for src, dst, ei in ((ec, eT, 0), (om, omT, 1), (w2, exT, 2)):
            bidx = ei % 2
            bank = pb[bidx]
            p.use_bank("tensor", bidx)
            cm = p.op("tensor", "pe", lambda h, o=bank[0:128, 0:HH],
                      a=src[0:HH, cc * 128:(cc + 1) * 128], i=idt:
                      h.transpose(o, a, i[0:HH, 0:HH]))
            p.wait("scalar", "pe", cm)
            if last_gs:
                p.wait_t("scalar", last_gs)
            ce = p.op("scalar", "act", lambda h, o=ev[:, ei * 32:ei * 32 + HH],
                      a=bank[0:128, 0:HH]: h.copy(o, a))
            p.set_bank_gate(bidx, "act", ce)
            last_gs = p.dma("scalar", dst[cc, :, :], ev[:, ei * 32:ei * 32 + HH], "st")
            p.wait_t("scalar", last_gs)
    p.fence()
    p.phase_end()




def ph_scan(p):
    nck = T // 128
    p.phase_begin()
    xpn = p.scratch("xpn", [25 * 128, T], role="r")
    xconv = p.scratch("xconv", [12, 128, T], role="r")
    eT = p.scratch("e_T", [nck, 128, H1], role="r")
    omT = p.scratch("om_T", [nck, 128, H1], role="r")
    exT = p.scratch("ex_T", [nck, 128, H1], role="r")
    ltneg = p.inp("LTNEG", [128, 128])
    ident = p.inp("IDENT", [128, 128])
    yfm = p.scratch("y_fm", [12, 128, T], role="w")

    bfm = p.alloc(128)
    cfm = p.alloc(128)
    xfm = p.alloc(128)
    xtok = p.alloc(64)
    btok = p.alloc(64)
    dsb = p.alloc(128)
    wsb = p.alloc(128)
    y2 = p.alloc(64)
    ysb = p.alloc(64)
    yev = p.alloc(128)
    bw = p.alloc(64)
    erow = p.alloc(128)
    de = p.alloc(4)
    decol = p.alloc(4)
    eyedec = p.alloc(64)
    eTt = p.alloc(32)
    omTt = p.alloc(32)
    exTt = p.alloc(32)
    mneg = p.alloc(128)
    idt = p.alloc(128)
    onesr = p.alloc(128)
    hsA = p.alloc(H1 * 64)
    hsB = p.alloc(H1 * 64)
    pb = p.psum_banks()

    g_c = p.ldgroup()
    p.dma_g("sync", mneg, ltneg[:, :], g_c)
    p.dma_g("sync", idt, ident[:, :], g_c)
    p.wait_t("tensor", g_c)
    p.wait_t("vector", g_c)
    czo = p.op("vector", "dve", lambda h, o=onesr[0:1, :]: h.memset(o, 1.0))
    czA = p.op("vector", "dve", lambda h, o=hsA[0:64, :]: h.memset(o, 0.0))
    p.wait("tensor", "dve", czA)

    hprev = [czA] * H1
    cur, nxt = hsA, hsB
    last_chunk_gate = 0
    for cc in range(nck):
        if last_chunk_gate:
            p.wait("sync", "dve", last_chunk_gate)
        g_e = p.ldgroup()
        p.dma_g("sync", eTt[:, 0:H1], eT[cc, :, :], g_e)
        p.dma_g("sync", omTt[:, 0:H1], omT[cc, :, :], g_e)
        p.dma_g("sync", exTt[:, 0:H1], exT[cc, :, :], g_e)
        p.wait_t("tensor", g_e)
        p.wait_t("vector", g_e)
        for hd in range(H1):
            hp, off = hd // 2, 64 * (hd % 2)
            r0 = hd * 129 + 1
            g1 = p.ldgroup()
            p.dma_g("sync", bfm[0:S1, :], xpn[r0:r0 + S1, cc * 128:(cc + 1) * 128], g1)
            p.dma_g("sync", cfm[0:S1, :], xpn[r0 + S1:r0 + 2 * S1, cc * 128:(cc + 1) * 128], g1)
            p.dma_g("sync", xfm[0:64, :], xconv[hp, off:off + 64, cc * 128:(cc + 1) * 128], g1)
            p.wait_t("tensor", g1)
            p.use_bank("tensor", 7)
            cm_er = p.op("tensor", "pe", lambda h, o=pb[7][0:1, 0:128],
                         a=eTt[:, hd:hd + 1], i=idt: h.transpose(o, a, i))
            p.wait("scalar", "pe", cm_er)
            ce_er = p.op("scalar", "act", lambda h, o=erow[0:1, :], a=pb[7][0:1, 0:128]:
                         h.copy(o, a))
            p.set_bank_gate(7, "act", ce_er)
            c_de = p.op("scalar", "act", lambda h, o=de[0:1, 0:1], a=erow[0:1, 127:128]:
                        h.activation(o, a, AF.Exp))
            p.use_bank("tensor", 0)
            p.wait("tensor", "act", ce_er)
            p.wait("tensor", "dve", czo)
            m1 = p.op("tensor", "pe", lambda h, o=pb[0][0:128, 0:128], a=onesr[0:1, :],
                      b=erow[0:1, :]: h.matmul(o, a, b, start=True, stop=True))
            p.wait("vector", "pe", m1)
            s1 = p.op("vector", "dve", lambda h, o=dsb, a=pb[0][0:128, 0:128],
                      m=eTt[:, hd:hd + 1], b=mneg:
                      h.scalar_tensor_tensor(o, a, m, b, ALU.subtract, ALU.add))
            p.set_bank_gate(0, "dve", s1)
            p.wait("scalar", "dve", s1)
            e1 = p.op("scalar", "act", lambda h, o=dsb, a=dsb: h.activation(o, a, AF.Exp))
            p.wait("vector", "act", e1)
            s2 = p.op("vector", "dve", lambda h, o=dsb, a=dsb, m=omTt[:, hd:hd + 1]:
                      h.tensor_scalar(o, a, m, None, ALU.mult))
            p.use_bank("tensor", 1)
            m2 = p.op("tensor", "pe", lambda h, o=pb[1][0:128, 0:128], a=bfm[0:S1, :],
                      b=cfm[0:S1, :]: h.matmul(o, a, b, start=True, stop=True))
            p.wait("vector", "pe", m2)
            p.wait("vector", "dve", s2)
            s3 = p.op("vector", "dve", lambda h, o=wsb, a=pb[1][0:128, 0:128], b=dsb:
                      h.tensor_mul(o, a, b))
            p.set_bank_gate(1, "dve", s3)
            p.use_bank("tensor", 5)
            mtX = p.op("tensor", "pe", lambda h, o=pb[5][0:128, 0:64], a=xfm[0:64, :],
                       i=idt: h.transpose(o, a, i[0:64, 0:64]))
            p.wait("vector", "pe", mtX)
            seX = p.op("vector", "dve", lambda h, o=xtok[:, 0:64], a=pb[5][0:128, 0:64]:
                       h.tensor_copy(o, a))
            p.set_bank_gate(5, "dve", seX)
            p.use_bank("tensor", 6)
            mtB = p.op("tensor", "pe", lambda h, o=pb[6][0:128, 0:64], a=bfm[0:S1, :],
                       i=idt: h.transpose(o, a, i[0:64, 0:64]))
            p.wait("vector", "pe", mtB)
            seB = p.op("vector", "dve", lambda h, o=btok[:, 0:64], a=pb[6][0:128, 0:64]:
                       h.tensor_copy(o, a))
            p.set_bank_gate(6, "dve", seB)
            p.use_bank("tensor", 2)
            p.wait("tensor", "dve", s3)
            p.wait("tensor", "dve", seX)
            m3 = p.op("tensor", "pe", lambda h, o=pb[2][0:128, 0:64], a=wsb,
                      b=xtok[:, 0:64]: h.matmul(o, a, b, start=True, stop=True))
            p.use_bank("tensor", 3)
            p.wait("tensor", "dve", hprev[hd])
            m4 = p.op("tensor", "pe", lambda h, o=pb[3][0:128, 0:64], a=cfm[0:S1, :],
                      b=cur[0:64, hd * 64:(hd + 1) * 64]: h.matmul(o, a, b, start=True, stop=True))
            p.wait("vector", "pe", m4)
            s4 = p.op("vector", "dve", lambda h, o=y2[:, 0:64], a=pb[3][0:128, 0:64],
                      m=exTt[:, hd:hd + 1]: h.tensor_scalar(o, a, m, None, ALU.mult))
            p.set_bank_gate(3, "dve", s4)
            p.wait("vector", "pe", m3)
            p.wait("vector", "dve", s4)
            s5 = p.op("vector", "dve", lambda h, o=ysb[:, 0:64], a=pb[2][0:128, 0:64],
                      b=y2[:, 0:64]: h.tensor_add(o, a, b))
            p.set_bank_gate(2, "dve", s5)
            p.use_bank("tensor", 5)
            p.wait("tensor", "dve", s5)
            m5 = p.op("tensor", "pe", lambda h, o=pb[5][0:64, 0:128], a=ysb[:, 0:64],
                      i=idt: h.transpose(o, a, i))
            p.wait("scalar", "pe", m5)
            s6 = p.op("scalar", "act", lambda h, o=yev[0:64, :], a=pb[5][0:64, 0:128]:
                      h.copy(o, a))
            p.set_bank_gate(5, "act", s6)
            gy = p.dma("scalar", yfm[hp, off:off + 64, cc * 128:(cc + 1) * 128],
                       yev[0:64, :], "st")
            p.wait_t("scalar", gy)
            p.wait("vector", "dve", seB)
            s7 = p.op("vector", "dve", lambda h, o=bw[:, 0:64], a=btok[:, 0:64],
                      m=dsb[:, 127:128]: h.tensor_scalar(o, a, m, None, ALU.mult))
            p.use_bank("tensor", 4)
            p.wait("tensor", "dve", s7)
            m6 = p.op("tensor", "pe", lambda h, o=pb[4][0:64, 0:64], a=bw[:, 0:64],
                      b=xtok[:, 0:64]: h.matmul(o, a, b, start=True, stop=False))
            p.use_bank("tensor", 7)
            p.wait("tensor", "act", c_de)
            m7 = p.op("tensor", "pe", lambda h, o=pb[7][0:64, 0:1], a=onesr[0:1, 0:64],
                      b=de[0:1, 0:1]: h.matmul(o, a, b, start=True, stop=True))
            p.wait("vector", "pe", m7)
            s9 = p.op("vector", "dve", lambda h, o=decol[0:64, 0:1], a=pb[7][0:64, 0:1]:
                      h.tensor_copy(o, a))
            p.set_bank_gate(7, "dve", s9)
            p.wait("vector", "dve", s9)
            s10 = p.op("vector", "dve", lambda h, o=eyedec[0:64, 0:64], a=idt[0:64, 0:64],
                       m=decol[0:64, 0:1]: h.tensor_scalar(o, a, m, None, ALU.mult))
            p.wait("tensor", "dve", s10)
            m8 = p.op("tensor", "pe", lambda h, o=pb[4][0:64, 0:64], a=eyedec[0:64, 0:64],
                      b=cur[0:64, hd * 64:(hd + 1) * 64]: h.matmul(o, a, b, start=False, stop=True))
            p.wait("vector", "pe", m8)
            s11 = p.op("vector", "dve", lambda h, o=nxt[0:64, hd * 64:(hd + 1) * 64],
                       a=pb[4][0:64, 0:64]: h.tensor_copy(o, a))
            p.set_bank_gate(4, "dve", s11)
            hprev[hd] = s11
            p.wait("sync", "dve", s11)
        cur, nxt = nxt, cur
        last_chunk_gate = s11
    p.fence()
    p.phase_end()


def ph_gate(p):
    p.phase_begin()
    yf = p.scratch("y_fm", [12, 128, T], role="r")
    xz = p.scratch("xz_fm", [24, 128, T], role="r")
    out = p.scratch("yg_fm", [12, 128, T], role="w")
    W = 512
    a = p.alloc(W)
    b = p.alloc(W)
    last_gs = None
    for c in range(12):
        for it in range(T // W):
            if last_gs:
                p.wait_t("sync", last_gs)
            g1 = p.ldgroup()
            p.dma_g("sync", a, yf[c, :, it * W:(it + 1) * W], g1)
            p.dma_g("sync", b, xz[12 + c, :, it * W:(it + 1) * W], g1)
            p.wait_t("scalar", g1)
            c1 = p.op("scalar", "act", lambda h, o=b, i=b: h.activation(o, i, AF.Silu))
            p.wait("vector", "act", c1)
            c2 = p.op("vector", "dve", lambda h, o=a, x=a, y=b: h.tensor_mul(o, x, y))
            p.wait("scalar", "dve", c2)
            last_gs = p.dma("scalar", out[c, :, it * W:(it + 1) * W], a, "st")
            p.wait_t("scalar", last_gs)
    p.fence()
    p.phase_end()


def ph_router(p):
    p.phase_begin()
    xn = p.scratch("xn1", [NCH, 128, T], role="r")
    rw = p.inp("L1_ffn_router_w", [D, NE])
    iota8 = p.inp("IOTA8", [128, NE])
    ident = p.inp("IDENT", [128, 128])
    cwfm = p.scratch("cw_fm", [NE, T], role="w")

    wt = p.alloc(NCH * NE)
    xt = p.alloc(NCH * 128)
    sc8 = p.alloc(128)
    ptok = p.alloc(NE)
    io8 = p.alloc(NE)
    msk = p.alloc(3 * NE, I32)
    m1c = p.alloc(16)
    cw = p.alloc(NE)
    cwT = p.alloc(128)
    idt = p.alloc(128)
    pb = p.psum_banks()

    g_w = p.ldgroup()
    for k in range(NCH):
        p.dma_g("sync", wt[:, k * NE:(k + 1) * NE], rw[k * 128:(k + 1) * 128, :], g_w)
    p.dma_g("sync", io8[:, 0:NE], iota8[:, :], g_w)
    p.dma_g("sync", idt, ident[:, :], g_w)
    p.wait_t("tensor", g_w)
    p.wait_t("vector", g_w)

    last_gs = None
    ntile = T // 128
    for i in range(ntile):
        if last_gs:
            p.wait_t("sync", last_gs)
        g_x = p.ldgroup()
        for k in range(NCH):
            p.dma_g("sync", xt[:, k * 128:(k + 1) * 128],
                    xn[k, :, i * 128:(i + 1) * 128], g_x)
        p.wait_t("tensor", g_x)
        p.use_bank("tensor", 0)
        for k in range(NCH):
            cm = p.op("tensor", "pe", lambda h, o=pb[0][0:NE, 0:128],
                      a=wt[:, k * NE:(k + 1) * NE], b=xt[:, k * 128:(k + 1) * 128],
                      s=(k == 0), e=(k == NCH - 1): h.matmul(o, a, b, start=s, stop=e))
        p.wait("vector", "pe", cm)
        cv = p.op("vector", "dve", lambda h, o=sc8[0:NE, 0:128], a=pb[0][0:NE, 0:128]:
                  h.tensor_copy(o, a))
        p.set_bank_gate(0, "dve", cv)
        p.use_bank("tensor", 1)
        p.wait("tensor", "dve", cv)
        cm2 = p.op("tensor", "pe", lambda h, o=pb[1][0:128, 0:NE], a=sc8[0:NE, 0:128],
                   i=idt: h.transpose(o, a, i[0:NE, 0:NE]))
        p.wait("vector", "pe", cm2)
        cv2 = p.op("vector", "dve", lambda h, o=ptok[:, 0:NE], a=pb[1][0:128, 0:NE]:
                   h.tensor_copy(o, a))
        p.set_bank_gate(1, "dve", cv2)
        p.wait("vector", "dve", cv2)
        c1 = p.op("vector", "dve", lambda h, o=m1c[:, 0:1], a=ptok[:, 0:NE]:
                  h.reduce_max(o, a, axis=AX.X, negate=True))
        p.wait("scalar", "dve", c1)
        c2 = p.op("scalar", "act", lambda h, o=ptok[:, 0:NE], a=ptok[:, 0:NE],
                  bb=m1c[:, 0:1]: h.activation(o, a, AF.Exp, bias=bb))
        p.wait("vector", "act", c2)
        c3 = p.op("vector", "dve", lambda h, o=m1c[:, 1:2], a=ptok[:, 0:NE]:
                  h.reduce_sum(o, a, axis=AX.X))
        p.wait("vector", "dve", c3)
        c4 = p.op("vector", "dve", lambda h, o=m1c[:, 2:3], a=m1c[:, 1:2]: h.reciprocal(o, a))
        p.wait("vector", "dve", c4)
        c5 = p.op("vector", "dve", lambda h, o=ptok[:, 0:NE], a=ptok[:, 0:NE],
                  m=m1c[:, 2:3]: h.tensor_scalar(o, a, m, None, ALU.mult))
        p.wait("vector", "dve", c5)
        c6 = p.op("vector", "dve", lambda h, o=m1c[:, 3:4], a=ptok[:, 0:NE]:
                  h.reduce_max(o, a, axis=AX.X))
        p.wait("vector", "dve", c6)
        c7 = p.op("vector", "dve", lambda h, o=msk.bitcast(I32)[:, 0:NE], a=ptok[:, 0:NE],
                  m=m1c[:, 3:4]: h.tensor_scalar(o, a, m, None, ALU.is_ge))
        p.wait("vector", "dve", c7)
        c8 = p.op("vector", "dve", lambda h, o=cw[:, 0:NE], a=msk.bitcast(I32)[:, 0:NE],
                  x=io8[:, 0:NE], y=io8[:, 0:NE]: h.select(o, a, x, y)) if False else None
        c8a = p.op("vector", "dve", lambda h, o=cwT[:, 0:NE]: h.memset(o, 8.0))
        p.wait("vector", "dve", c8a)
        c8b = p.op("vector", "dve", lambda h, o=cw[:, 0:NE], a=msk.bitcast(I32)[:, 0:NE],
                   x=io8[:, 0:NE], y=cwT[:, 0:NE]: h.select(o, a, x, y))
        p.wait("vector", "dve", c8b)
        c9 = p.op("vector", "dve", lambda h, o=m1c[:, 4:5], a=cw[:, 0:NE]:
                  h.tensor_reduce(o, a, axis=AX.X, op=ALU.min))
        p.wait("vector", "dve", c9)
        c10 = p.op("vector", "dve", lambda h, o=msk.bitcast(I32)[:, 0:NE], a=io8[:, 0:NE],
                   m=m1c[:, 4:5]: h.tensor_scalar(o, a, m, None, ALU.is_equal))
        p.wait("vector", "dve", c10)
        c11 = p.op("vector", "dve", lambda h, o=cwT[:, 0:NE]: h.memset(o, -1.0))
        p.wait("vector", "dve", c11)
        c12 = p.op("vector", "dve", lambda h, o=cw[:, 0:NE], a=msk.bitcast(I32)[:, 0:NE],
                   x=cwT[:, 0:NE], y=ptok[:, 0:NE]: h.select(o, a, x, y))
        p.wait("vector", "dve", c12)
        c13 = p.op("vector", "dve", lambda h, o=m1c[:, 5:6], a=cw[:, 0:NE]:
                   h.reduce_max(o, a, axis=AX.X))
        p.wait("vector", "dve", c13)
        c14 = p.op("vector", "dve", lambda h, o=msk.bitcast(I32)[:, NE:2 * NE], a=cw[:, 0:NE],
                   m=m1c[:, 5:6]: h.tensor_scalar(o, a, m, None, ALU.is_ge))
        p.wait("vector", "dve", c14)
        c15 = p.op("vector", "dve", lambda h, o=cw[:, 0:NE], a=msk.bitcast(I32)[:, NE:2 * NE],
                   x=io8[:, 0:NE], y=cwT2 if False else cwT[:, 0:NE]: h.select(o, a, x, y)) if False else None
        c15a = p.op("vector", "dve", lambda h, o=cwT[:, 0:NE]: h.memset(o, 8.0))
        p.wait("vector", "dve", c15a)
        c15b = p.op("vector", "dve", lambda h, o=cw[:, 0:NE], a=msk.bitcast(I32)[:, NE:2 * NE],
                    x=io8[:, 0:NE], y=cwT[:, 0:NE]: h.select(o, a, x, y))
        p.wait("vector", "dve", c15b)
        c16 = p.op("vector", "dve", lambda h, o=m1c[:, 6:7], a=cw[:, 0:NE]:
                   h.tensor_reduce(o, a, axis=AX.X, op=ALU.min))
        p.wait("vector", "dve", c16)
        c17 = p.op("vector", "dve", lambda h, o=msk.bitcast(I32)[:, NE:2 * NE], a=io8[:, 0:NE],
                   m=m1c[:, 6:7]: h.tensor_scalar(o, a, m, None, ALU.is_equal))
        p.wait("vector", "dve", c17)
        c18 = p.op("vector", "dve", lambda h, o=m1c[:, 7:8], a=m1c[:, 3:4], b=m1c[:, 5:6]:
                   h.tensor_add(o, a, b))
        p.wait("vector", "dve", c18)
        c19 = p.op("vector", "dve", lambda h, o=m1c[:, 8:9], a=m1c[:, 7:8]: h.reciprocal(o, a))
        p.wait("vector", "dve", c19)
        c20 = p.op("vector", "dve", lambda h, o=m1c[:, 9:10], a=m1c[:, 3:4], b=m1c[:, 8:9]:
                   h.tensor_mul(o, a, b))
        p.wait("vector", "dve", c20)
        c21 = p.op("vector", "dve", lambda h, o=m1c[:, 10:11], a=m1c[:, 5:6], b=m1c[:, 8:9]:
                   h.tensor_mul(o, a, b))
        p.wait("vector", "dve", c21)
        c22 = p.op("vector", "dve", lambda h, o=cw[:, 0:NE], a=msk.bitcast(I32)[:, 0:NE]:
                   h.tensor_copy(o, a))
        p.wait("vector", "dve", c22)
        c23 = p.op("vector", "dve", lambda h, o=cw[:, 0:NE], a=cw[:, 0:NE],
                   m=m1c[:, 9:10]: h.tensor_scalar(o, a, m, None, ALU.mult))
        p.wait("vector", "dve", c23)
        c24 = p.op("vector", "dve", lambda h, o=cwT[:, 0:NE], a=msk.bitcast(I32)[:, NE:2 * NE]:
                   h.tensor_copy(o, a))
        p.wait("vector", "dve", c24)
        c25 = p.op("vector", "dve", lambda h, o=cwT[:, 0:NE], a=cwT[:, 0:NE],
                   m=m1c[:, 10:11]: h.tensor_scalar(o, a, m, None, ALU.mult))
        p.wait("vector", "dve", c25)
        c26 = p.op("vector", "dve", lambda h, o=cw[:, 0:NE], a=cw[:, 0:NE],
                   b=cwT[:, 0:NE]: h.tensor_add(o, a, b))
        p.use_bank("tensor", 2)
        p.wait("tensor", "dve", c26)
        cm3 = p.op("tensor", "pe", lambda h, o=pb[2][0:NE, 0:128], a=cw[:, 0:NE],
                   i=idt: h.transpose(o, a, i))
        p.wait("vector", "pe", cm3)
        cv3 = p.op("vector", "dve", lambda h, o=cwT[0:NE, 0:128], a=pb[2][0:NE, 0:128]:
                   h.tensor_copy(o, a))
        p.set_bank_gate(2, "dve", cv3)
        p.wait("scalar", "dve", cv3)
        last_gs = p.dma("scalar", cwfm[:, i * 128:(i + 1) * 128], cwT[0:NE, 0:128], "st")
        p.wait_t("scalar", last_gs)
    p.fence()
    p.phase_end()


def ph_moe_expert(p, e):
    tb = 512
    p.phase_begin()
    xn = p.scratch("xn1", [NCH, 128, T], role="r")
    wg = p.inp("L1_ffn_gate_proj", [NE, D, FE])
    wu = p.inp("L1_ffn_up_proj", [NE, D, FE])
    wd = p.inp("L1_ffn_down_proj", [NE, FE, D])
    cwfm = p.scratch("cw_fm", [NE, T], role="r")
    hh = p.scratch("moe_h", [FE // 128, 128, T], role="w")
    macc = p.scratch("moe_acc", [NCH, 128, T], role="w")

    nk = NCH
    nm = FE // 128
    wgt = p.alloc(nk * FE)
    wut = p.alloc(nk * FE)
    xt = p.alloc(nk * tb)
    sg = p.alloc(tb)
    ev = p.alloc(tb)
    pb = p.psum_banks()

    g_w = p.ldgroup()
    for k in range(nk):
        p.dma_g("sync", wgt[:, k * FE:(k + 1) * FE], wg[e, k * 128:(k + 1) * 128, :], g_w)
        p.dma_g("sync", wut[:, k * FE:(k + 1) * FE], wu[e, k * 128:(k + 1) * 128, :], g_w)
    p.wait_t("tensor", g_w)

    last_mm = 0
    for it in range(T // tb):
        if last_mm:
            p.wait("sync", "pe", last_mm)
        g_x = p.ldgroup()
        for k in range(nk):
            p.dma_g("sync", xt[:, k * tb:(k + 1) * tb],
                    xn[k, :, it * tb:(it + 1) * tb], g_x)
        p.wait_t("tensor", g_x)
        for m in range(nm):
            bg = m % 2
            bu = 2 + m % 2
            p.use_bank("tensor", bg)
            for k in range(nk):
                cm = p.op("tensor", "pe", lambda h, o=pb[bg][0:128, 0:tb],
                          a=wgt[:, k * FE + m * 128:k * FE + m * 128 + 128],
                          b=xt[:, k * tb:(k + 1) * tb], s=(k == 0), e2=(k == nk - 1):
                          h.matmul(o, a, b, start=s, stop=e2))
            p.use_bank("tensor", bu)
            for k in range(nk):
                cm = p.op("tensor", "pe", lambda h, o=pb[bu][0:128, 0:tb],
                          a=wut[:, k * FE + m * 128:k * FE + m * 128 + 128],
                          b=xt[:, k * tb:(k + 1) * tb], s=(k == 0), e2=(k == nk - 1):
                          h.matmul(o, a, b, start=s, stop=e2))
            last_mm = cm
            p.wait("scalar", "pe", cm)
            c1 = p.op("scalar", "act", lambda h, o=sg, a=pb[bg][0:128, 0:tb]:
                      h.activation(o, a, AF.Silu))
            p.set_bank_gate(bg, "act", c1)
            p.wait("vector", "act", c1)
            c2 = p.op("vector", "dve", lambda h, o=ev, a=pb[bu][0:128, 0:tb], b=sg:
                      h.tensor_mul(o, a, b))
            p.set_bank_gate(bu, "dve", c2)
            p.wait("scalar", "dve", c2)
            gs = p.dma("scalar", hh[m, :, it * tb:(it + 1) * tb], ev, "st")
            p.wait_t("scalar", gs)
            p.wait_t("vector", gs)
    p.fence()
    p.phase_end()
    p.phase_begin()
    wdt = p.alloc(nm * D)
    ht = p.alloc(nm * tb)
    cwr = p.alloc(tb)
    cwb = p.alloc(tb)
    at = p.alloc(tb)
    onesr2 = p.alloc(128)
    g_w2 = p.ldgroup()
    for k in range(nm):
        p.dma_g("sync", wdt[:, k * D:(k + 1) * D], wd[e, k * 128:(k + 1) * 128, :], g_w2)
    p.wait_t("tensor", g_w2)
    czz = p.op("vector", "dve", lambda h, o=onesr2[0:1, :]: h.memset(o, 1.0))
    p.wait("tensor", "dve", czz)
    last_mm = 0
    last_acc_gs = None
    for it in range(T // tb):
        if last_mm:
            p.wait("sync", "pe", last_mm)
        g_h = p.ldgroup()
        for k in range(nm):
            p.dma_g("sync", ht[:, k * tb:(k + 1) * tb], hh[k, :, it * tb:(it + 1) * tb], g_h)
        g_cw = p.dma("sync", cwr[0:1, :], cwfm[e:e + 1, it * tb:(it + 1) * tb], "ld")
        p.use_bank("tensor", 4)
        p.wait_t("tensor", g_cw)
        cmb = p.op("tensor", "pe", lambda h, o=pb[4][0:128, 0:512], a=onesr2[0:1, :],
                   b=cwr[0:1, :]: h.matmul(o, a, b, start=True, stop=True))
        p.wait("vector", "pe", cmb)
        p.wait_t("vector", g_h)
        cvb = p.op("vector", "dve", lambda h, o=cwb, a=pb[4][0:128, 0:512]:
                   h.tensor_copy(o, a))
        p.set_bank_gate(4, "dve", cvb)
        p.wait("vector", "dve", cvb)
        ch = 0
        for k in range(nm):
            ch = p.op("vector", "dve", lambda h, o=ht[:, k * tb:(k + 1) * tb],
                      a=ht[:, k * tb:(k + 1) * tb], b=cwb: h.tensor_mul(o, a, b))
        p.wait("tensor", "dve", ch)
        for md in range(16):
            bidx = md % 2
            p.use_bank("tensor", bidx)
            for k in range(nm):
                cm = p.op("tensor", "pe", lambda h, o=pb[bidx][0:128, 0:tb],
                          a=wdt[:, k * D + md * 128:k * D + md * 128 + 128],
                          b=ht[:, k * tb:(k + 1) * tb], s=(k == 0), e2=(k == nm - 1):
                          h.matmul(o, a, b, start=s, stop=e2))
            last_mm = cm
            if e == 0:
                p.wait("scalar", "pe", cm)
                ce = p.op("scalar", "act", lambda h, o=at, a=pb[bidx][0:128, 0:tb]:
                          h.copy(o, a))
                p.set_bank_gate(bidx, "act", ce)
                p.wait("scalar", "act", ce)
            else:
                g_a = p.dma("sync", at, macc[md, :, it * tb:(it + 1) * tb], "ld")
                p.wait_t("vector", g_a)
                p.wait("vector", "pe", cm)
                ce = p.op("vector", "dve", lambda h, o=at, a=pb[bidx][0:128, 0:tb], b=at:
                          h.tensor_add(o, a, b))
                p.set_bank_gate(bidx, "dve", ce)
                p.wait("scalar", "dve", ce)
            gs = p.dma("scalar", macc[md, :, it * tb:(it + 1) * tb], at, "st")
            p.wait_t("scalar", gs)
            p.wait_t("vector", gs)
            p.wait_t("sync", gs)
            last_acc_gs = gs
    p.fence()
    p.phase_end()




def run_phases(phases, inputs_by_core, fused=False, name="prog", n_cores=8):
    p = Prog(name, fused_scratch=fused)
    for ph in phases:
        ph(p)
    nc = p.finalize()
    in_names = [n for n, k in p.hbm_kind.items() if k == "in"]
    in_maps = []
    for core in range(n_cores):
        m = {n: inputs_by_core[core][n] for n in in_names}
        in_maps.append(m)
    res = run_bass_kernel_spmd(nc, in_maps, core_ids=list(range(n_cores)))
    return res.results


def l0_phases():
    return [
        lambda p: ph_ln(p, "x", "xn0"),
        lambda p: ph_matmul_fm(p, "L0_attn_Wq", [D, 2048], "xn0", "q_fm", 2048),
        lambda p: ph_matmul_fm(p, "L0_attn_Wk", [D, 512], "xn0", "k_fm", 512),
        lambda p: ph_matmul_fm(p, "L0_attn_Wv", [D, 512], "xn0", "v_fm", 512),
        lambda p: ph_attn(p),
        lambda p: ph_matmul_fm(p, "L0_attn_Wo", [2048, D], "attn_o", "wo_out", D),
        lambda p: ph_deq(p),
        lambda p: ph_ffn(p),
        lambda p: ph_merge(p, "x1_tok", "x", [("wo_out", None), ("ffn_out", None)],
                           base_is_input=True),
    ]


def l1_phases():
    return [
        lambda p: ph_ln(p, "x1_tok", "xn1", want_rms="xr1"),
        lambda p: ph_matmul_fm(p, "L1_attn_in_proj_w", [2 * DI, D], "xr1", "xz_fm",
                               2 * DI, transpose_w=True, bias_name="L1_attn_in_proj_b",
                               rhs_scale_name="L1_attn_norm_w", mslice=(0, DI)),
        lambda p: ph_matmul_fm(p, "L1_attn_in_proj_w", [2 * DI, D], "xr1", "xz_fm",
                               2 * DI, transpose_w=True, bias_name="L1_attn_in_proj_b",
                               rhs_scale_name="L1_attn_norm_w", mslice=(DI, DI)),
        lambda p: ph_conv(p),
        lambda p: ph_matmul_fm(p, "L1_attn_x_proj_w", [PD, DI], "xconv", "xp_fm",
                               PD, transpose_w=True, bias_name="L1_attn_x_proj_b"),
        lambda p: ph_scanprep1(p),
        lambda p: ph_scanprep2(p),
        lambda p: ph_scan(p),
        lambda p: ph_gate(p),
        lambda p: ph_matmul_fm(p, "L1_attn_out_proj_w", [D, DI], "yg_fm", "op_out",
                               D, transpose_w=True, bias_name="L1_attn_out_proj_b"),
        lambda p: ph_router(p),
    ] + [(lambda e: lambda p: ph_moe_expert(p, e))(e) for e in range(NE)] + [
        lambda p: ph_merge(p, "x2_tok", "x1_tok",
                           [("xn1", None), ("op_out", "rg"), ("moe_acc", None)],
                           base_is_input=False),
    ]


def kernel(**inputs):
    consts = host_constants()
    base = dict(consts)
    for k, v in inputs.items():
        if k != "L0_attn_x":
            base[k] = np.ascontiguousarray(np.asarray(v))
    xfull = np.asarray(inputs["L0_attn_x"])
    percore = []
    for b in range(8):
        m = dict(base)
        m["x"] = np.ascontiguousarray(xfull[b])
        percore.append(m)

    res_a = run_phases(l0_phases(), percore, name="mk_l0", n_cores=8)
    for b in range(8):
        percore[b]["x1_tok"] = np.ascontiguousarray(res_a[b]["x1_tok"])
    res_b = run_phases(l1_phases(), percore, name="mk_l1", n_cores=8)
    out = np.stack([res_b[b]["x2_tok"] for b in range(8)], axis=0)
    return out.astype(np.float32)


# version 02
# speedup vs baseline: 1.0026x; 1.0026x over previous
import math
from contextlib import ExitStack

import numpy as np
import concourse.bass as bass
import concourse.mybir as mybir
from concourse.bass_utils import run_bass_kernel_spmd

F32 = mybir.dt.float32
BF16 = mybir.dt.bfloat16
I32 = mybir.dt.int32
AF = mybir.ActivationFunctionType
ALU = mybir.AluOpType
AX = mybir.AxisListType

B, T, D = 8, 4096, 2048
NCH = D // 128
H0, HK0, HD0, WIN = 16, 4, 128, 512
NWIN = T // WIN
FF = 8192
H1, S1, DH1 = 24, 64, 64
DI = H1 * DH1
PD = H1 * (1 + 2 * S1)
NE, FE = 8, 1024

ENGINES = ("sync", "tensor", "vector", "scalar", "gpsimd")
ARENA_F32 = 47616
NEG = -1e30
F32EPS = 1.1920928955078125e-07


class Prog:

    def __init__(self, name, fused_scratch=False):
        self.nc = bass.Bass("TRN2", target_bir_lowering=False,
                            detect_race_conditions=False)
        self.name = name
        self.fused_scratch = fused_scratch
        self.external_outs = {"x2_tok"}
        self.tape = {e: [] for e in ENGINES}
        self.cnt = {}
        self.sem_names = (["pe", "dve", "act", "pool", "fence"]
                          + [f"ld{i}" for i in range(8)]
                          + [f"st{i}" for i in range(8)])
        self._ld_rr = 0
        self._st_rr = 0
        self.sems = {}
        self.ctx = ExitStack()
        self.hbm = {}
        self.hbm_kind = {}
        self._arena = None
        self._arena_off = 0
        self._phase_mark = 0
        self.psum = []
        self.bank_gate = {}
        for s in self.sem_names:
            self.cnt[s] = 0

    def inp(self, name, shape, dt=F32):
        if name not in self.hbm:
            self.hbm[name] = self.nc.dram_tensor(name, list(shape), dt,
                                                 kind="ExternalInput").ap()
            self.hbm_kind[name] = "in"
        return self.hbm[name]

    def out(self, name, shape, dt=F32):
        if name not in self.hbm:
            self.hbm[name] = self.nc.dram_tensor(name, list(shape), dt,
                                                 kind="ExternalOutput").ap()
            self.hbm_kind[name] = "out"
        return self.hbm[name]

    def scratch(self, name, shape, dt=F32, role="rw"):
        if name in self.hbm:
            return self.hbm[name]
        if self.fused_scratch and name not in self.external_outs:
            ap = self.nc.dram_tensor(name, list(shape), dt, kind="Internal").ap()
            self.hbm_kind[name] = "internal"
        elif role == "r":
            ap = self.nc.dram_tensor(name, list(shape), dt, kind="ExternalInput").ap()
            self.hbm_kind[name] = "in"
        else:
            ap = self.nc.dram_tensor(name, list(shape), dt, kind="ExternalOutput").ap()
            self.hbm_kind[name] = "out"
        self.hbm[name] = ap
        return ap

    def _ensure_arena(self):
        if self._arena is None:
            self._arena = self.ctx.enter_context(
                self.nc.sbuf_tensor("arena", [128, ARENA_F32], F32))

    def alloc(self, n, dt=F32):
        self._ensure_arena()
        nf = n if dt in (F32, I32) else (n + 1) // 2
        off = (self._arena_off + 127) & ~127
        self._arena_off = off + nf
        assert self._arena_off <= ARENA_F32, (
            f"{self.name}: arena overflow {self._arena_off} > {ARENA_F32}")
        sl = self._arena[:, off:off + nf]
        if dt == F32:
            return sl
        if dt == I32:
            return sl.bitcast(I32)
        return sl.bitcast(BF16)[:, 0:n]

    def phase_begin(self):
        self._phase_mark = self._arena_off

    def phase_end(self):
        self._arena_off = self._phase_mark

    def psum_banks(self):
        if not self.psum:
            for i in range(8):
                self.psum.append(self.ctx.enter_context(
                    self.nc.psum_tensor(f"pb{i}", [128, 512], F32)))
        return self.psum

    def use_bank(self, eng, idx):
        g = self.bank_gate.get(idx)
        if g:
            self.wait(eng, g[0], g[1])

    def set_bank_gate(self, idx, sem, count):
        self.bank_gate[idx] = (sem, count)

    def emit(self, eng, fn):
        self.tape[eng].append(fn)

    def bump(self, sem, k=1):
        self.cnt[sem] += k
        return self.cnt[sem]

    def ldgroup(self):
        self._ld_rr = (self._ld_rr + 1) % 8
        return [f"ld{self._ld_rr}", self.cnt[f"ld{self._ld_rr}"]]

    def stgroup(self):
        self._st_rr = (self._st_rr + 1) % 8
        return [f"st{self._st_rr}", self.cnt[f"st{self._st_rr}"]]

    def dma_g(self, eng, out_ap, in_ap, grp):
        sem = grp[0]
        c = self.bump(sem, 16)
        grp[1] = c
        self.emit(eng, lambda h, o=out_ap, i=in_ap, s=sem: h.dma_start(
            out=o, in_=i).then_inc(self.sems[s], 16))
        return c

    def wait_t(self, eng, grp):
        self.wait(eng, grp[0], grp[1])

    def dma(self, eng, out_ap, in_ap, grp):
        g = self.ldgroup() if grp == "ld" or grp == "ld2" else self.stgroup()
        self.dma_g(eng, out_ap, in_ap, g)
        return g

    def wait(self, eng, sem, count):
        if count <= 0:
            return
        self.emit(eng, lambda h, s=sem, c=count: h.wait_ge(self.sems[s], c))

    def op(self, eng, sem, fn):
        c = self.bump(sem, 1)
        self.emit(eng, lambda h, f=fn, s=sem: f(h).then_inc(self.sems[s], 1))
        return c

    def opq(self, eng, fn):
        self.emit(eng, lambda h, f=fn: f(h))

    def fence(self):
        for e in ENGINES:
            self.bump("fence", 1)
            self.emit(e, lambda h: h.nop().then_inc(self.sems["fence"], 1))
        tgt = self.cnt["fence"]
        for e in ENGINES:
            self.wait(e, "fence", tgt)

    def finalize(self):
        nc = self.nc
        with self.ctx:
            for s in self.sem_names:
                self.sems[s] = self.ctx.enter_context(nc.semaphore(s))
            self.psum_banks()
            with nc.Block() as block:
                @block.sync
                def _(h):
                    for fn in self.tape["sync"]:
                        fn(h)

                @block.tensor
                def _(h):
                    for fn in self.tape["tensor"]:
                        fn(h)

                @block.vector
                def _(h):
                    for fn in self.tape["vector"]:
                        fn(h)

                @block.scalar
                def _(h):
                    for fn in self.tape["scalar"]:
                        fn(h)

                @block.gpsimd
                def _(h):
                    for fn in self.tape["gpsimd"]:
                        fn(h)
        return nc



def host_constants():
    c = {}
    c["IDENT"] = np.eye(128, dtype=np.float32)
    r = np.arange(WIN)[:, None]
    col = np.arange(2 * WIN)[None, :]
    allowed = np.where(col < WIN, col > r, col - WIN <= r)
    c["MASKG"] = np.where(allowed, 0.0, NEG).astype(np.float32)
    c["MASK0"] = np.where((col >= WIN) & (col - WIN <= r), 0.0, NEG).astype(np.float32)
    tau = np.arange(128)[:, None]
    tt = np.arange(128)[None, :]
    c["LTNEG"] = np.where(tau <= tt, 0.0, NEG).astype(np.float32)
    c["EXP32"] = (np.arange(128)[None, :] // 32 == np.arange(4)[:, None]).astype(np.float32)
    c["IOTA8"] = np.tile(np.arange(8, dtype=np.float32)[None, :], (128, 1))
    return c



def ph_ln(p, x_name, xn_name, want_rms=None):
    eps = 1e-5
    p.phase_begin()
    x = p.inp("x", [T, D]) if x_name == "x" else p.scratch(x_name, [T, D], role="r")
    xn = p.scratch(xn_name, [NCH, 128, T], role="w")
    xr = p.scratch(want_rms, [NCH, 128, T], role="w") if want_rms else None
    ident = p.inp("IDENT", [128, 128])

    xt = p.alloc(D)
    sq = p.alloc(D)
    stats = p.alloc(16)
    xnt = p.alloc(D)
    xrt = p.alloc(D) if want_rms else None
    idt = p.alloc(128)
    evac = p.alloc(256)
    pb = p.psum_banks()

    g_id = p.dma("sync", idt, ident[:, :], "ld")
    p.wait_t("tensor", g_id)

    g_store = None
    ntile = T // 128
    for i in range(ntile):
        if g_store:
            p.wait_t("sync", g_store)
        g_x = p.dma("sync", xt, x[i * 128:(i + 1) * 128, :], "ld")
        p.wait_t("vector", g_x)
        c0 = p.op("vector", "dve", lambda h, o=stats[:, 0:1], a=xt: h.reduce_sum(o, a, axis=AX.X))
        p.wait("vector", "dve", c0)
        c0b = p.op("vector", "dve", lambda h, o=stats[:, 1:2], a=stats[:, 0:1]: h.tensor_scalar(
            o, a, 1.0 / D, None, ALU.mult))
        p.wait("vector", "dve", c0b)
        c1 = p.op("vector", "dve", lambda h, o=xnt, a=xt, m=stats[:, 1:2]: h.tensor_scalar(
            o, a, m, None, ALU.subtract))
        p.wait("scalar", "dve", c1)
        c2 = p.op("scalar", "act", lambda h, o=sq, a=xnt: h.activation(o, a, AF.Square))
        p.wait("vector", "act", c2)
        c2b = p.op("vector", "dve", lambda h, o=stats[:, 2:3], a=sq: h.reduce_sum(o, a, axis=AX.X))
        p.wait("vector", "dve", c2b)
        c2c = p.op("vector", "dve", lambda h, o=stats[:, 3:4], a=stats[:, 2:3]: h.tensor_scalar(
            o, a, 1.0 / D, eps, ALU.mult, ALU.add))
        p.wait("vector", "dve", c2c)
        c3 = p.op("vector", "dve", lambda h, o=stats[:, 4:5], a=stats[:, 3:4]: h.reciprocal(o, a))
        p.wait("scalar", "dve", c3)
        c4 = p.op("scalar", "act", lambda h, o=stats[:, 5:6], a=stats[:, 4:5]: h.activation(
            o, a, AF.Sqrt))
        p.wait("vector", "act", c4)
        cln = p.op("vector", "dve", lambda h, o=xnt, a=xnt, m=stats[:, 5:6]: h.tensor_scalar(
            o, a, m, None, ALU.mult))
        clast = cln
        if want_rms:
            p.wait("scalar", "dve", cln)
            c6 = p.op("scalar", "act", lambda h, o=sq, a=xnt: h.activation(o, a, AF.Square))
            p.wait("vector", "act", c6)
            c6b = p.op("vector", "dve", lambda h, o=stats[:, 6:7], a=sq: h.reduce_sum(o, a, axis=AX.X))
            p.wait("vector", "dve", c6b)
            c6c = p.op("vector", "dve", lambda h, o=stats[:, 7:8], a=stats[:, 6:7]: h.tensor_scalar(
                o, a, 1.0 / D, F32EPS, ALU.mult, ALU.add))
            p.wait("vector", "dve", c6c)
            c7 = p.op("vector", "dve", lambda h, o=stats[:, 8:9], a=stats[:, 7:8]: h.reciprocal(o, a))
            p.wait("scalar", "dve", c7)
            c8 = p.op("scalar", "act", lambda h, o=stats[:, 9:10], a=stats[:, 8:9]: h.activation(
                o, a, AF.Sqrt))
            p.wait("vector", "act", c8)
            clast = p.op("vector", "dve", lambda h, o=xrt, a=xnt, m=stats[:, 9:10]:
                         h.tensor_scalar(o, a, m, None, ALU.mult))
        p.wait("tensor", "dve", clast)
        g_store = p.stgroup()
        for c in range(NCH):
            bank = pb[c % 2]
            p.use_bank("tensor", c % 2)
            cm = p.op("tensor", "pe", lambda h, o=bank[0:128, 0:128],
                      a=xnt[:, c * 128:(c + 1) * 128], i=idt: h.transpose(o, a, i))
            p.wait("scalar", "pe", cm)
            ce = p.op("scalar", "act", lambda h, o=evac[:, 0:128], a=bank[0:128, 0:128]:
                      h.copy(o, a))
            if not want_rms:
                p.set_bank_gate(c % 2, "act", ce)
            gs = p.dma("scalar", xn[c, :, i * 128:(i + 1) * 128], evac[:, 0:128], "st")
            p.wait_t("scalar", gs)
            if want_rms:
                cm2 = p.op("tensor", "pe", lambda h, o=bank[0:128, 128:256],
                           a=xrt[:, c * 128:(c + 1) * 128], i=idt: h.transpose(o, a, i))
                p.wait("scalar", "pe", cm2)
                ce2 = p.op("scalar", "act", lambda h, o=evac[:, 128:256], a=bank[0:128, 128:256]:
                           h.copy(o, a))
                p.set_bank_gate(c % 2, "act", ce2)
                gs = p.dma("scalar", xr[c, :, i * 128:(i + 1) * 128], evac[:, 128:256], "st")
                p.wait_t("scalar", gs)
        g_store = gs
    p.fence()
    p.phase_end()


def ph_matmul_fm(p, w_name, w_shape, in_name, out_name, n_out,
                 transpose_w=False, bias_name=None, tb=512, out_scale=None,
                 rhs_scale_name=None, mslice=None):
    K = w_shape[0] if not transpose_w else w_shape[1]
    nm_all = (n_out + 127) // 128
    m0, M = (0, n_out) if mslice is None else mslice
    assert m0 % 128 == 0
    nk = K // 128
    nm = (M + 127) // 128
    cm0 = m0 // 128
    p.phase_begin()
    w = p.inp(w_name, list(w_shape))
    xin = p.scratch(in_name, [nk, 128, T], role="r")
    xout = p.scratch(out_name, [nm_all, 128, T], role="w")
    bias = p.inp(bias_name, [n_out]) if bias_name else None
    rsv = p.inp(rhs_scale_name, [K]) if rhs_scale_name else None

    wt = p.alloc(nk * M)
    xt = p.alloc(nk * tb)
    ev = p.alloc(2 * tb)
    bt = p.alloc(nm) if bias_name else None
    rst = p.alloc(nk) if rhs_scale_name else None
    pb = p.psum_banks()

    c_dve_w = 0
    g_w = p.ldgroup()
    if not transpose_w:
        for k in range(nk):
            p.dma_g("sync", wt[:, k * M:(k + 1) * M],
                    w[k * 128:(k + 1) * 128, m0:m0 + M], g_w)
    else:
        ident = p.inp("IDENT", [128, 128])
        idt = p.alloc(128)
        wtmp = p.alloc(K)
        g_id = p.dma("sync", idt, ident[:, :], "ld")
        p.wait_t("tensor", g_id)
        for m in range(nm):
            mrows = min(128, M - m * 128)
            g_wt = p.dma("sync", wtmp[0:mrows, :],
                         w[m0 + m * 128:m0 + m * 128 + mrows, :], "ld")
            p.wait_t("tensor", g_wt)
            for k in range(nk):
                bidx = 6 + (k % 2)
                bank = pb[bidx]
                p.use_bank("tensor", bidx)
                cm = p.op("tensor", "pe", lambda h, o=bank[0:128, 0:128],
                          a=wtmp[0:mrows, k * 128:(k + 1) * 128], i=idt, mr=mrows:
                          h.transpose(o[0:128, 0:mr], a, i[0:mr, 0:mr]))
                p.wait("vector", "pe", cm)
                c_dve_w = p.op("vector", "dve",
                               lambda h, o=wt[:, k * M + m * 128:k * M + m * 128 + mrows],
                               a=bank[0:128, 0:mrows]: h.tensor_copy(o, a))
                p.set_bank_gate(bidx, "dve", c_dve_w)
            p.wait("sync", "dve", c_dve_w)
    if bias_name:
        for m in range(nm):
            mrows = min(128, M - m * 128)
            p.dma_g("sync", bt[0:mrows, m:m + 1],
                    bias[m0 + m * 128:m0 + m * 128 + mrows].unsqueeze(1), g_w)
    if rhs_scale_name:
        for k in range(nk):
            p.dma_g("sync", rst[0:128, k:k + 1],
                    rsv[k * 128:(k + 1) * 128].unsqueeze(1), g_w)

    ntb = T // tb
    p.wait_t("scalar", g_w)
    p.wait_t("vector", g_w)
    p.wait_t("tensor", g_w)
    last_mm = 0
    gs_slot = [None, None]
    for it in range(ntb):
        if last_mm:
            p.wait("sync", "pe", last_mm)
        g_x = p.ldgroup()
        for k in range(nk):
            p.dma_g("sync", xt[:, k * tb:(k + 1) * tb],
                    xin[k, :, it * tb:(it + 1) * tb], g_x)
        if rhs_scale_name:
            p.wait_t("vector", g_x)
            cvs = 0
            for k in range(nk):
                cvs = p.op("vector", "dve", lambda h, o=xt[:, k * tb:(k + 1) * tb],
                           a=xt[:, k * tb:(k + 1) * tb], m=rst[0:128, k:k + 1]:
                           h.tensor_scalar(o, a, m, None, ALU.mult))
            p.wait("tensor", "dve", cvs)
        p.wait_t("tensor", g_x)
        if transpose_w:
            p.wait("tensor", "dve", c_dve_w)
        for m in range(nm):
            mrows = min(128, M - m * 128)
            bank = pb[m % 2]
            p.use_bank("tensor", m % 2)
            for k in range(nk):
                cm = p.op("tensor", "pe", lambda h, o=bank[0:mrows, 0:tb],
                          a=wt[:, k * M + m * 128:k * M + m * 128 + mrows],
                          b=xt[:, k * tb:(k + 1) * tb], s=(k == 0), e=(k == nk - 1):
                          h.matmul(o, a, b, start=s, stop=e))
            last_mm = cm
            eslot = ev[:, (m % 2) * tb:(m % 2) * tb + tb]
            if bias_name:
                p.wait("vector", "pe", cm)
                if gs_slot[m % 2]:
                    p.wait_t("vector", gs_slot[m % 2])
                ce = p.op("vector", "dve", lambda h, o=eslot, a=bank[0:mrows, 0:tb],
                          bb=bt[0:mrows, m:m + 1], mr=mrows:
                          h.tensor_scalar(o[0:mr, :], a, bb, None, ALU.add))
                p.set_bank_gate(m % 2, "dve", ce)
                p.wait("scalar", "dve", ce)
            else:
                p.wait("scalar", "pe", cm)
                if out_scale is not None:
                    ce = p.op("scalar", "act", lambda h, o=eslot, a=bank[0:mrows, 0:tb],
                              mr=mrows: h.activation(o[0:mr, :], a, AF.Copy, scale=out_scale))
                else:
                    ce = p.op("scalar", "act", lambda h, o=eslot, a=bank[0:mrows, 0:tb],
                              mr=mrows: h.copy(o[0:mr, :], a))
                p.set_bank_gate(m % 2, "act", ce)
            gs = p.dma("scalar", xout[cm0 + m, 0:mrows, it * tb:(it + 1) * tb],
                       eslot[0:mrows, :], "st")
            p.wait_t("scalar", gs)
            gs_slot[m % 2] = gs
    p.fence()
    p.phase_end()


def ph_attn(p):
    nwin = T // WIN
    scale = 1.0 / math.sqrt(HD0)
    p.phase_begin()
    q_fm = p.scratch("q_fm", [H0, 128, T], role="r")
    k_fm = p.scratch("k_fm", [HK0, 128, T], role="r")
    v_fm = p.scratch("v_fm", [HK0, 128, T], role="r")
    attn_o = p.scratch("attn_o", [H0, 128, T], role="w")
    maskg = p.inp("MASKG", [WIN, 2 * WIN])
    mask0 = p.inp("MASK0", [WIN, 2 * WIN])
    ident = p.inp("IDENT", [128, 128])

    kc = [p.alloc(1024) for _ in range(HK0)]
    vt = [p.alloc(1024) for _ in range(HK0)]
    vstg = p.alloc(512)
    qh = p.alloc(512)
    sc = p.alloc(1024)
    pp = p.alloc(1024)
    pt = p.alloc(8 * 512)
    oe = p.alloc(512)
    mskg = p.alloc(4 * 1024)
    msk0 = p.alloc(4 * 1024)
    stats = p.alloc(8)
    idt = p.alloc(128)
    pb = p.psum_banks()

    g_const = p.ldgroup()
    p.dma_g("sync", idt, ident[:, :], g_const)
    for sub in range(4):
        p.dma_g("sync", mskg[:, sub * 1024:(sub + 1) * 1024],
                maskg[sub * 128:(sub + 1) * 128, :], g_const)
        p.dma_g("sync", msk0[:, sub * 1024:(sub + 1) * 1024],
                mask0[sub * 128:(sub + 1) * 128, :], g_const)
    p.wait_t("tensor", g_const)
    p.wait_t("vector", g_const)
    cz = 0
    for g in range(HK0):
        p.op("vector", "dve", lambda h, o=kc[g][:, 0:512]: h.memset(o, 0.0))
        cz = p.op("vector", "dve", lambda h, o=vt[g][:, 0:512]: h.memset(o, 0.0))
    p.wait("tensor", "dve", cz)

    last_pv = 0
    for n in range(nwin):
        msk = msk0 if n == 0 else mskg
        for g in range(HK0):
            if last_pv:
                p.wait("sync", "pe", last_pv)
            g_k = p.ldgroup()
            if n > 0:
                p.dma_g("sync", kc[g][:, 0:512], k_fm[g, :, (n - 1) * WIN:n * WIN], g_k)
            p.dma_g("sync", kc[g][:, 512:1024], k_fm[g, :, n * WIN:(n + 1) * WIN], g_k)
            halves = ([1] if n == 0 else [0, 1])
            c_vt = 0
            last_tr = 0
            for hv in halves:
                wv = n - 1 + hv
                if last_tr:
                    p.wait("sync", "pe", last_tr)
                g_v = p.dma("sync", vstg, v_fm[g, :, wv * WIN:(wv + 1) * WIN], "ld")
                p.wait_t("tensor", g_v)
                for j in range(4):
                    bidx = 5 + (j % 2)
                    bank = pb[bidx]
                    p.use_bank("tensor", bidx)
                    cm = p.op("tensor", "pe", lambda h, o=bank[0:128, 0:128],
                              a=vstg[:, j * 128:(j + 1) * 128], i=idt: h.transpose(o, a, i))
                    p.wait("scalar", "pe", cm)
                    c_vt = p.op("scalar", "act", lambda h,
                                o=vt[g][:, (hv * 4 + j) * 128:(hv * 4 + j + 1) * 128],
                                a=bank[0:128, 0:128]: h.copy(o, a))
                    p.set_bank_gate(bidx, "act", c_vt)
                last_tr = cm
            p.wait_t("tensor", g_k)
            p.wait("tensor", "act", c_vt)
            for r in range(4):
                hq = g * 4 + r
                if last_pv:
                    p.wait("sync", "pe", last_pv)
                g_q = p.dma("sync", qh, q_fm[hq, :, n * WIN:(n + 1) * WIN], "ld")
                p.wait_t("tensor", g_q)
                cpt = 0
                for sub in range(4):
                    p.use_bank("tensor", 0)
                    p.use_bank("tensor", 1)
                    p.op("tensor", "pe", lambda h, o=pb[0][0:128, 0:512],
                         a=qh[:, sub * 128:(sub + 1) * 128], b=kc[g][:, 0:512]:
                         h.matmul(o, a, b, start=True, stop=True))
                    cm1 = p.op("tensor", "pe", lambda h, o=pb[1][0:128, 0:512],
                               a=qh[:, sub * 128:(sub + 1) * 128], b=kc[g][:, 512:1024]:
                               h.matmul(o, a, b, start=True, stop=True))
                    p.wait("vector", "pe", cm1)
                    cd0 = p.op("vector", "dve", lambda h, o=sc[:, 0:512],
                               a=pb[0][0:128, 0:512], m=msk[:, sub * 1024:sub * 1024 + 512]:
                               h.scalar_tensor_tensor(o, a, scale, m, ALU.mult, ALU.add))
                    p.set_bank_gate(0, "dve", cd0)
                    cd1 = p.op("vector", "dve", lambda h, o=sc[:, 512:1024],
                               a=pb[1][0:128, 0:512], m=msk[:, sub * 1024 + 512:sub * 1024 + 1024]:
                               h.scalar_tensor_tensor(o, a, scale, m, ALU.mult, ALU.add))
                    p.set_bank_gate(1, "dve", cd1)
                    cd2 = p.op("vector", "dve", lambda h, o=stats[:, 0:1], a=sc:
                               h.reduce_max(o, a, axis=AX.X, negate=True))
                    p.wait("scalar", "dve", cd2)
                    ce = p.op("scalar", "act", lambda h, o=pp, a=sc, b=stats[:, 0:1]:
                              h.activation(o, a, AF.Exp, bias=b))
                    p.wait("vector", "act", ce)
                    cd3 = p.op("vector", "dve", lambda h, o=stats[:, 1:2], a=pp:
                               h.reduce_sum(o, a, axis=AX.X))
                    p.wait("vector", "dve", cd3)
                    cd4 = p.op("vector", "dve", lambda h, o=stats[:, 2:3], a=stats[:, 1:2]:
                               h.reciprocal(o, a))
                    p.wait("vector", "dve", cd4)
                    cd5 = p.op("vector", "dve", lambda h, o=pp, a=pp, m=stats[:, 2:3]:
                               h.tensor_scalar(o, a, m, None, ALU.mult))
                    p.wait("tensor", "dve", cd5)
                    for j in range(8):
                        bidx = 2 + (j % 2)
                        bank = pb[bidx]
                        p.use_bank("tensor", bidx)
                        cm = p.op("tensor", "pe", lambda h, o=bank[0:128, 0:128],
                                  a=pp[:, j * 128:(j + 1) * 128], i=idt: h.transpose(o, a, i))
                        p.wait("scalar", "pe", cm)
                        cpt = p.op("scalar", "act", lambda h,
                                   o=pt[:, j * 512 + sub * 128:j * 512 + sub * 128 + 128],
                                   a=bank[0:128, 0:128]: h.copy(o, a))
                        p.set_bank_gate(bidx, "act", cpt)
                p.use_bank("tensor", 4)
                p.wait("tensor", "act", cpt)
                for j in range(8):
                    cm = p.op("tensor", "pe", lambda h, o=pb[4][0:128, 0:512],
                              a=vt[g][:, j * 128:(j + 1) * 128], b=pt[:, j * 512:(j + 1) * 512],
                              s=(j == 0), e=(j == 7): h.matmul(o, a, b, start=s, stop=e))
                last_pv = cm
                p.wait("scalar", "pe", cm)
                co = p.op("scalar", "act", lambda h, o=oe, a=pb[4][0:128, 0:512]: h.copy(o, a))
                p.set_bank_gate(4, "act", co)
                gs = p.dma("scalar", attn_o[hq, :, n * WIN:(n + 1) * WIN], oe, "st")
                p.wait_t("scalar", gs)
    p.fence()
    p.phase_end()




def ph_merge(p, out_name, base_name, terms, base_is_input=False):
    p.phase_begin()
    base = (p.inp(base_name, [T, D]) if base_is_input
            else p.scratch(base_name, [T, D], role="r"))
    outt = p.scratch(out_name, [T, D], role="w")
    fms = [p.scratch(nm, [NCH, 128, T], role="r") for nm, _ in terms]
    ident = p.inp("IDENT", [128, 128])
    need_rg = any(k == "rg" for _, k in terms)

    xb = p.alloc(D)
    ft = [p.alloc(D) for _ in terms]
    idt = p.alloc(128)
    rgc = p.alloc(4) if need_rg else None
    onec = p.alloc(128) if need_rg else None
    pb = p.psum_banks()

    g_c = p.ldgroup()
    p.dma_g("sync", idt, ident[:, :], g_c)
    if need_rg:
        rg = p.inp("L1_attn_res_gate", [1])
        p.dma_g("sync", rgc[0:1, 0:1], rg[0:1].unsqueeze(1), g_c)
    p.wait_t("tensor", g_c)
    p.wait_t("vector", g_c)
    if need_rg:
        cz = p.op("vector", "dve", lambda h, o=onec[0:1, 0:128]: h.memset(o, 1.0))
        p.wait("tensor", "dve", cz)
        cmb = p.op("tensor", "pe", lambda h, o=pb[3][0:128, 0:1], a=onec[0:1, 0:128],
                   b=rgc[0:1, 0:1]: h.matmul(o, a, b, start=True, stop=True))
        p.wait("vector", "pe", cmb)
        crg = p.op("vector", "dve", lambda h, o=rgc[:, 1:2], a=pb[3][0:128, 0:1]:
                   h.tensor_copy(o, a))
        p.set_bank_gate(3, "dve", crg)

    ntile = T // 128
    last_gs = None
    for i in range(ntile):
        if last_gs:
            p.wait_t("sync", last_gs)
        g_x = p.ldgroup()
        p.dma_g("sync", xb, base[i * 128:(i + 1) * 128, :], g_x)
        for t_i, (nm, _) in enumerate(terms):
            p.dma_g("sync", ft[t_i].rearrange("p (c t) -> p c t", t=128),
                    fms[t_i][:, :, i * 128:(i + 1) * 128].rearrange("c p t -> p c t"), g_x)
        p.wait_t("tensor", g_x)
        p.wait_t("vector", g_x)
        for c in range(NCH):
            for t_i, (nm, kind) in enumerate(terms):
                bidx = t_i % 2
                bank = pb[bidx]
                p.use_bank("tensor", bidx)
                cm = p.op("tensor", "pe", lambda h, o=bank[0:128, 0:128],
                          a=ft[t_i][:, c * 128:(c + 1) * 128], i=idt: h.transpose(o, a, i))
                p.wait("vector", "pe", cm)
                if kind == "rg":
                    cd = p.op("vector", "dve", lambda h, o=xb[:, c * 128:(c + 1) * 128],
                              a=bank[0:128, 0:128], m=rgc[:, 1:2], x2=xb[:, c * 128:(c + 1) * 128]:
                              h.scalar_tensor_tensor(o, a, m, x2, ALU.mult, ALU.add))
                else:
                    cd = p.op("vector", "dve", lambda h, o=xb[:, c * 128:(c + 1) * 128],
                              a=bank[0:128, 0:128], x2=xb[:, c * 128:(c + 1) * 128]:
                              h.tensor_tensor(o, a, x2, ALU.add))
                p.set_bank_gate(bidx, "dve", cd)
        p.wait("scalar", "dve", cd)
        gs = p.dma("scalar", outt[i * 128:(i + 1) * 128, :], xb, "st")
        p.wait_t("scalar", gs)
        last_gs = gs
    p.fence()
    p.phase_end()


def _deq_weight(p, w_name, Kd, Nd, out_name, sc_name):
    nkc = Kd // 128
    nkb = Kd // 32
    nnc = Nd // 128
    w = p.inp(w_name, [Kd, Nd])
    wout = p.scratch(out_name, [nkc, 128, Nd], BF16, role="w")
    schbm = p.scratch(sc_name, [nkb, Nd], role="w")
    sihbm = p.scratch(sc_name + "_inv", [nkb, Nd], role="w")
    ident = p.inp("IDENT", [128, 128])
    exp32 = p.inp("EXP32", [4, 128])
    pb = p.psum_banks()

    wblk = p.alloc(128)
    wT = p.alloc(128)
    scT = p.alloc(nkc * 4)
    scTi = p.alloc(nkc * 4, I32)
    msk_m = p.alloc(nkc * 4, I32)
    idt = p.alloc(128)
    e32 = p.alloc(128)
    ev = p.alloc(128)

    g_c = p.ldgroup()
    p.dma_g("sync", idt, ident[:, :], g_c)
    p.dma_g("sync", e32[0:4, :], exp32[:, :], g_c)
    p.wait_t("tensor", g_c)
    p.wait_t("vector", g_c)
    cmsk = p.op("vector", "dve", lambda h, o=msk_m: h.memset(o, -8388608))

    last_sc_store = None
    last_ce = 0
    for nn in range(nnc):
        if last_sc_store:
            p.wait_t("sync", last_sc_store)
        if last_ce:
            p.wait("vector", "act", last_ce)
        prev_evt = 0
        for kk in range(nkc):
            if prev_evt:
                p.wait("sync", "dve", prev_evt)
            g_wb = p.dma("sync", wblk, w[kk * 128:(kk + 1) * 128,
                                         nn * 128:(nn + 1) * 128], "ld")
            p.wait_t("tensor", g_wb)
            bidx = 0 if kk % 2 == 0 else 1
            bank = pb[bidx]
            p.use_bank("tensor", bidx)
            cm = p.op("tensor", "pe", lambda h, o=bank[0:128, 0:128], a=wblk, i=idt:
                      h.transpose(o, a, i))
            p.wait("vector", "pe", cm)
            cd = p.op("vector", "dve", lambda h, o=wT, a=bank[0:128, 0:128]:
                      h.tensor_copy(o, a))
            p.set_bank_gate(bidx, "dve", cd)
            p.wait("vector", "dve", cd)
            prev_evt = p.op("vector", "dve", lambda h, o=scT[:, kk * 4:(kk + 1) * 4],
                            a=wT.rearrange("p (g k) -> p g k", k=32):
                            h.reduce_max(o, a, axis=AX.X, apply_absolute_value=True))
        p.wait("vector", "dve", prev_evt)
        ci1 = p.op("vector", "dve", lambda h, o=scTi, a=scT.bitcast(I32), m=msk_m:
                   h.tensor_tensor(o, a, m, ALU.bitwise_and))
        p.wait("vector", "dve", ci1)
        ci2 = p.op("vector", "dve", lambda h, o=scT, a=scTi.bitcast(F32):
                   h.tensor_scalar(o, a, 0.25, None, ALU.mult))
        p.wait("vector", "dve", ci2)
        ci3 = p.op("vector", "dve", lambda h, o=scTi, a=scT.bitcast(I32):
                   h.tensor_scalar(o, a, -1.0, 2130706432.0, ALU.mult, ALU.add))
        p.wait("tensor", "dve", ci3)
        nkb_cols = nkc * 4
        for src, dsthbm in ((scT, schbm), (scTi.bitcast(F32), sihbm)):
            for q0 in range(0, nkb_cols, 128):
                qn = min(128, nkb_cols - q0)
                bank = pb[2]
                p.use_bank("tensor", 2)
                cm = p.op("tensor", "pe", lambda h, o=bank[0:128, 0:128], a=src[:, q0:q0 + qn],
                          i=idt, q=qn: h.transpose(o[0:q, 0:128], a, i))
                p.wait("scalar", "pe", cm)
                ce = p.op("scalar", "act", lambda h, o=ev[0:128, 0:128],
                          a=bank[0:128, 0:128], q=qn: h.copy(o[0:q, :], a[0:q, :]))
                p.set_bank_gate(2, "act", ce)
                last_ce = ce
                last_sc_store = p.dma("scalar", dsthbm[q0:q0 + qn, nn * 128:(nn + 1) * 128],
                                      ev[0:qn, 0:128], "st")
                p.wait_t("scalar", last_sc_store)

    p.wait_t("sync", last_sc_store)
    CW = min(Nd, 4096)
    nch_col = Nd // CW
    wsl = p.alloc(CW)
    scn = p.alloc(CW)
    inv = p.alloc(CW)
    aa = p.alloc(CW)
    t1 = p.alloc(CW, I32)
    low = p.alloc(CW)
    lowb = p.alloc(CW)
    hmk = p.alloc(CW, I32)
    sgn = p.alloc(CW)
    obf = p.alloc(CW, BF16)
    last_b = None
    for kk in range(nkc):
        for cc in range(nch_col):
            if last_b:
                p.wait_t("sync", last_b)
            g_w = p.ldgroup()
            p.dma_g("sync", wsl, w[kk * 128:(kk + 1) * 128, cc * CW:(cc + 1) * CW], g_w)
            g_s = p.ldgroup()
            p.dma_g("sync", lowb[0:4, 0:CW],
                    schbm[kk * 4:(kk + 1) * 4, cc * CW:(cc + 1) * CW], g_s)
            p.dma_g("sync", hmk.bitcast(F32)[0:4, 0:CW],
                    sihbm[kk * 4:(kk + 1) * 4, cc * CW:(cc + 1) * CW], g_s)
            p.wait_t("tensor", g_s)
            nslice = CW // 512
            cdx = 0
            for src4, dstfull in ((lowb[0:4, 0:CW], scn), (hmk.bitcast(F32)[0:4, 0:CW], inv)):
                for jj in range(nslice):
                    bidx = 3 + (jj % 2)
                    bank = pb[bidx]
                    p.use_bank("tensor", bidx)
                    cm = p.op("tensor", "pe", lambda h, o=bank[0:128, 0:512],
                              a=e32[0:4, :], b=src4[:, jj * 512:(jj + 1) * 512]:
                              h.matmul(o, a, b, start=True, stop=True))
                    p.wait("vector", "pe", cm)
                    cdx = p.op("vector", "dve", lambda h, o=dstfull[:, jj * 512:(jj + 1) * 512],
                               a=bank[0:128, 0:512]: h.tensor_copy(o, a))
                    p.set_bank_gate(bidx, "dve", cdx)
            p.wait("vector", "dve", cdx)
            p.wait_t("vector", g_w)
            c2 = cdx
            p.wait_t("scalar", g_w)
            c3 = p.op("scalar", "act", lambda h, o=aa, a=wsl: h.activation(o, a, AF.Abs))
            c3b = p.op("scalar", "act", lambda h, o=sgn, a=wsl: h.activation(o, a, AF.Sign))
            p.wait("vector", "act", c3b)
            c4 = p.op("vector", "dve", lambda h, o=aa, a=aa, b=inv: h.tensor_mul(o, a, b))
            p.wait("vector", "dve", c4)
            cch = p.op("vector", "dve", lambda h, o=low, a=aa: h.tensor_scalar(
                o, a, 0.25, None, ALU.is_ge))
            for thr in (0.75, 1.25, 1.75, 2.5, 2.5, 3.5, 3.5, 5.0, 5.0, 5.0, 5.0):
                p.wait("vector", "dve", cch)
                cch = p.op("vector", "dve", lambda h, o=low, a=aa, b=low, t=thr:
                           h.scalar_tensor_tensor(o, a, t, b, ALU.is_ge, ALU.add))
            p.wait("vector", "dve", cch)
            c13 = p.op("vector", "dve", lambda h, o=aa, a=low, b=sgn: h.tensor_mul(o, a, b))
            p.wait("vector", "dve", c13)
            c14 = p.op("vector", "dve", lambda h, o=obf, a=aa, b=scn:
                       h.scalar_tensor_tensor(o, a, 0.5, b, ALU.mult, ALU.mult))
            p.wait("scalar", "dve", c14)
            last_b = p.dma("scalar", wout[kk, :, cc * CW:(cc + 1) * CW], obf, "st")
            p.wait_t("scalar", last_b)


def ph_deq(p):
    p.phase_begin()
    _deq_weight(p, "L0_ffn_up_kernel", D, FF, "wup_bf", "sc_up")
    p.fence()
    p.phase_end()
    p.phase_begin()
    _deq_weight(p, "L0_ffn_down_kernel", FF, D, "wdn_bf", "sc_dn")
    p.fence()
    p.phase_end()


def ph_ffn(p):
    tb = 512
    p.phase_begin()
    xn = p.scratch("xn0", [NCH, 128, T], role="r")
    wup = p.scratch("wup_bf", [NCH, 128, FF], BF16, role="r")
    wdn = p.scratch("wdn_bf", [FF // 128, 128, D], BF16, role="r")
    bup = p.inp("L0_ffn_up_bias", [FF])
    bdn = p.inp("L0_ffn_down_bias", [D])
    out = p.scratch("ffn_out", [NCH, 128, T], role="w")

    NQ = 4
    QH = FF // NQ
    nmq = QH // 128

    wbuf = p.alloc(16 * QH, BF16)
    xhi = p.alloc(16 * tb, BF16)
    xlo = p.alloc(16 * tb, BF16)
    hhi = p.alloc(nmq * tb, BF16)
    hlo = p.alloc(nmq * tb, BF16)
    acc = p.alloc(16 * tb)
    ght = p.alloc(tb)
    ev = p.alloc(tb)
    bupt = p.alloc(FF // 128)
    bdnt = p.alloc(16)
    pb = p.psum_banks()

    g_b = p.ldgroup()
    for m in range(FF // 128):
        p.dma_g("sync", bupt[:, m:m + 1], bup[m * 128:(m + 1) * 128].unsqueeze(1), g_b)
    for m in range(16):
        p.dma_g("sync", bdnt[:, m:m + 1], bdn[m * 128:(m + 1) * 128].unsqueeze(1), g_b)
    p.wait_t("scalar", g_b)
    p.wait_t("vector", g_b)

    last_up_mm = 0
    last_dn_mm = 0
    for it in range(T // tb):
        if last_up_mm:
            p.wait("sync", "pe", last_up_mm)
        cspl = 0
        for k in range(NCH):
            if cspl:
                p.wait("sync", "dve", cspl)
            g_k = p.dma("sync", ght, xn[k, :, it * tb:(it + 1) * tb], "ld")
            p.wait_t("vector", g_k)
            ch = p.op("vector", "dve", lambda h, o=xhi[:, k * tb:(k + 1) * tb], a=ght:
                      h.tensor_copy(o, a))
            p.wait("vector", "dve", ch)
            cspl = p.op("vector", "dve", lambda h, o=xlo[:, k * tb:(k + 1) * tb], a=ght,
                        b=xhi[:, k * tb:(k + 1) * tb]: h.tensor_sub(o, a, b))
        p.wait("tensor", "dve", cspl)
        for q in range(NQ):
            if last_up_mm:
                p.wait("sync", "pe", max(last_up_mm, last_dn_mm))
            g_w = p.ldgroup()
            for k in range(NCH):
                p.dma_g("sync", wbuf[:, k * QH:(k + 1) * QH],
                        wup[k, :, q * QH:(q + 1) * QH], g_w)
            p.wait_t("tensor", g_w)
            chl = 0
            for m in range(nmq):
                bidx = m % 2
                bank = pb[bidx]
                p.use_bank("tensor", bidx)
                for k in range(NCH):
                    cm = p.op("tensor", "pe", lambda h, o=bank[0:128, 0:tb],
                              a=wbuf[:, k * QH + m * 128:k * QH + m * 128 + 128],
                              b=xhi[:, k * tb:(k + 1) * tb], s=(k == 0):
                              h.matmul(o, a, b, start=s, stop=False))
                    cm = p.op("tensor", "pe", lambda h, o=bank[0:128, 0:tb],
                              a=wbuf[:, k * QH + m * 128:k * QH + m * 128 + 128],
                              b=xlo[:, k * tb:(k + 1) * tb], e=(k == NCH - 1):
                              h.matmul(o, a, b, start=False, stop=e))
                last_up_mm = cm
                p.wait("scalar", "pe", cm)
                cg = p.op("scalar", "act", lambda h, o=ght, a=bank[0:128, 0:tb],
                          bb=bupt[:, q * nmq + m:q * nmq + m + 1]:
                          h.activation(o, a, AF.Gelu_apprx_tanh, bias=bb))
                p.set_bank_gate(bidx, "act", cg)
                p.wait("vector", "act", cg)
                ch = p.op("vector", "dve", lambda h, o=hhi[:, m * tb:(m + 1) * tb], a=ght:
                          h.tensor_copy(o, a))
                p.wait("vector", "dve", ch)
                chl = p.op("vector", "dve", lambda h, o=hlo[:, m * tb:(m + 1) * tb], a=ght,
                           b=hhi[:, m * tb:(m + 1) * tb]: h.tensor_sub(o, a, b))
            p.wait("sync", "pe", last_up_mm)
            g_w2 = p.ldgroup()
            for k in range(nmq):
                p.dma_g("sync", wbuf[:, k * D:(k + 1) * D],
                        wdn[q * nmq + k, :, :], g_w2)
            p.wait_t("tensor", g_w2)
            p.wait("tensor", "dve", chl)
            for md in range(16):
                bidx = 2 + (md % 2)
                bank = pb[bidx]
                p.use_bank("tensor", bidx)
                for k in range(nmq):
                    cm = p.op("tensor", "pe", lambda h, o=bank[0:128, 0:tb],
                              a=wbuf[:, k * D + md * 128:k * D + md * 128 + 128],
                              b=hhi[:, k * tb:(k + 1) * tb], s=(k == 0):
                              h.matmul(o, a, b, start=s, stop=False))
                    cm = p.op("tensor", "pe", lambda h, o=bank[0:128, 0:tb],
                              a=wbuf[:, k * D + md * 128:k * D + md * 128 + 128],
                              b=hlo[:, k * tb:(k + 1) * tb], e=(k == nmq - 1):
                              h.matmul(o, a, b, start=False, stop=e))
                last_dn_mm = cm
                p.wait("vector", "pe", cm)
                if q == 0:
                    cacc = p.op("vector", "dve", lambda h, o=acc[:, md * tb:(md + 1) * tb],
                                a=bank[0:128, 0:tb]: h.tensor_copy(o, a))
                else:
                    cacc = p.op("vector", "dve", lambda h, o=acc[:, md * tb:(md + 1) * tb],
                                a=bank[0:128, 0:tb], b=acc[:, md * tb:(md + 1) * tb]:
                                h.tensor_add(o, a, b))
                p.set_bank_gate(bidx, "dve", cacc)
        for md in range(16):
            cf = p.op("vector", "dve", lambda h, o=ev, a=acc[:, md * tb:(md + 1) * tb],
                      bb=bdnt[:, md:md + 1]: h.tensor_scalar(o, a, bb, None, ALU.add))
            p.wait("scalar", "dve", cf)
            gs = p.dma("scalar", out[md, :, it * tb:(it + 1) * tb], ev, "st")
            p.wait_t("scalar", gs)
            p.wait_t("vector", gs)
    p.fence()
    p.phase_end()





def ph_conv(p):
    p.phase_begin()
    xz = p.scratch("xz_fm", [24, 128, T], role="r")
    cw = p.inp("L1_attn_conv_w", [DI, 1, 4])
    cb = p.inp("L1_attn_conv_b", [DI])
    out = p.scratch("xconv", [12, 128, T], role="w")

    xt = p.alloc(T + 4)
    acc = p.alloc(T)
    wc = p.alloc(48)
    bc = p.alloc(12)
    g_c = p.ldgroup()
    for c in range(12):
        p.dma_g("sync", wc[:, c * 4:(c + 1) * 4],
                cw[c * 128:(c + 1) * 128, 0, :], g_c)
        p.dma_g("sync", bc[:, c:c + 1], cb[c * 128:(c + 1) * 128].unsqueeze(1), g_c)
    p.wait_t("vector", g_c)
    p.wait_t("scalar", g_c)

    last_gs = None
    for c in range(12):
        if last_gs:
            p.wait_t("sync", last_gs)
        cz = p.op("vector", "dve", lambda h, o=xt[:, 0:3]: h.memset(o, 0.0))
        g_x = p.dma("sync", xt[:, 3:T + 3], xz[c, :, :], "ld")
        p.wait_t("vector", g_x)
        p.wait("vector", "dve", cz)
        c0 = p.op("vector", "dve", lambda h, o=acc, a=xt[:, 0:T], m=wc[:, c * 4:c * 4 + 1]:
                  h.tensor_scalar(o, a, m, None, ALU.mult))
        for j in range(1, 4):
            p.wait("vector", "dve", c0)
            c0 = p.op("vector", "dve", lambda h, o=acc, a=xt[:, j:T + j],
                      m=wc[:, c * 4 + j:c * 4 + j + 1], b=acc:
                      h.scalar_tensor_tensor(o, a, m, b, ALU.mult, ALU.add))
        p.wait("scalar", "dve", c0)
        cs = p.op("scalar", "act", lambda h, o=acc, a=acc, bb=bc[:, c:c + 1]:
                  h.activation(o, a, AF.Silu, bias=bb))
        last_gs = p.dma("scalar", out[c, :, :], acc, "st")
        p.wait_t("scalar", last_gs)
    p.fence()
    p.phase_end()


def ph_scanprep1(p):
    p.phase_begin()
    xp = p.scratch("xp_fm", [25, 128, T], role="r").rearrange("c p t -> (c p) t")
    xpn = p.scratch("xpn", [25 * 128, T], role="w")
    xconv = p.scratch("xconv", [12, 128, T], role="r")
    snbh = p.scratch("snb_hbm", [H1, T], role="w")
    mallh = p.scratch("mall_hbm", [H1, T], role="w")

    W = 512
    bt = p.alloc(W)
    sq = p.alloc(W)
    row = p.alloc(4 * W)
    ones = p.alloc(128)
    pb = p.psum_banks()
    cz = p.op("vector", "dve", lambda h, o=ones: h.memset(o, 1.0))
    p.wait("tensor", "dve", cz)

    last_gs = None
    last_cm = 0
    for hd in range(H1):
        for part in range(2):
            r0 = hd * 129 + 1 + part * S1
            for it in range(T // W):
                if last_gs:
                    p.wait_t("sync", last_gs)
                g_b = p.dma("sync", bt[0:S1, :], xp[r0:r0 + S1, it * W:(it + 1) * W], "ld")
                p.wait_t("scalar", g_b)
                if last_cm:
                    p.wait("scalar", "pe", last_cm)
                c1 = p.op("scalar", "act", lambda h, o=sq[0:S1, :], a=bt[0:S1, :]:
                          h.activation(o, a, AF.Square))
                p.wait("tensor", "act", c1)
                p.use_bank("tensor", 0)
                cm = p.op("tensor", "pe", lambda h, o=pb[0][0:1, 0:W], a=ones[0:S1, 0:1],
                          b=sq[0:S1, :]: h.matmul(o, a, b, start=True, stop=True))
                last_cm = cm
                p.wait("vector", "pe", cm)
                c2 = p.op("vector", "dve", lambda h, o=row[0:1, 0:W], a=pb[0][0:1, 0:W]:
                          h.tensor_copy(o, a))
                p.set_bank_gate(0, "dve", c2)
                p.wait("scalar", "dve", c2)
                c3 = p.op("scalar", "act", lambda h, o=row[0:1, W:2 * W], a=row[0:1, 0:W]:
                          h.activation(o, a, AF.Sqrt))
                p.wait("vector", "act", c3)
                c4 = p.op("vector", "dve", lambda h, o=row[0:1, W:2 * W], a=row[0:1, W:2 * W]:
                          h.tensor_scalar(o, a, 1e-12, None, ALU.max))
                p.wait("vector", "dve", c4)
                c5 = p.op("vector", "dve", lambda h, o=row[0:1, 2 * W:3 * W],
                          a=row[0:1, W:2 * W]: h.reciprocal(o, a))
                p.wait("vector", "dve", c5)
                if part == 0:
                    c6 = p.op("vector", "dve", lambda h, o=row[0:1, 3 * W:4 * W],
                              a=row[0:1, 2 * W:3 * W], b=row[0:1, 2 * W:3 * W]:
                              h.tensor_mul(o, a, b))
                    p.wait("vector", "dve", c6)
                    c7 = p.op("vector", "dve", lambda h, o=row[0:1, 3 * W:4 * W],
                              a=row[0:1, 3 * W:4 * W], b=row[0:1, 0:W]:
                              h.tensor_mul(o, a, b))
                    p.wait("scalar", "dve", c7)
                    gs0 = p.dma("scalar", snbh[hd:hd + 1, it * W:(it + 1) * W],
                                row[0:1, 3 * W:4 * W], "st")
                    p.wait_t("scalar", gs0)
                p.use_bank("tensor", 1)
                p.wait("tensor", "dve", c5)
                cmb = p.op("tensor", "pe", lambda h, o=pb[1][0:S1, 0:W], a=ones[0:1, 0:S1],
                           b=row[0:1, 2 * W:3 * W]: h.matmul(o, a, b, start=True, stop=True))
                p.wait("vector", "pe", cmb)
                c8 = p.op("vector", "dve", lambda h, o=bt[0:S1, :], a=bt[0:S1, :],
                          b=pb[1][0:S1, 0:W]: h.tensor_mul(o, a, b))
                p.set_bank_gate(1, "dve", c8)
                p.wait("scalar", "dve", c8)
                last_gs = p.dma("scalar", xpn[r0:r0 + S1, it * W:(it + 1) * W],
                                bt[0:S1, :], "st")
                p.wait_t("scalar", last_gs)
    for hd in range(H1):
        hp, off = hd // 2, 64 * (hd % 2)
        for it in range(T // W):
            if last_gs:
                p.wait_t("sync", last_gs)
            g_b = p.dma("sync", bt[0:S1, :], xconv[hp, off:off + 64, it * W:(it + 1) * W], "ld")
            p.wait_t("tensor", g_b)
            p.use_bank("tensor", 0)
            cm = p.op("tensor", "pe", lambda h, o=pb[0][0:1, 0:W], a=ones[0:64, 0:1],
                      b=bt[0:64, :]: h.matmul(o, a, b, start=True, stop=True))
            p.wait("vector", "pe", cm)
            c2 = p.op("vector", "dve", lambda h, o=row[0:1, 0:W], a=pb[0][0:1, 0:W]:
                      h.tensor_scalar(o, a, 1.0 / 64.0, None, ALU.mult))
            p.set_bank_gate(0, "dve", c2)
            p.wait("scalar", "dve", c2)
            last_gs = p.dma("scalar", mallh[hd:hd + 1, it * W:(it + 1) * W],
                            row[0:1, 0:W], "st")
            p.wait_t("scalar", last_gs)
    p.fence()
    p.phase_end()


def ph_scanprep2(p):
    p.phase_begin()
    xp = p.scratch("xp_fm", [25, 128, T], role="r").rearrange("c p t -> (c p) t")
    snbh = p.scratch("snb_hbm", [H1, T], role="r")
    mallh = p.scratch("mall_hbm", [H1, T], role="r")
    alog = p.inp("L1_attn_A_log", [H1])
    l2ab = p.inp("L1_attn_log2_alpha_base", [H1])
    l2b = p.inp("L1_attn_log2_beta", [H1])
    sema = p.inp("L1_attn_surprise_ema", [H1])
    ident = p.inp("IDENT", [128, 128])
    nck = T // 128
    eT = p.scratch("e_T", [nck, 128, H1], role="w")
    omT = p.scratch("om_T", [nck, 128, H1], role="w")
    exT = p.scratch("ex_T", [nck, 128, H1], role="w")

    HH = H1
    LN2 = 0.6931471805599453
    dt = p.alloc(T)
    mm = p.alloc(T)
    sb = p.alloc(T)
    w1 = p.alloc(T)
    w2 = p.alloc(T)
    al = p.alloc(T)
    om = p.alloc(T)
    ec = p.alloc(T)
    colv = p.alloc(16)
    onesd = p.alloc(128)
    idt = p.alloc(128)
    ev = p.alloc(3 * 32)
    pb = p.psum_banks()

    g_c = p.ldgroup()
    p.dma_g("sync", idt, ident[:, :], g_c)
    p.dma_g("sync", colv[0:HH, 0:1], alog[:].unsqueeze(1), g_c)
    p.dma_g("sync", colv[0:HH, 1:2], l2ab[:].unsqueeze(1), g_c)
    p.dma_g("sync", colv[0:HH, 2:3], l2b[:].unsqueeze(1), g_c)
    p.dma_g("sync", colv[0:HH, 3:4], sema[:].unsqueeze(1), g_c)
    g_d = p.ldgroup()
    p.dma_g("sync", dt[0:HH, :], xp[0:24 * 129:129, :], g_d)
    p.dma_g("sync", mm[0:HH, :], mallh[:, :], g_d)
    p.dma_g("sync", sb[0:HH, :], snbh[:, :], g_d)
    for e in ("vector", "scalar", "tensor"):
        p.wait_t(e, g_c)
        p.wait_t(e, g_d)

    def vop(fn):
        c = p.op("vector", "dve", fn)
        p.wait("vector", "dve", c)
        return c

    def sop(fn):
        c = p.op("scalar", "act", fn)
        p.wait("scalar", "act", c)
        return c

    c = vop(lambda h, o=colv[0:HH, 4:5], a=colv[0:HH, 0:1]: h.tensor_scalar(
        o, a, -2.3, -0.01, ALU.max, ALU.min))
    c = vop(lambda h, o=colv[0:HH, 5:6], a=colv[0:HH, 1:2]: h.tensor_scalar(
        o, a, -3.32, -0.015, ALU.max, ALU.min))
    p.wait("scalar", "dve", c)
    c = sop(lambda h, o=colv[0:HH, 5:6], a=colv[0:HH, 5:6]: h.activation(
        o, a, AF.Exp, scale=LN2))
    p.wait("vector", "act", c)
    c = vop(lambda h, o=colv[0:HH, 5:6], a=colv[0:HH, 5:6]: h.tensor_scalar(
        o, a, -1.0, 1.0, ALU.mult, ALU.add))
    c = vop(lambda h, o=colv[0:HH, 6:7], a=colv[0:HH, 5:6]: h.tensor_scalar(
        o, a, -1.0, 1.0, ALU.mult, ALU.add))
    c = vop(lambda h, o=colv[0:HH, 7:8], a=colv[0:HH, 2:3]: h.tensor_scalar(
        o, a, -2.0, 2.0, ALU.max, ALU.min))
    p.wait("scalar", "dve", c)
    c = sop(lambda h, o=colv[0:HH, 7:8], a=colv[0:HH, 7:8]: h.activation(
        o, a, AF.Exp, scale=LN2))
    p.wait("vector", "act", c)
    c = vop(lambda h, o=colv[0:HH, 9:10], a=colv[0:HH, 3:4]: h.tensor_scalar(
        o, a, 1e-6, None, ALU.add))
    c = vop(lambda h, o=colv[0:HH, 9:10], a=colv[0:HH, 9:10]: h.reciprocal(o, a))
    cz = vop(lambda h, o=onesd[0:HH, :]: h.memset(o, 1.0))
    p.wait("scalar", "dve", cz)

    c = sop(lambda h, o=w1[0:HH, :], a=dt[0:HH, :]: h.activation(o, a, AF.Abs))
    c = sop(lambda h, o=w1[0:HH, :], a=w1[0:HH, :]: h.activation(o, a, AF.Exp, scale=-1.0))
    c = sop(lambda h, o=w1[0:HH, :], a=w1[0:HH, :], bb=onesd[0:HH, 0:1]:
            h.activation(o, a, AF.Ln, bias=bb))
    p.wait("vector", "act", c)
    c = vop(lambda h, o=w2[0:HH, :], a=dt[0:HH, :]: h.tensor_scalar(o, a, 0.0, None, ALU.max))
    c = vop(lambda h, o=w1[0:HH, :], a=w1[0:HH, :], b=w2[0:HH, :]: h.tensor_add(o, a, b))
    c = vop(lambda h, o=w1[0:HH, :], a=w1[0:HH, :], m=colv[0:HH, 4:5]:
            h.tensor_scalar(o, a, m, None, ALU.mult))
    p.wait("scalar", "dve", c)
    c = sop(lambda h, o=w2[0:HH, :], a=w1[0:HH, :]: h.activation(o, a, AF.Exp))
    p.wait("vector", "act", c)
    c = vop(lambda h, o=w2[0:HH, :], a=w2[0:HH, :]: h.tensor_scalar(
        o, a, -1.0, 1.0, ALU.mult, ALU.add))
    c = vop(lambda h, o=w2[0:HH, :], a=w2[0:HH, :], b=mm[0:HH, :]: h.tensor_mul(o, a, b))
    p.wait("scalar", "dve", c)
    c = sop(lambda h, o=w2[0:HH, :], a=w2[0:HH, :]: h.activation(o, a, AF.Square))
    p.wait("vector", "act", c)
    c = vop(lambda h, o=w2[0:HH, :], a=w2[0:HH, :], b=sb[0:HH, :]: h.tensor_mul(o, a, b))
    c = vop(lambda h, o=w2[0:HH, :], a=w2[0:HH, :], m=colv[0:HH, 9:10]:
            h.tensor_scalar(o, a, m, 1.0 / S1, ALU.mult, ALU.mult))
    c = vop(lambda h, o=w2[0:HH, :], a=w2[0:HH, :], m=colv[0:HH, 7:8]:
            h.tensor_scalar(o, a, m, None, ALU.mult))
    p.wait("scalar", "dve", c)
    c = sop(lambda h, o=w2[0:HH, :], a=w2[0:HH, :]: h.activation(o, a, AF.Tanh))
    p.wait("vector", "act", c)
    c = vop(lambda h, o=w2[0:HH, :], a=w2[0:HH, :]: h.tensor_scalar(o, a, 0.0, None, ALU.max))
    c = vop(lambda h, o=al[0:HH, :], a=w2[0:HH, :], m=colv[0:HH, 6:7]:
            h.tensor_scalar(o, a, m, None, ALU.mult))
    c = vop(lambda h, o=al[0:HH, :], a=al[0:HH, :], m=colv[0:HH, 5:6]:
            h.tensor_scalar(o, a, m, None, ALU.add))
    c = vop(lambda h, o=al[0:HH, :], a=al[0:HH, :]: h.tensor_scalar(
        o, a, 0.01, 0.999, ALU.max, ALU.min))
    c = vop(lambda h, o=om[0:HH, :], a=al[0:HH, :]: h.tensor_scalar(
        o, a, -1.0, 1.0, ALU.mult, ALU.add))
    p.wait("scalar", "dve", c)
    c = sop(lambda h, o=w1[0:HH, :], a=al[0:HH, :]: h.activation(o, a, AF.Ln))
    p.wait("vector", "act", c)
    for cc in range(nck):
        c = vop(lambda h, o=ec[0:HH, cc * 128:(cc + 1) * 128], d0=onesd[0:HH, :],
                d1=w1[0:HH, cc * 128:(cc + 1) * 128]:
                h.tensor_tensor_scan(o, d0, d1, 0.0, ALU.mult, ALU.add))
    p.wait("scalar", "dve", c)
    c = sop(lambda h, o=w2[0:HH, :], a=ec[0:HH, :]: h.activation(o, a, AF.Exp))
    p.wait("tensor", "act", c)
    p.wait("tensor", "dve", c)
    last_gs = None
    for cc in range(nck):
        for src, dst, ei in ((ec, eT, 0), (om, omT, 1), (w2, exT, 2)):
            bidx = ei % 2
            bank = pb[bidx]
            p.use_bank("tensor", bidx)
            cm = p.op("tensor", "pe", lambda h, o=bank[0:128, 0:HH],
                      a=src[0:HH, cc * 128:(cc + 1) * 128], i=idt:
                      h.transpose(o, a, i[0:HH, 0:HH]))
            p.wait("scalar", "pe", cm)
            if last_gs:
                p.wait_t("scalar", last_gs)
            ce = p.op("scalar", "act", lambda h, o=ev[:, ei * 32:ei * 32 + HH],
                      a=bank[0:128, 0:HH]: h.copy(o, a))
            p.set_bank_gate(bidx, "act", ce)
            last_gs = p.dma("scalar", dst[cc, :, :], ev[:, ei * 32:ei * 32 + HH], "st")
            p.wait_t("scalar", last_gs)
    p.fence()
    p.phase_end()




def ph_scan(p):
    nck = T // 128
    p.phase_begin()
    xpn = p.scratch("xpn", [25 * 128, T], role="r")
    xconv = p.scratch("xconv", [12, 128, T], role="r")
    eT = p.scratch("e_T", [nck, 128, H1], role="r")
    omT = p.scratch("om_T", [nck, 128, H1], role="r")
    exT = p.scratch("ex_T", [nck, 128, H1], role="r")
    ltneg = p.inp("LTNEG", [128, 128])
    ident = p.inp("IDENT", [128, 128])
    yfm = p.scratch("y_fm", [12, 128, T], role="w")

    bfm = p.alloc(128)
    cfm = p.alloc(128)
    xfm = p.alloc(128)
    xtok = p.alloc(64)
    btok = p.alloc(64)
    dsb = p.alloc(128)
    wsb = p.alloc(128)
    y2 = p.alloc(64)
    ysb = p.alloc(64)
    yev = p.alloc(128)
    bw = p.alloc(64)
    erow = p.alloc(128)
    de = p.alloc(4)
    decol = p.alloc(4)
    eyedec = p.alloc(64)
    eTt = p.alloc(32)
    omTt = p.alloc(32)
    exTt = p.alloc(32)
    mneg = p.alloc(128)
    idt = p.alloc(128)
    onesr = p.alloc(128)
    hsA = p.alloc(H1 * 64)
    hsB = p.alloc(H1 * 64)
    pb = p.psum_banks()

    g_c = p.ldgroup()
    p.dma_g("sync", mneg, ltneg[:, :], g_c)
    p.dma_g("sync", idt, ident[:, :], g_c)
    p.wait_t("tensor", g_c)
    p.wait_t("vector", g_c)
    czo = p.op("vector", "dve", lambda h, o=onesr[0:1, :]: h.memset(o, 1.0))
    czA = p.op("vector", "dve", lambda h, o=hsA[0:64, :]: h.memset(o, 0.0))
    p.wait("tensor", "dve", czA)

    hprev = [czA] * H1
    cur, nxt = hsA, hsB
    last_chunk_gate = 0
    for cc in range(nck):
        if last_chunk_gate:
            p.wait("sync", "dve", last_chunk_gate)
        g_e = p.ldgroup()
        p.dma_g("sync", eTt[:, 0:H1], eT[cc, :, :], g_e)
        p.dma_g("sync", omTt[:, 0:H1], omT[cc, :, :], g_e)
        p.dma_g("sync", exTt[:, 0:H1], exT[cc, :, :], g_e)
        p.wait_t("tensor", g_e)
        p.wait_t("vector", g_e)
        for hd in range(H1):
            hp, off = hd // 2, 64 * (hd % 2)
            r0 = hd * 129 + 1
            g1 = p.ldgroup()
            p.dma_g("sync", bfm[0:S1, :], xpn[r0:r0 + S1, cc * 128:(cc + 1) * 128], g1)
            p.dma_g("sync", cfm[0:S1, :], xpn[r0 + S1:r0 + 2 * S1, cc * 128:(cc + 1) * 128], g1)
            p.dma_g("sync", xfm[0:64, :], xconv[hp, off:off + 64, cc * 128:(cc + 1) * 128], g1)
            p.wait_t("tensor", g1)
            p.use_bank("tensor", 7)
            cm_er = p.op("tensor", "pe", lambda h, o=pb[7][0:1, 0:128],
                         a=eTt[:, hd:hd + 1], i=idt: h.transpose(o, a, i))
            p.wait("scalar", "pe", cm_er)
            ce_er = p.op("scalar", "act", lambda h, o=erow[0:1, :], a=pb[7][0:1, 0:128]:
                         h.copy(o, a))
            p.set_bank_gate(7, "act", ce_er)
            c_de = p.op("scalar", "act", lambda h, o=de[0:1, 0:1], a=erow[0:1, 127:128]:
                        h.activation(o, a, AF.Exp))
            p.use_bank("tensor", 0)
            p.wait("tensor", "act", ce_er)
            p.wait("tensor", "dve", czo)
            m1 = p.op("tensor", "pe", lambda h, o=pb[0][0:128, 0:128], a=onesr[0:1, :],
                      b=erow[0:1, :]: h.matmul(o, a, b, start=True, stop=True))
            p.wait("vector", "pe", m1)
            s1 = p.op("vector", "dve", lambda h, o=dsb, a=pb[0][0:128, 0:128],
                      m=eTt[:, hd:hd + 1], b=mneg:
                      h.scalar_tensor_tensor(o, a, m, b, ALU.subtract, ALU.add))
            p.set_bank_gate(0, "dve", s1)
            p.wait("scalar", "dve", s1)
            e1 = p.op("scalar", "act", lambda h, o=dsb, a=dsb: h.activation(o, a, AF.Exp))
            p.wait("vector", "act", e1)
            s2 = p.op("vector", "dve", lambda h, o=dsb, a=dsb, m=omTt[:, hd:hd + 1]:
                      h.tensor_scalar(o, a, m, None, ALU.mult))
            p.use_bank("tensor", 1)
            m2 = p.op("tensor", "pe", lambda h, o=pb[1][0:128, 0:128], a=bfm[0:S1, :],
                      b=cfm[0:S1, :]: h.matmul(o, a, b, start=True, stop=True))
            p.wait("vector", "pe", m2)
            p.wait("vector", "dve", s2)
            s3 = p.op("vector", "dve", lambda h, o=wsb, a=pb[1][0:128, 0:128], b=dsb:
                      h.tensor_mul(o, a, b))
            p.set_bank_gate(1, "dve", s3)
            p.use_bank("tensor", 5)
            mtX = p.op("tensor", "pe", lambda h, o=pb[5][0:128, 0:64], a=xfm[0:64, :],
                       i=idt: h.transpose(o, a, i[0:64, 0:64]))
            p.wait("vector", "pe", mtX)
            seX = p.op("vector", "dve", lambda h, o=xtok[:, 0:64], a=pb[5][0:128, 0:64]:
                       h.tensor_copy(o, a))
            p.set_bank_gate(5, "dve", seX)
            p.use_bank("tensor", 6)
            mtB = p.op("tensor", "pe", lambda h, o=pb[6][0:128, 0:64], a=bfm[0:S1, :],
                       i=idt: h.transpose(o, a, i[0:64, 0:64]))
            p.wait("vector", "pe", mtB)
            seB = p.op("vector", "dve", lambda h, o=btok[:, 0:64], a=pb[6][0:128, 0:64]:
                       h.tensor_copy(o, a))
            p.set_bank_gate(6, "dve", seB)
            p.use_bank("tensor", 2)
            p.wait("tensor", "dve", s3)
            p.wait("tensor", "dve", seX)
            m3 = p.op("tensor", "pe", lambda h, o=pb[2][0:128, 0:64], a=wsb,
                      b=xtok[:, 0:64]: h.matmul(o, a, b, start=True, stop=True))
            p.use_bank("tensor", 3)
            p.wait("tensor", "dve", hprev[hd])
            m4 = p.op("tensor", "pe", lambda h, o=pb[3][0:128, 0:64], a=cfm[0:S1, :],
                      b=cur[0:64, hd * 64:(hd + 1) * 64]: h.matmul(o, a, b, start=True, stop=True))
            p.wait("vector", "pe", m4)
            s4 = p.op("vector", "dve", lambda h, o=y2[:, 0:64], a=pb[3][0:128, 0:64],
                      m=exTt[:, hd:hd + 1]: h.tensor_scalar(o, a, m, None, ALU.mult))
            p.set_bank_gate(3, "dve", s4)
            p.wait("vector", "pe", m3)
            p.wait("vector", "dve", s4)
            s5 = p.op("vector", "dve", lambda h, o=ysb[:, 0:64], a=pb[2][0:128, 0:64],
                      b=y2[:, 0:64]: h.tensor_add(o, a, b))
            p.set_bank_gate(2, "dve", s5)
            p.use_bank("tensor", 5)
            p.wait("tensor", "dve", s5)
            m5 = p.op("tensor", "pe", lambda h, o=pb[5][0:64, 0:128], a=ysb[:, 0:64],
                      i=idt: h.transpose(o, a, i))
            p.wait("scalar", "pe", m5)
            s6 = p.op("scalar", "act", lambda h, o=yev[0:64, :], a=pb[5][0:64, 0:128]:
                      h.copy(o, a))
            p.set_bank_gate(5, "act", s6)
            gy = p.dma("scalar", yfm[hp, off:off + 64, cc * 128:(cc + 1) * 128],
                       yev[0:64, :], "st")
            p.wait_t("scalar", gy)
            p.wait("vector", "dve", seB)
            s7 = p.op("vector", "dve", lambda h, o=bw[:, 0:64], a=btok[:, 0:64],
                      m=dsb[:, 127:128]: h.tensor_scalar(o, a, m, None, ALU.mult))
            p.use_bank("tensor", 4)
            p.wait("tensor", "dve", s7)
            m6 = p.op("tensor", "pe", lambda h, o=pb[4][0:64, 0:64], a=bw[:, 0:64],
                      b=xtok[:, 0:64]: h.matmul(o, a, b, start=True, stop=False))
            p.use_bank("tensor", 7)
            p.wait("tensor", "act", c_de)
            m7 = p.op("tensor", "pe", lambda h, o=pb[7][0:64, 0:1], a=onesr[0:1, 0:64],
                      b=de[0:1, 0:1]: h.matmul(o, a, b, start=True, stop=True))
            p.wait("vector", "pe", m7)
            s9 = p.op("vector", "dve", lambda h, o=decol[0:64, 0:1], a=pb[7][0:64, 0:1]:
                      h.tensor_copy(o, a))
            p.set_bank_gate(7, "dve", s9)
            p.wait("vector", "dve", s9)
            s10 = p.op("vector", "dve", lambda h, o=eyedec[0:64, 0:64], a=idt[0:64, 0:64],
                       m=decol[0:64, 0:1]: h.tensor_scalar(o, a, m, None, ALU.mult))
            p.wait("tensor", "dve", s10)
            m8 = p.op("tensor", "pe", lambda h, o=pb[4][0:64, 0:64], a=eyedec[0:64, 0:64],
                      b=cur[0:64, hd * 64:(hd + 1) * 64]: h.matmul(o, a, b, start=False, stop=True))
            p.wait("vector", "pe", m8)
            s11 = p.op("vector", "dve", lambda h, o=nxt[0:64, hd * 64:(hd + 1) * 64],
                       a=pb[4][0:64, 0:64]: h.tensor_copy(o, a))
            p.set_bank_gate(4, "dve", s11)
            hprev[hd] = s11
            p.wait("sync", "dve", s11)
        cur, nxt = nxt, cur
        last_chunk_gate = s11
    p.fence()
    p.phase_end()


def ph_gate(p):
    p.phase_begin()
    yf = p.scratch("y_fm", [12, 128, T], role="r")
    xz = p.scratch("xz_fm", [24, 128, T], role="r")
    out = p.scratch("yg_fm", [12, 128, T], role="w")
    W = 512
    a = p.alloc(W)
    b = p.alloc(W)
    last_gs = None
    for c in range(12):
        for it in range(T // W):
            if last_gs:
                p.wait_t("sync", last_gs)
            g1 = p.ldgroup()
            p.dma_g("sync", a, yf[c, :, it * W:(it + 1) * W], g1)
            p.dma_g("sync", b, xz[12 + c, :, it * W:(it + 1) * W], g1)
            p.wait_t("scalar", g1)
            c1 = p.op("scalar", "act", lambda h, o=b, i=b: h.activation(o, i, AF.Silu))
            p.wait("vector", "act", c1)
            c2 = p.op("vector", "dve", lambda h, o=a, x=a, y=b: h.tensor_mul(o, x, y))
            p.wait("scalar", "dve", c2)
            last_gs = p.dma("scalar", out[c, :, it * W:(it + 1) * W], a, "st")
            p.wait_t("scalar", last_gs)
    p.fence()
    p.phase_end()


def ph_router(p):
    p.phase_begin()
    xn = p.scratch("xn1", [NCH, 128, T], role="r")
    rw = p.inp("L1_ffn_router_w", [D, NE])
    iota8 = p.inp("IOTA8", [128, NE])
    ident = p.inp("IDENT", [128, 128])
    cwfm = p.scratch("cw_fm", [NE, T], role="w")

    wt = p.alloc(NCH * NE)
    xt = p.alloc(NCH * 128)
    sc8 = p.alloc(128)
    ptok = p.alloc(NE)
    io8 = p.alloc(NE)
    msk = p.alloc(3 * NE, I32)
    m1c = p.alloc(16)
    cw = p.alloc(NE)
    cwT = p.alloc(128)
    idt = p.alloc(128)
    pb = p.psum_banks()

    g_w = p.ldgroup()
    for k in range(NCH):
        p.dma_g("sync", wt[:, k * NE:(k + 1) * NE], rw[k * 128:(k + 1) * 128, :], g_w)
    p.dma_g("sync", io8[:, 0:NE], iota8[:, :], g_w)
    p.dma_g("sync", idt, ident[:, :], g_w)
    p.wait_t("tensor", g_w)
    p.wait_t("vector", g_w)

    last_gs = None
    ntile = T // 128
    for i in range(ntile):
        if last_gs:
            p.wait_t("sync", last_gs)
        g_x = p.ldgroup()
        for k in range(NCH):
            p.dma_g("sync", xt[:, k * 128:(k + 1) * 128],
                    xn[k, :, i * 128:(i + 1) * 128], g_x)
        p.wait_t("tensor", g_x)
        p.use_bank("tensor", 0)
        for k in range(NCH):
            cm = p.op("tensor", "pe", lambda h, o=pb[0][0:NE, 0:128],
                      a=wt[:, k * NE:(k + 1) * NE], b=xt[:, k * 128:(k + 1) * 128],
                      s=(k == 0), e=(k == NCH - 1): h.matmul(o, a, b, start=s, stop=e))
        p.wait("vector", "pe", cm)
        cv = p.op("vector", "dve", lambda h, o=sc8[0:NE, 0:128], a=pb[0][0:NE, 0:128]:
                  h.tensor_copy(o, a))
        p.set_bank_gate(0, "dve", cv)
        p.use_bank("tensor", 1)
        p.wait("tensor", "dve", cv)
        cm2 = p.op("tensor", "pe", lambda h, o=pb[1][0:128, 0:NE], a=sc8[0:NE, 0:128],
                   i=idt: h.transpose(o, a, i[0:NE, 0:NE]))
        p.wait("vector", "pe", cm2)
        cv2 = p.op("vector", "dve", lambda h, o=ptok[:, 0:NE], a=pb[1][0:128, 0:NE]:
                   h.tensor_copy(o, a))
        p.set_bank_gate(1, "dve", cv2)
        p.wait("vector", "dve", cv2)
        c1 = p.op("vector", "dve", lambda h, o=m1c[:, 0:1], a=ptok[:, 0:NE]:
                  h.reduce_max(o, a, axis=AX.X, negate=True))
        p.wait("scalar", "dve", c1)
        c2 = p.op("scalar", "act", lambda h, o=ptok[:, 0:NE], a=ptok[:, 0:NE],
                  bb=m1c[:, 0:1]: h.activation(o, a, AF.Exp, bias=bb))
        p.wait("vector", "act", c2)
        c3 = p.op("vector", "dve", lambda h, o=m1c[:, 1:2], a=ptok[:, 0:NE]:
                  h.reduce_sum(o, a, axis=AX.X))
        p.wait("vector", "dve", c3)
        c4 = p.op("vector", "dve", lambda h, o=m1c[:, 2:3], a=m1c[:, 1:2]: h.reciprocal(o, a))
        p.wait("vector", "dve", c4)
        c5 = p.op("vector", "dve", lambda h, o=ptok[:, 0:NE], a=ptok[:, 0:NE],
                  m=m1c[:, 2:3]: h.tensor_scalar(o, a, m, None, ALU.mult))
        p.wait("vector", "dve", c5)
        c6 = p.op("vector", "dve", lambda h, o=m1c[:, 3:4], a=ptok[:, 0:NE]:
                  h.reduce_max(o, a, axis=AX.X))
        p.wait("vector", "dve", c6)
        c7 = p.op("vector", "dve", lambda h, o=msk.bitcast(I32)[:, 0:NE], a=ptok[:, 0:NE],
                  m=m1c[:, 3:4]: h.tensor_scalar(o, a, m, None, ALU.is_ge))
        p.wait("vector", "dve", c7)
        c8 = p.op("vector", "dve", lambda h, o=cw[:, 0:NE], a=msk.bitcast(I32)[:, 0:NE],
                  x=io8[:, 0:NE], y=io8[:, 0:NE]: h.select(o, a, x, y)) if False else None
        c8a = p.op("vector", "dve", lambda h, o=cwT[:, 0:NE]: h.memset(o, 8.0))
        p.wait("vector", "dve", c8a)
        c8b = p.op("vector", "dve", lambda h, o=cw[:, 0:NE], a=msk.bitcast(I32)[:, 0:NE],
                   x=io8[:, 0:NE], y=cwT[:, 0:NE]: h.select(o, a, x, y))
        p.wait("vector", "dve", c8b)
        c9 = p.op("vector", "dve", lambda h, o=m1c[:, 4:5], a=cw[:, 0:NE]:
                  h.tensor_reduce(o, a, axis=AX.X, op=ALU.min))
        p.wait("vector", "dve", c9)
        c10 = p.op("vector", "dve", lambda h, o=msk.bitcast(I32)[:, 0:NE], a=io8[:, 0:NE],
                   m=m1c[:, 4:5]: h.tensor_scalar(o, a, m, None, ALU.is_equal))
        p.wait("vector", "dve", c10)
        c11 = p.op("vector", "dve", lambda h, o=cwT[:, 0:NE]: h.memset(o, -1.0))
        p.wait("vector", "dve", c11)
        c12 = p.op("vector", "dve", lambda h, o=cw[:, 0:NE], a=msk.bitcast(I32)[:, 0:NE],
                   x=cwT[:, 0:NE], y=ptok[:, 0:NE]: h.select(o, a, x, y))
        p.wait("vector", "dve", c12)
        c13 = p.op("vector", "dve", lambda h, o=m1c[:, 5:6], a=cw[:, 0:NE]:
                   h.reduce_max(o, a, axis=AX.X))
        p.wait("vector", "dve", c13)
        c14 = p.op("vector", "dve", lambda h, o=msk.bitcast(I32)[:, NE:2 * NE], a=cw[:, 0:NE],
                   m=m1c[:, 5:6]: h.tensor_scalar(o, a, m, None, ALU.is_ge))
        p.wait("vector", "dve", c14)
        c15 = p.op("vector", "dve", lambda h, o=cw[:, 0:NE], a=msk.bitcast(I32)[:, NE:2 * NE],
                   x=io8[:, 0:NE], y=cwT2 if False else cwT[:, 0:NE]: h.select(o, a, x, y)) if False else None
        c15a = p.op("vector", "dve", lambda h, o=cwT[:, 0:NE]: h.memset(o, 8.0))
        p.wait("vector", "dve", c15a)
        c15b = p.op("vector", "dve", lambda h, o=cw[:, 0:NE], a=msk.bitcast(I32)[:, NE:2 * NE],
                    x=io8[:, 0:NE], y=cwT[:, 0:NE]: h.select(o, a, x, y))
        p.wait("vector", "dve", c15b)
        c16 = p.op("vector", "dve", lambda h, o=m1c[:, 6:7], a=cw[:, 0:NE]:
                   h.tensor_reduce(o, a, axis=AX.X, op=ALU.min))
        p.wait("vector", "dve", c16)
        c17 = p.op("vector", "dve", lambda h, o=msk.bitcast(I32)[:, NE:2 * NE], a=io8[:, 0:NE],
                   m=m1c[:, 6:7]: h.tensor_scalar(o, a, m, None, ALU.is_equal))
        p.wait("vector", "dve", c17)
        c18 = p.op("vector", "dve", lambda h, o=m1c[:, 7:8], a=m1c[:, 3:4], b=m1c[:, 5:6]:
                   h.tensor_add(o, a, b))
        p.wait("vector", "dve", c18)
        c19 = p.op("vector", "dve", lambda h, o=m1c[:, 8:9], a=m1c[:, 7:8]: h.reciprocal(o, a))
        p.wait("vector", "dve", c19)
        c20 = p.op("vector", "dve", lambda h, o=m1c[:, 9:10], a=m1c[:, 3:4], b=m1c[:, 8:9]:
                   h.tensor_mul(o, a, b))
        p.wait("vector", "dve", c20)
        c21 = p.op("vector", "dve", lambda h, o=m1c[:, 10:11], a=m1c[:, 5:6], b=m1c[:, 8:9]:
                   h.tensor_mul(o, a, b))
        p.wait("vector", "dve", c21)
        c22 = p.op("vector", "dve", lambda h, o=cw[:, 0:NE], a=msk.bitcast(I32)[:, 0:NE]:
                   h.tensor_copy(o, a))
        p.wait("vector", "dve", c22)
        c23 = p.op("vector", "dve", lambda h, o=cw[:, 0:NE], a=cw[:, 0:NE],
                   m=m1c[:, 9:10]: h.tensor_scalar(o, a, m, None, ALU.mult))
        p.wait("vector", "dve", c23)
        c24 = p.op("vector", "dve", lambda h, o=cwT[:, 0:NE], a=msk.bitcast(I32)[:, NE:2 * NE]:
                   h.tensor_copy(o, a))
        p.wait("vector", "dve", c24)
        c25 = p.op("vector", "dve", lambda h, o=cwT[:, 0:NE], a=cwT[:, 0:NE],
                   m=m1c[:, 10:11]: h.tensor_scalar(o, a, m, None, ALU.mult))
        p.wait("vector", "dve", c25)
        c26 = p.op("vector", "dve", lambda h, o=cw[:, 0:NE], a=cw[:, 0:NE],
                   b=cwT[:, 0:NE]: h.tensor_add(o, a, b))
        p.use_bank("tensor", 2)
        p.wait("tensor", "dve", c26)
        cm3 = p.op("tensor", "pe", lambda h, o=pb[2][0:NE, 0:128], a=cw[:, 0:NE],
                   i=idt: h.transpose(o, a, i))
        p.wait("vector", "pe", cm3)
        cv3 = p.op("vector", "dve", lambda h, o=cwT[0:NE, 0:128], a=pb[2][0:NE, 0:128]:
                   h.tensor_copy(o, a))
        p.set_bank_gate(2, "dve", cv3)
        p.wait("scalar", "dve", cv3)
        last_gs = p.dma("scalar", cwfm[:, i * 128:(i + 1) * 128], cwT[0:NE, 0:128], "st")
        p.wait_t("scalar", last_gs)
    p.fence()
    p.phase_end()


def ph_moe_expert(p, e):
    tb = 512
    p.phase_begin()
    xn = p.scratch("xn1", [NCH, 128, T], role="r")
    wg = p.inp("L1_ffn_gate_proj", [NE, D, FE])
    wu = p.inp("L1_ffn_up_proj", [NE, D, FE])
    wd = p.inp("L1_ffn_down_proj", [NE, FE, D])
    cwfm = p.scratch("cw_fm", [NE, T], role="r")
    hh = p.scratch("moe_h", [FE // 128, 128, T], role="w")
    macc = p.scratch("moe_acc", [NCH, 128, T], role="w")

    nk = NCH
    nm = FE // 128
    wgt = p.alloc(nk * FE)
    wut = p.alloc(nk * FE)
    xt = p.alloc(nk * tb)
    sg = p.alloc(tb)
    ev = p.alloc(tb)
    pb = p.psum_banks()

    g_w = p.ldgroup()
    for k in range(nk):
        p.dma_g("sync", wgt[:, k * FE:(k + 1) * FE], wg[e, k * 128:(k + 1) * 128, :], g_w)
        p.dma_g("sync", wut[:, k * FE:(k + 1) * FE], wu[e, k * 128:(k + 1) * 128, :], g_w)
    p.wait_t("tensor", g_w)

    last_mm = 0
    for it in range(T // tb):
        if last_mm:
            p.wait("sync", "pe", last_mm)
        g_x = p.ldgroup()
        for k in range(nk):
            p.dma_g("sync", xt[:, k * tb:(k + 1) * tb],
                    xn[k, :, it * tb:(it + 1) * tb], g_x)
        p.wait_t("tensor", g_x)
        for m in range(nm):
            bg = m % 2
            bu = 2 + m % 2
            p.use_bank("tensor", bg)
            for k in range(nk):
                cm = p.op("tensor", "pe", lambda h, o=pb[bg][0:128, 0:tb],
                          a=wgt[:, k * FE + m * 128:k * FE + m * 128 + 128],
                          b=xt[:, k * tb:(k + 1) * tb], s=(k == 0), e2=(k == nk - 1):
                          h.matmul(o, a, b, start=s, stop=e2))
            p.use_bank("tensor", bu)
            for k in range(nk):
                cm = p.op("tensor", "pe", lambda h, o=pb[bu][0:128, 0:tb],
                          a=wut[:, k * FE + m * 128:k * FE + m * 128 + 128],
                          b=xt[:, k * tb:(k + 1) * tb], s=(k == 0), e2=(k == nk - 1):
                          h.matmul(o, a, b, start=s, stop=e2))
            last_mm = cm
            p.wait("scalar", "pe", cm)
            c1 = p.op("scalar", "act", lambda h, o=sg, a=pb[bg][0:128, 0:tb]:
                      h.activation(o, a, AF.Silu))
            p.set_bank_gate(bg, "act", c1)
            p.wait("vector", "act", c1)
            c2 = p.op("vector", "dve", lambda h, o=ev, a=pb[bu][0:128, 0:tb], b=sg:
                      h.tensor_mul(o, a, b))
            p.set_bank_gate(bu, "dve", c2)
            p.wait("scalar", "dve", c2)
            gs = p.dma("scalar", hh[m, :, it * tb:(it + 1) * tb], ev, "st")
            p.wait_t("scalar", gs)
            p.wait_t("vector", gs)
    p.fence()
    p.phase_end()
    p.phase_begin()
    wdt = p.alloc(nm * D)
    ht = p.alloc(nm * tb)
    cwr = p.alloc(tb)
    cwb = p.alloc(tb)
    at = p.alloc(tb)
    onesr2 = p.alloc(128)
    g_w2 = p.ldgroup()
    for k in range(nm):
        p.dma_g("sync", wdt[:, k * D:(k + 1) * D], wd[e, k * 128:(k + 1) * 128, :], g_w2)
    p.wait_t("tensor", g_w2)
    czz = p.op("vector", "dve", lambda h, o=onesr2[0:1, :]: h.memset(o, 1.0))
    p.wait("tensor", "dve", czz)
    last_mm = 0
    last_acc_gs = None
    for it in range(T // tb):
        if last_mm:
            p.wait("sync", "pe", last_mm)
        g_h = p.ldgroup()
        for k in range(nm):
            p.dma_g("sync", ht[:, k * tb:(k + 1) * tb], hh[k, :, it * tb:(it + 1) * tb], g_h)
        g_cw = p.dma("sync", cwr[0:1, :], cwfm[e:e + 1, it * tb:(it + 1) * tb], "ld")
        p.use_bank("tensor", 4)
        p.wait_t("tensor", g_cw)
        cmb = p.op("tensor", "pe", lambda h, o=pb[4][0:128, 0:512], a=onesr2[0:1, :],
                   b=cwr[0:1, :]: h.matmul(o, a, b, start=True, stop=True))
        p.wait("vector", "pe", cmb)
        p.wait_t("vector", g_h)
        cvb = p.op("vector", "dve", lambda h, o=cwb, a=pb[4][0:128, 0:512]:
                   h.tensor_copy(o, a))
        p.set_bank_gate(4, "dve", cvb)
        p.wait("vector", "dve", cvb)
        ch = 0
        for k in range(nm):
            ch = p.op("vector", "dve", lambda h, o=ht[:, k * tb:(k + 1) * tb],
                      a=ht[:, k * tb:(k + 1) * tb], b=cwb: h.tensor_mul(o, a, b))
        p.wait("tensor", "dve", ch)
        for md in range(16):
            bidx = md % 2
            p.use_bank("tensor", bidx)
            for k in range(nm):
                cm = p.op("tensor", "pe", lambda h, o=pb[bidx][0:128, 0:tb],
                          a=wdt[:, k * D + md * 128:k * D + md * 128 + 128],
                          b=ht[:, k * tb:(k + 1) * tb], s=(k == 0), e2=(k == nm - 1):
                          h.matmul(o, a, b, start=s, stop=e2))
            last_mm = cm
            if e == 0:
                p.wait("scalar", "pe", cm)
                ce = p.op("scalar", "act", lambda h, o=at, a=pb[bidx][0:128, 0:tb]:
                          h.copy(o, a))
                p.set_bank_gate(bidx, "act", ce)
                p.wait("scalar", "act", ce)
            else:
                g_a = p.dma("sync", at, macc[md, :, it * tb:(it + 1) * tb], "ld")
                p.wait_t("vector", g_a)
                p.wait("vector", "pe", cm)
                ce = p.op("vector", "dve", lambda h, o=at, a=pb[bidx][0:128, 0:tb], b=at:
                          h.tensor_add(o, a, b))
                p.set_bank_gate(bidx, "dve", ce)
                p.wait("scalar", "dve", ce)
            gs = p.dma("scalar", macc[md, :, it * tb:(it + 1) * tb], at, "st")
            p.wait_t("scalar", gs)
            p.wait_t("vector", gs)
            p.wait_t("sync", gs)
            last_acc_gs = gs
    p.fence()
    p.phase_end()




def run_phases(phases, inputs_by_core, fused=False, name="prog", n_cores=8):
    p = Prog(name, fused_scratch=fused)
    for ph in phases:
        ph(p)
    nc = p.finalize()
    in_names = [n for n, k in p.hbm_kind.items() if k == "in"]
    in_maps = []
    for core in range(n_cores):
        m = {n: inputs_by_core[core][n] for n in in_names}
        in_maps.append(m)
    res = run_bass_kernel_spmd(nc, in_maps, core_ids=list(range(n_cores)))
    return res.results


def l0_phases():
    return [
        lambda p: ph_ln(p, "x", "xn0"),
        lambda p: ph_matmul_fm(p, "L0_attn_Wq", [D, 2048], "xn0", "q_fm", 2048),
        lambda p: ph_matmul_fm(p, "L0_attn_Wk", [D, 512], "xn0", "k_fm", 512),
        lambda p: ph_matmul_fm(p, "L0_attn_Wv", [D, 512], "xn0", "v_fm", 512),
        lambda p: ph_attn(p),
        lambda p: ph_matmul_fm(p, "L0_attn_Wo", [2048, D], "attn_o", "wo_out", D),
        lambda p: ph_deq(p),
        lambda p: ph_ffn(p),
        lambda p: ph_merge(p, "x1_tok", "x", [("wo_out", None), ("ffn_out", None)],
                           base_is_input=True),
    ]


def l1_phases():
    return [
        lambda p: ph_ln(p, "x1_tok", "xn1", want_rms="xr1"),
        lambda p: ph_matmul_fm(p, "L1_attn_in_proj_w", [2 * DI, D], "xr1", "xz_fm",
                               2 * DI, transpose_w=True, bias_name="L1_attn_in_proj_b",
                               rhs_scale_name="L1_attn_norm_w", mslice=(0, DI)),
        lambda p: ph_matmul_fm(p, "L1_attn_in_proj_w", [2 * DI, D], "xr1", "xz_fm",
                               2 * DI, transpose_w=True, bias_name="L1_attn_in_proj_b",
                               rhs_scale_name="L1_attn_norm_w", mslice=(DI, DI)),
        lambda p: ph_conv(p),
        lambda p: ph_matmul_fm(p, "L1_attn_x_proj_w", [PD, DI], "xconv", "xp_fm",
                               PD, transpose_w=True, bias_name="L1_attn_x_proj_b"),
        lambda p: ph_scanprep1(p),
        lambda p: ph_scanprep2(p),
        lambda p: ph_scan(p),
        lambda p: ph_gate(p),
        lambda p: ph_matmul_fm(p, "L1_attn_out_proj_w", [D, DI], "yg_fm", "op_out",
                               D, transpose_w=True, bias_name="L1_attn_out_proj_b"),
        lambda p: ph_router(p),
    ] + [(lambda e: lambda p: ph_moe_expert(p, e))(e) for e in range(NE)] + [
        lambda p: ph_merge(p, "x2_tok", "x1_tok",
                           [("xn1", None), ("op_out", "rg"), ("moe_acc", None)],
                           base_is_input=False),
    ]


FUSED = True


def kernel(**inputs):
    consts = host_constants()
    base = dict(consts)
    for k, v in inputs.items():
        if k != "L0_attn_x":
            base[k] = np.ascontiguousarray(np.asarray(v))
    xfull = np.asarray(inputs["L0_attn_x"])
    percore = []
    for b in range(8):
        m = dict(base)
        m["x"] = np.ascontiguousarray(xfull[b])
        percore.append(m)

    if FUSED:
        res = run_phases(l0_phases() + l1_phases(), percore, fused=True,
                         name="mk_full", n_cores=8)
        out = np.stack([res[b]["x2_tok"] for b in range(8)], axis=0)
        return out.astype(np.float32)
    res_a = run_phases(l0_phases(), percore, name="mk_l0", n_cores=8)
    for b in range(8):
        percore[b]["x1_tok"] = np.ascontiguousarray(res_a[b]["x1_tok"])
    res_b = run_phases(l1_phases(), percore, name="mk_l1", n_cores=8)
    out = np.stack([res_b[b]["x2_tok"] for b in range(8)], axis=0)
    return out.astype(np.float32)


# version 03
# speedup vs baseline: 1.0624x; 1.0596x over previous
import math
from contextlib import ExitStack

import numpy as np
import concourse.bass as bass
import concourse.mybir as mybir
from concourse.bass_utils import run_bass_kernel_spmd

F32 = mybir.dt.float32
BF16 = mybir.dt.bfloat16
I32 = mybir.dt.int32
AF = mybir.ActivationFunctionType
ALU = mybir.AluOpType
AX = mybir.AxisListType

B, T, D = 8, 4096, 2048
NCH = D // 128
H0, HK0, HD0, WIN = 16, 4, 128, 512
NWIN = T // WIN
FF = 8192
H1, S1, DH1 = 24, 64, 64
DI = H1 * DH1
PD = H1 * (1 + 2 * S1)
NE, FE = 8, 1024

ENGINES = ("sync", "tensor", "vector", "scalar", "gpsimd")
ARENA_F32 = 47616
NEG = -1e30
F32EPS = 1.1920928955078125e-07


class Prog:

    def __init__(self, name, fused_scratch=False):
        self.nc = bass.Bass("TRN2", target_bir_lowering=False,
                            detect_race_conditions=False)
        self.name = name
        self.fused_scratch = fused_scratch
        self.external_outs = {"x2_tok"}
        self.tape = {e: [] for e in ENGINES}
        self.cnt = {}
        self.sem_names = (["pe", "dve", "act", "pool", "fence"]
                          + [f"ld{i}" for i in range(8)]
                          + [f"st{i}" for i in range(8)])
        self._ld_rr = 0
        self._st_rr = 0
        self.sems = {}
        self.ctx = ExitStack()
        self.hbm = {}
        self.hbm_kind = {}
        self._arena = None
        self._arena_off = 0
        self._phase_mark = 0
        self.psum = []
        self.bank_gate = {}
        for s in self.sem_names:
            self.cnt[s] = 0

    def inp(self, name, shape, dt=F32):
        if name not in self.hbm:
            self.hbm[name] = self.nc.dram_tensor(name, list(shape), dt,
                                                 kind="ExternalInput").ap()
            self.hbm_kind[name] = "in"
        return self.hbm[name]

    def out(self, name, shape, dt=F32):
        if name not in self.hbm:
            self.hbm[name] = self.nc.dram_tensor(name, list(shape), dt,
                                                 kind="ExternalOutput").ap()
            self.hbm_kind[name] = "out"
        return self.hbm[name]

    def scratch(self, name, shape, dt=F32, role="rw"):
        if name in self.hbm:
            return self.hbm[name]
        if self.fused_scratch and name not in self.external_outs:
            ap = self.nc.dram_tensor(name, list(shape), dt, kind="Internal").ap()
            self.hbm_kind[name] = "internal"
        elif role == "r":
            ap = self.nc.dram_tensor(name, list(shape), dt, kind="ExternalInput").ap()
            self.hbm_kind[name] = "in"
        else:
            ap = self.nc.dram_tensor(name, list(shape), dt, kind="ExternalOutput").ap()
            self.hbm_kind[name] = "out"
        self.hbm[name] = ap
        return ap

    def _ensure_arena(self):
        if self._arena is None:
            self._arena = self.ctx.enter_context(
                self.nc.sbuf_tensor("arena", [128, ARENA_F32], F32))

    def alloc(self, n, dt=F32):
        self._ensure_arena()
        nf = n if dt in (F32, I32) else (n + 1) // 2
        off = (self._arena_off + 127) & ~127
        self._arena_off = off + nf
        assert self._arena_off <= ARENA_F32, (
            f"{self.name}: arena overflow {self._arena_off} > {ARENA_F32}")
        sl = self._arena[:, off:off + nf]
        if dt == F32:
            return sl
        if dt == I32:
            return sl.bitcast(I32)
        return sl.bitcast(BF16)[:, 0:n]

    def phase_begin(self):
        self._phase_mark = self._arena_off

    def phase_end(self):
        self._arena_off = self._phase_mark

    def psum_banks(self):
        if not self.psum:
            for i in range(8):
                self.psum.append(self.ctx.enter_context(
                    self.nc.psum_tensor(f"pb{i}", [128, 512], F32)))
        return self.psum

    def use_bank(self, eng, idx):
        g = self.bank_gate.get(idx)
        if g:
            self.wait(eng, g[0], g[1])

    def set_bank_gate(self, idx, sem, count):
        self.bank_gate[idx] = (sem, count)

    def emit(self, eng, fn):
        self.tape[eng].append(fn)

    def bump(self, sem, k=1):
        self.cnt[sem] += k
        return self.cnt[sem]

    def ldgroup(self):
        self._ld_rr = (self._ld_rr + 1) % 8
        return [f"ld{self._ld_rr}", self.cnt[f"ld{self._ld_rr}"]]

    def stgroup(self):
        self._st_rr = (self._st_rr + 1) % 8
        return [f"st{self._st_rr}", self.cnt[f"st{self._st_rr}"]]

    def dma_g(self, eng, out_ap, in_ap, grp):
        sem = grp[0]
        c = self.bump(sem, 16)
        grp[1] = c
        self.emit(eng, lambda h, o=out_ap, i=in_ap, s=sem: h.dma_start(
            out=o, in_=i).then_inc(self.sems[s], 16))
        return c

    def wait_t(self, eng, grp):
        self.wait(eng, grp[0], grp[1])

    def dma(self, eng, out_ap, in_ap, grp):
        g = self.ldgroup() if grp == "ld" or grp == "ld2" else self.stgroup()
        self.dma_g(eng, out_ap, in_ap, g)
        return g

    def wait(self, eng, sem, count):
        if count <= 0:
            return
        self.emit(eng, lambda h, s=sem, c=count: h.wait_ge(self.sems[s], c))

    def op(self, eng, sem, fn):
        c = self.bump(sem, 1)
        self.emit(eng, lambda h, f=fn, s=sem: f(h).then_inc(self.sems[s], 1))
        return c

    def opq(self, eng, fn):
        self.emit(eng, lambda h, f=fn: f(h))

    def fence(self):
        for e in ENGINES:
            self.bump("fence", 1)
            self.emit(e, lambda h: h.nop().then_inc(self.sems["fence"], 1))
        tgt = self.cnt["fence"]
        for e in ENGINES:
            self.wait(e, "fence", tgt)

    def finalize(self):
        nc = self.nc
        with self.ctx:
            for s in self.sem_names:
                self.sems[s] = self.ctx.enter_context(nc.semaphore(s))
            self.psum_banks()
            with nc.Block() as block:
                @block.sync
                def _(h):
                    for fn in self.tape["sync"]:
                        fn(h)

                @block.tensor
                def _(h):
                    for fn in self.tape["tensor"]:
                        fn(h)

                @block.vector
                def _(h):
                    for fn in self.tape["vector"]:
                        fn(h)

                @block.scalar
                def _(h):
                    for fn in self.tape["scalar"]:
                        fn(h)

                @block.gpsimd
                def _(h):
                    for fn in self.tape["gpsimd"]:
                        fn(h)
        return nc



def host_constants():
    c = {}
    c["IDENT"] = np.eye(128, dtype=np.float32)
    r = np.arange(WIN)[:, None]
    col = np.arange(2 * WIN)[None, :]
    allowed = np.where(col < WIN, col > r, col - WIN <= r)
    c["MASKG"] = np.where(allowed, 0.0, NEG).astype(np.float32)
    c["MASK0"] = np.where((col >= WIN) & (col - WIN <= r), 0.0, NEG).astype(np.float32)
    tau = np.arange(128)[:, None]
    tt = np.arange(128)[None, :]
    c["LTNEG"] = np.where(tau <= tt, 0.0, NEG).astype(np.float32)
    c["EXP32"] = (np.arange(128)[None, :] // 32 == np.arange(4)[:, None]).astype(np.float32)
    c["IOTA8"] = np.tile(np.arange(8, dtype=np.float32)[None, :], (128, 1))
    return c



def ph_ln(p, x_name, xn_name, want_rms=None):
    eps = 1e-5
    p.phase_begin()
    x = p.inp("x", [T, D]) if x_name == "x" else p.scratch(x_name, [T, D], role="r")
    xn = p.scratch(xn_name, [NCH, 128, T], role="w")
    xr = p.scratch(want_rms, [NCH, 128, T], role="w") if want_rms else None
    ident = p.inp("IDENT", [128, 128])

    xt = p.alloc(D)
    sq = p.alloc(D)
    stats = p.alloc(16)
    xnt = p.alloc(D)
    xrt = p.alloc(D) if want_rms else None
    idt = p.alloc(128)
    evac = p.alloc(256)
    pb = p.psum_banks()

    g_id = p.dma("sync", idt, ident[:, :], "ld")
    p.wait_t("tensor", g_id)

    g_store = None
    ntile = T // 128
    for i in range(ntile):
        if g_store:
            p.wait_t("sync", g_store)
        g_x = p.dma("sync", xt, x[i * 128:(i + 1) * 128, :], "ld")
        p.wait_t("vector", g_x)
        c0 = p.op("vector", "dve", lambda h, o=stats[:, 0:1], a=xt: h.reduce_sum(o, a, axis=AX.X))
        p.wait("vector", "dve", c0)
        c0b = p.op("vector", "dve", lambda h, o=stats[:, 1:2], a=stats[:, 0:1]: h.tensor_scalar(
            o, a, 1.0 / D, None, ALU.mult))
        p.wait("vector", "dve", c0b)
        c1 = p.op("vector", "dve", lambda h, o=xnt, a=xt, m=stats[:, 1:2]: h.tensor_scalar(
            o, a, m, None, ALU.subtract))
        p.wait("scalar", "dve", c1)
        c2 = p.op("scalar", "act", lambda h, o=sq, a=xnt: h.activation(o, a, AF.Square))
        p.wait("vector", "act", c2)
        c2b = p.op("vector", "dve", lambda h, o=stats[:, 2:3], a=sq: h.reduce_sum(o, a, axis=AX.X))
        p.wait("vector", "dve", c2b)
        c2c = p.op("vector", "dve", lambda h, o=stats[:, 3:4], a=stats[:, 2:3]: h.tensor_scalar(
            o, a, 1.0 / D, eps, ALU.mult, ALU.add))
        p.wait("vector", "dve", c2c)
        c3 = p.op("vector", "dve", lambda h, o=stats[:, 4:5], a=stats[:, 3:4]: h.reciprocal(o, a))
        p.wait("scalar", "dve", c3)
        c4 = p.op("scalar", "act", lambda h, o=stats[:, 5:6], a=stats[:, 4:5]: h.activation(
            o, a, AF.Sqrt))
        p.wait("vector", "act", c4)
        cln = p.op("vector", "dve", lambda h, o=xnt, a=xnt, m=stats[:, 5:6]: h.tensor_scalar(
            o, a, m, None, ALU.mult))
        clast = cln
        if want_rms:
            p.wait("scalar", "dve", cln)
            c6 = p.op("scalar", "act", lambda h, o=sq, a=xnt: h.activation(o, a, AF.Square))
            p.wait("vector", "act", c6)
            c6b = p.op("vector", "dve", lambda h, o=stats[:, 6:7], a=sq: h.reduce_sum(o, a, axis=AX.X))
            p.wait("vector", "dve", c6b)
            c6c = p.op("vector", "dve", lambda h, o=stats[:, 7:8], a=stats[:, 6:7]: h.tensor_scalar(
                o, a, 1.0 / D, F32EPS, ALU.mult, ALU.add))
            p.wait("vector", "dve", c6c)
            c7 = p.op("vector", "dve", lambda h, o=stats[:, 8:9], a=stats[:, 7:8]: h.reciprocal(o, a))
            p.wait("scalar", "dve", c7)
            c8 = p.op("scalar", "act", lambda h, o=stats[:, 9:10], a=stats[:, 8:9]: h.activation(
                o, a, AF.Sqrt))
            p.wait("vector", "act", c8)
            clast = p.op("vector", "dve", lambda h, o=xrt, a=xnt, m=stats[:, 9:10]:
                         h.tensor_scalar(o, a, m, None, ALU.mult))
        p.wait("tensor", "dve", clast)
        g_store = p.stgroup()
        for c in range(NCH):
            bank = pb[c % 2]
            p.use_bank("tensor", c % 2)
            cm = p.op("tensor", "pe", lambda h, o=bank[0:128, 0:128],
                      a=xnt[:, c * 128:(c + 1) * 128], i=idt: h.transpose(o, a, i))
            p.wait("scalar", "pe", cm)
            ce = p.op("scalar", "act", lambda h, o=evac[:, 0:128], a=bank[0:128, 0:128]:
                      h.copy(o, a))
            if not want_rms:
                p.set_bank_gate(c % 2, "act", ce)
            gs = p.dma("scalar", xn[c, :, i * 128:(i + 1) * 128], evac[:, 0:128], "st")
            p.wait_t("scalar", gs)
            if want_rms:
                cm2 = p.op("tensor", "pe", lambda h, o=bank[0:128, 128:256],
                           a=xrt[:, c * 128:(c + 1) * 128], i=idt: h.transpose(o, a, i))
                p.wait("scalar", "pe", cm2)
                ce2 = p.op("scalar", "act", lambda h, o=evac[:, 128:256], a=bank[0:128, 128:256]:
                           h.copy(o, a))
                p.set_bank_gate(c % 2, "act", ce2)
                gs = p.dma("scalar", xr[c, :, i * 128:(i + 1) * 128], evac[:, 128:256], "st")
                p.wait_t("scalar", gs)
        g_store = gs
    p.fence()
    p.phase_end()


def ph_matmul_fm(p, w_name, w_shape, in_name, out_name, n_out,
                 transpose_w=False, bias_name=None, tb=None, out_scale=None,
                 rhs_scale_name=None, mslice=None):
    K = w_shape[0] if not transpose_w else w_shape[1]
    nm_all = (n_out + 127) // 128
    m0, M = (0, n_out) if mslice is None else mslice
    assert m0 % 128 == 0
    nk = K // 128
    nm = (M + 127) // 128
    cm0 = m0 // 128
    if tb is None:
        tb = 256 if nk * M > 24576 else 512
    p.phase_begin()
    w = p.inp(w_name, list(w_shape))
    xin = p.scratch(in_name, [nk, 128, T], role="r")
    xout = p.scratch(out_name, [nm_all, 128, T], role="w")
    bias = p.inp(bias_name, [n_out]) if bias_name else None
    rsv = p.inp(rhs_scale_name, [K]) if rhs_scale_name else None

    wt = p.alloc(nk * M)
    xt = [p.alloc(nk * tb), p.alloc(nk * tb)]
    ev = p.alloc(4 * tb)
    bt = p.alloc(nm) if bias_name else None
    rst = p.alloc(nk) if rhs_scale_name else None
    pb = p.psum_banks()

    c_dve_w = 0
    g_w = p.ldgroup()
    if not transpose_w:
        for k in range(nk):
            p.dma_g("sync", wt[:, k * M:(k + 1) * M],
                    w[k * 128:(k + 1) * 128, m0:m0 + M], g_w)
    else:
        ident = p.inp("IDENT", [128, 128])
        idt = p.alloc(128)
        wtmp = p.alloc(K)
        g_id = p.dma("sync", idt, ident[:, :], "ld")
        p.wait_t("tensor", g_id)
        for m in range(nm):
            mrows = min(128, M - m * 128)
            g_wt = p.dma("sync", wtmp[0:mrows, :],
                         w[m0 + m * 128:m0 + m * 128 + mrows, :], "ld")
            p.wait_t("tensor", g_wt)
            for k in range(nk):
                bidx = 6 + (k % 2)
                bank = pb[bidx]
                p.use_bank("tensor", bidx)
                cm = p.op("tensor", "pe", lambda h, o=bank[0:128, 0:128],
                          a=wtmp[0:mrows, k * 128:(k + 1) * 128], i=idt, mr=mrows:
                          h.transpose(o[0:128, 0:mr], a, i[0:mr, 0:mr]))
                p.wait("vector", "pe", cm)
                c_dve_w = p.op("vector", "dve",
                               lambda h, o=wt[:, k * M + m * 128:k * M + m * 128 + mrows],
                               a=bank[0:128, 0:mrows]: h.tensor_copy(o, a))
                p.set_bank_gate(bidx, "dve", c_dve_w)
            p.wait("sync", "dve", c_dve_w)
    if bias_name:
        for m in range(nm):
            mrows = min(128, M - m * 128)
            p.dma_g("sync", bt[0:mrows, m:m + 1],
                    bias[m0 + m * 128:m0 + m * 128 + mrows].unsqueeze(1), g_w)
    if rhs_scale_name:
        for k in range(nk):
            p.dma_g("sync", rst[0:128, k:k + 1],
                    rsv[k * 128:(k + 1) * 128].unsqueeze(1), g_w)

    ntb = T // tb
    p.wait_t("scalar", g_w)
    p.wait_t("vector", g_w)
    p.wait_t("tensor", g_w)

    def issue_loads(it):
        buf = it % 2
        g_x = p.ldgroup()
        for k in range(nk):
            p.dma_g("sync", xt[buf][:, k * tb:(k + 1) * tb],
                    xin[k, :, it * tb:(it + 1) * tb], g_x)
        return g_x

    gs_slot = [None] * 4
    mm_of = {}
    g_of = {}
    g_of[0] = issue_loads(0)
    for it in range(ntb):
        buf = it % 2
        if it + 1 < ntb:
            if it >= 1:
                p.wait("sync", "pe", mm_of[it - 1])
            g_of[it + 1] = issue_loads(it + 1)
        g_x = g_of[it]
        xb = xt[buf]
        if rhs_scale_name:
            p.wait_t("vector", g_x)
            cvs = 0
            for k in range(nk):
                cvs = p.op("vector", "dve", lambda h, o=xb[:, k * tb:(k + 1) * tb],
                           a=xb[:, k * tb:(k + 1) * tb], m=rst[0:128, k:k + 1]:
                           h.tensor_scalar(o, a, m, None, ALU.mult))
            p.wait("tensor", "dve", cvs)
        p.wait_t("tensor", g_x)
        if transpose_w:
            p.wait("tensor", "dve", c_dve_w)
        for m in range(nm):
            mrows = min(128, M - m * 128)
            bank = pb[m % 2]
            p.use_bank("tensor", m % 2)
            for k in range(nk):
                cm = p.op("tensor", "pe", lambda h, o=bank[0:mrows, 0:tb],
                          a=wt[:, k * M + m * 128:k * M + m * 128 + mrows],
                          b=xb[:, k * tb:(k + 1) * tb], s=(k == 0), e=(k == nk - 1):
                          h.matmul(o, a, b, start=s, stop=e))
            mm_of[it] = cm
            slot = m % 4
            eslot = ev[:, slot * tb:slot * tb + tb]
            if bias_name:
                p.wait("vector", "pe", cm)
                if gs_slot[slot]:
                    p.wait_t("vector", gs_slot[slot])
                ce = p.op("vector", "dve", lambda h, o=eslot, a=bank[0:mrows, 0:tb],
                          bb=bt[0:mrows, m:m + 1], mr=mrows:
                          h.tensor_scalar(o[0:mr, :], a, bb, None, ALU.add))
                p.set_bank_gate(m % 2, "dve", ce)
                p.wait("scalar", "dve", ce)
            else:
                p.wait("scalar", "pe", cm)
                if gs_slot[slot]:
                    p.wait_t("scalar", gs_slot[slot])
                if out_scale is not None:
                    ce = p.op("scalar", "act", lambda h, o=eslot, a=bank[0:mrows, 0:tb],
                              mr=mrows: h.activation(o[0:mr, :], a, AF.Copy, scale=out_scale))
                else:
                    ce = p.op("scalar", "act", lambda h, o=eslot, a=bank[0:mrows, 0:tb],
                              mr=mrows: h.copy(o[0:mr, :], a))
                p.set_bank_gate(m % 2, "act", ce)
            gs_slot[slot] = p.dma("scalar", xout[cm0 + m, 0:mrows, it * tb:(it + 1) * tb],
                                  eslot[0:mrows, :], "st")
    for gslt in gs_slot:
        if gslt:
            p.wait_t("scalar", gslt)
    p.fence()
    p.phase_end()


def ph_attn(p):
    nwin = T // WIN
    scale = 1.0 / math.sqrt(HD0)
    p.phase_begin()
    q_fm = p.scratch("q_fm", [H0, 128, T], role="r")
    k_fm = p.scratch("k_fm", [HK0, 128, T], role="r")
    v_fm = p.scratch("v_fm", [HK0, 128, T], role="r")
    attn_o = p.scratch("attn_o", [H0, 128, T], role="w")
    maskg = p.inp("MASKG", [WIN, 2 * WIN])
    mask0 = p.inp("MASK0", [WIN, 2 * WIN])
    ident = p.inp("IDENT", [128, 128])

    kc = [p.alloc(1024) for _ in range(HK0)]
    vt = [p.alloc(1024) for _ in range(HK0)]
    vstg = p.alloc(512)
    qh = p.alloc(512)
    sc = p.alloc(1024)
    pp = p.alloc(1024)
    pt = p.alloc(8 * 512)
    oe = p.alloc(512)
    mskg = p.alloc(4 * 1024)
    msk0 = p.alloc(4 * 1024)
    stats = p.alloc(8)
    idt = p.alloc(128)
    pb = p.psum_banks()

    g_const = p.ldgroup()
    p.dma_g("sync", idt, ident[:, :], g_const)
    for sub in range(4):
        p.dma_g("sync", mskg[:, sub * 1024:(sub + 1) * 1024],
                maskg[sub * 128:(sub + 1) * 128, :], g_const)
        p.dma_g("sync", msk0[:, sub * 1024:(sub + 1) * 1024],
                mask0[sub * 128:(sub + 1) * 128, :], g_const)
    p.wait_t("tensor", g_const)
    p.wait_t("vector", g_const)
    cz = 0
    for g in range(HK0):
        p.op("vector", "dve", lambda h, o=kc[g][:, 0:512]: h.memset(o, 0.0))
        cz = p.op("vector", "dve", lambda h, o=vt[g][:, 0:512]: h.memset(o, 0.0))
    p.wait("tensor", "dve", cz)

    last_pv = 0
    for n in range(nwin):
        msk = msk0 if n == 0 else mskg
        for g in range(HK0):
            if last_pv:
                p.wait("sync", "pe", last_pv)
            g_k = p.ldgroup()
            if n > 0:
                p.dma_g("sync", kc[g][:, 0:512], k_fm[g, :, (n - 1) * WIN:n * WIN], g_k)
            p.dma_g("sync", kc[g][:, 512:1024], k_fm[g, :, n * WIN:(n + 1) * WIN], g_k)
            halves = ([1] if n == 0 else [0, 1])
            c_vt = 0
            last_tr = 0
            for hv in halves:
                wv = n - 1 + hv
                if last_tr:
                    p.wait("sync", "pe", last_tr)
                g_v = p.dma("sync", vstg, v_fm[g, :, wv * WIN:(wv + 1) * WIN], "ld")
                p.wait_t("tensor", g_v)
                for j in range(4):
                    bidx = 5 + (j % 2)
                    bank = pb[bidx]
                    p.use_bank("tensor", bidx)
                    cm = p.op("tensor", "pe", lambda h, o=bank[0:128, 0:128],
                              a=vstg[:, j * 128:(j + 1) * 128], i=idt: h.transpose(o, a, i))
                    p.wait("scalar", "pe", cm)
                    c_vt = p.op("scalar", "act", lambda h,
                                o=vt[g][:, (hv * 4 + j) * 128:(hv * 4 + j + 1) * 128],
                                a=bank[0:128, 0:128]: h.copy(o, a))
                    p.set_bank_gate(bidx, "act", c_vt)
                last_tr = cm
            p.wait_t("tensor", g_k)
            p.wait("tensor", "act", c_vt)
            for r in range(4):
                hq = g * 4 + r
                if last_pv:
                    p.wait("sync", "pe", last_pv)
                g_q = p.dma("sync", qh, q_fm[hq, :, n * WIN:(n + 1) * WIN], "ld")
                p.wait_t("tensor", g_q)
                cpt = 0
                for sub in range(4):
                    p.use_bank("tensor", 0)
                    p.use_bank("tensor", 1)
                    p.op("tensor", "pe", lambda h, o=pb[0][0:128, 0:512],
                         a=qh[:, sub * 128:(sub + 1) * 128], b=kc[g][:, 0:512]:
                         h.matmul(o, a, b, start=True, stop=True))
                    cm1 = p.op("tensor", "pe", lambda h, o=pb[1][0:128, 0:512],
                               a=qh[:, sub * 128:(sub + 1) * 128], b=kc[g][:, 512:1024]:
                               h.matmul(o, a, b, start=True, stop=True))
                    p.wait("vector", "pe", cm1)
                    cd0 = p.op("vector", "dve", lambda h, o=sc[:, 0:512],
                               a=pb[0][0:128, 0:512], m=msk[:, sub * 1024:sub * 1024 + 512]:
                               h.scalar_tensor_tensor(o, a, scale, m, ALU.mult, ALU.add))
                    p.set_bank_gate(0, "dve", cd0)
                    cd1 = p.op("vector", "dve", lambda h, o=sc[:, 512:1024],
                               a=pb[1][0:128, 0:512], m=msk[:, sub * 1024 + 512:sub * 1024 + 1024]:
                               h.scalar_tensor_tensor(o, a, scale, m, ALU.mult, ALU.add))
                    p.set_bank_gate(1, "dve", cd1)
                    cd2 = p.op("vector", "dve", lambda h, o=stats[:, 0:1], a=sc:
                               h.reduce_max(o, a, axis=AX.X, negate=True))
                    p.wait("scalar", "dve", cd2)
                    ce = p.op("scalar", "act", lambda h, o=pp, a=sc, b=stats[:, 0:1]:
                              h.activation(o, a, AF.Exp, bias=b))
                    p.wait("vector", "act", ce)
                    cd3 = p.op("vector", "dve", lambda h, o=stats[:, 1:2], a=pp:
                               h.reduce_sum(o, a, axis=AX.X))
                    p.wait("vector", "dve", cd3)
                    cd4 = p.op("vector", "dve", lambda h, o=stats[:, 2:3], a=stats[:, 1:2]:
                               h.reciprocal(o, a))
                    p.wait("vector", "dve", cd4)
                    cd5 = p.op("vector", "dve", lambda h, o=pp, a=pp, m=stats[:, 2:3]:
                               h.tensor_scalar(o, a, m, None, ALU.mult))
                    p.wait("tensor", "dve", cd5)
                    for j in range(8):
                        bidx = 2 + (j % 2)
                        bank = pb[bidx]
                        p.use_bank("tensor", bidx)
                        cm = p.op("tensor", "pe", lambda h, o=bank[0:128, 0:128],
                                  a=pp[:, j * 128:(j + 1) * 128], i=idt: h.transpose(o, a, i))
                        p.wait("scalar", "pe", cm)
                        cpt = p.op("scalar", "act", lambda h,
                                   o=pt[:, j * 512 + sub * 128:j * 512 + sub * 128 + 128],
                                   a=bank[0:128, 0:128]: h.copy(o, a))
                        p.set_bank_gate(bidx, "act", cpt)
                p.use_bank("tensor", 4)
                p.wait("tensor", "act", cpt)
                for j in range(8):
                    cm = p.op("tensor", "pe", lambda h, o=pb[4][0:128, 0:512],
                              a=vt[g][:, j * 128:(j + 1) * 128], b=pt[:, j * 512:(j + 1) * 512],
                              s=(j == 0), e=(j == 7): h.matmul(o, a, b, start=s, stop=e))
                last_pv = cm
                p.wait("scalar", "pe", cm)
                co = p.op("scalar", "act", lambda h, o=oe, a=pb[4][0:128, 0:512]: h.copy(o, a))
                p.set_bank_gate(4, "act", co)
                gs = p.dma("scalar", attn_o[hq, :, n * WIN:(n + 1) * WIN], oe, "st")
                p.wait_t("scalar", gs)
    p.fence()
    p.phase_end()




def ph_merge(p, out_name, base_name, terms, base_is_input=False):
    p.phase_begin()
    base = (p.inp(base_name, [T, D]) if base_is_input
            else p.scratch(base_name, [T, D], role="r"))
    outt = p.scratch(out_name, [T, D], role="w")
    fms = [p.scratch(nm, [NCH, 128, T], role="r") for nm, _ in terms]
    ident = p.inp("IDENT", [128, 128])
    need_rg = any(k == "rg" for _, k in terms)

    xb = p.alloc(D)
    ft = [p.alloc(D) for _ in terms]
    idt = p.alloc(128)
    rgc = p.alloc(4) if need_rg else None
    onec = p.alloc(128) if need_rg else None
    pb = p.psum_banks()

    g_c = p.ldgroup()
    p.dma_g("sync", idt, ident[:, :], g_c)
    if need_rg:
        rg = p.inp("L1_attn_res_gate", [1])
        p.dma_g("sync", rgc[0:1, 0:1], rg[0:1].unsqueeze(1), g_c)
    p.wait_t("tensor", g_c)
    p.wait_t("vector", g_c)
    if need_rg:
        cz = p.op("vector", "dve", lambda h, o=onec[0:1, 0:128]: h.memset(o, 1.0))
        p.wait("tensor", "dve", cz)
        cmb = p.op("tensor", "pe", lambda h, o=pb[3][0:128, 0:1], a=onec[0:1, 0:128],
                   b=rgc[0:1, 0:1]: h.matmul(o, a, b, start=True, stop=True))
        p.wait("vector", "pe", cmb)
        crg = p.op("vector", "dve", lambda h, o=rgc[:, 1:2], a=pb[3][0:128, 0:1]:
                   h.tensor_copy(o, a))
        p.set_bank_gate(3, "dve", crg)

    ntile = T // 128
    last_gs = None
    for i in range(ntile):
        if last_gs:
            p.wait_t("sync", last_gs)
        g_x = p.ldgroup()
        p.dma_g("sync", xb, base[i * 128:(i + 1) * 128, :], g_x)
        for t_i, (nm, _) in enumerate(terms):
            p.dma_g("sync", ft[t_i].rearrange("p (c t) -> p c t", t=128),
                    fms[t_i][:, :, i * 128:(i + 1) * 128].rearrange("c p t -> p c t"), g_x)
        p.wait_t("tensor", g_x)
        p.wait_t("vector", g_x)
        for c in range(NCH):
            for t_i, (nm, kind) in enumerate(terms):
                bidx = t_i % 2
                bank = pb[bidx]
                p.use_bank("tensor", bidx)
                cm = p.op("tensor", "pe", lambda h, o=bank[0:128, 0:128],
                          a=ft[t_i][:, c * 128:(c + 1) * 128], i=idt: h.transpose(o, a, i))
                p.wait("vector", "pe", cm)
                if kind == "rg":
                    cd = p.op("vector", "dve", lambda h, o=xb[:, c * 128:(c + 1) * 128],
                              a=bank[0:128, 0:128], m=rgc[:, 1:2], x2=xb[:, c * 128:(c + 1) * 128]:
                              h.scalar_tensor_tensor(o, a, m, x2, ALU.mult, ALU.add))
                else:
                    cd = p.op("vector", "dve", lambda h, o=xb[:, c * 128:(c + 1) * 128],
                              a=bank[0:128, 0:128], x2=xb[:, c * 128:(c + 1) * 128]:
                              h.tensor_tensor(o, a, x2, ALU.add))
                p.set_bank_gate(bidx, "dve", cd)
        p.wait("scalar", "dve", cd)
        gs = p.dma("scalar", outt[i * 128:(i + 1) * 128, :], xb, "st")
        p.wait_t("scalar", gs)
        last_gs = gs
    p.fence()
    p.phase_end()


def _deq_weight(p, w_name, Kd, Nd, out_name, sc_name):
    nkc = Kd // 128
    nkb = Kd // 32
    nnc = Nd // 128
    w = p.inp(w_name, [Kd, Nd])
    wout = p.scratch(out_name, [nkc, 128, Nd], BF16, role="w")
    schbm = p.scratch(sc_name, [nkb, Nd], role="w")
    sihbm = p.scratch(sc_name + "_inv", [nkb, Nd], role="w")
    ident = p.inp("IDENT", [128, 128])
    exp32 = p.inp("EXP32", [4, 128])
    pb = p.psum_banks()

    wblk = p.alloc(128)
    wT = p.alloc(128)
    scT = p.alloc(nkc * 4)
    scTi = p.alloc(nkc * 4, I32)
    msk_m = p.alloc(nkc * 4, I32)
    idt = p.alloc(128)
    e32 = p.alloc(128)
    ev = p.alloc(128)

    g_c = p.ldgroup()
    p.dma_g("sync", idt, ident[:, :], g_c)
    p.dma_g("sync", e32[0:4, :], exp32[:, :], g_c)
    p.wait_t("tensor", g_c)
    p.wait_t("vector", g_c)
    cmsk = p.op("vector", "dve", lambda h, o=msk_m: h.memset(o, -8388608))

    last_sc_store = None
    last_ce = 0
    for nn in range(nnc):
        if last_sc_store:
            p.wait_t("sync", last_sc_store)
        if last_ce:
            p.wait("vector", "act", last_ce)
        prev_evt = 0
        for kk in range(nkc):
            if prev_evt:
                p.wait("sync", "dve", prev_evt)
            g_wb = p.dma("sync", wblk, w[kk * 128:(kk + 1) * 128,
                                         nn * 128:(nn + 1) * 128], "ld")
            p.wait_t("tensor", g_wb)
            bidx = 0 if kk % 2 == 0 else 1
            bank = pb[bidx]
            p.use_bank("tensor", bidx)
            cm = p.op("tensor", "pe", lambda h, o=bank[0:128, 0:128], a=wblk, i=idt:
                      h.transpose(o, a, i))
            p.wait("vector", "pe", cm)
            cd = p.op("vector", "dve", lambda h, o=wT, a=bank[0:128, 0:128]:
                      h.tensor_copy(o, a))
            p.set_bank_gate(bidx, "dve", cd)
            p.wait("vector", "dve", cd)
            prev_evt = p.op("vector", "dve", lambda h, o=scT[:, kk * 4:(kk + 1) * 4],
                            a=wT.rearrange("p (g k) -> p g k", k=32):
                            h.reduce_max(o, a, axis=AX.X, apply_absolute_value=True))
        p.wait("vector", "dve", prev_evt)
        ci1 = p.op("vector", "dve", lambda h, o=scTi, a=scT.bitcast(I32), m=msk_m:
                   h.tensor_tensor(o, a, m, ALU.bitwise_and))
        p.wait("vector", "dve", ci1)
        ci2 = p.op("vector", "dve", lambda h, o=scT, a=scTi.bitcast(F32):
                   h.tensor_scalar(o, a, 0.25, None, ALU.mult))
        p.wait("vector", "dve", ci2)
        ci3 = p.op("vector", "dve", lambda h, o=scTi, a=scT.bitcast(I32):
                   h.tensor_scalar(o, a, -1.0, 2130706432.0, ALU.mult, ALU.add))
        p.wait("tensor", "dve", ci3)
        nkb_cols = nkc * 4
        for src, dsthbm in ((scT, schbm), (scTi.bitcast(F32), sihbm)):
            for q0 in range(0, nkb_cols, 128):
                qn = min(128, nkb_cols - q0)
                bank = pb[2]
                p.use_bank("tensor", 2)
                cm = p.op("tensor", "pe", lambda h, o=bank[0:128, 0:128], a=src[:, q0:q0 + qn],
                          i=idt, q=qn: h.transpose(o[0:q, 0:128], a, i))
                p.wait("scalar", "pe", cm)
                ce = p.op("scalar", "act", lambda h, o=ev[0:128, 0:128],
                          a=bank[0:128, 0:128], q=qn: h.copy(o[0:q, :], a[0:q, :]))
                p.set_bank_gate(2, "act", ce)
                last_ce = ce
                last_sc_store = p.dma("scalar", dsthbm[q0:q0 + qn, nn * 128:(nn + 1) * 128],
                                      ev[0:qn, 0:128], "st")
                p.wait_t("scalar", last_sc_store)

    p.wait_t("sync", last_sc_store)
    CW = min(Nd, 4096)
    nch_col = Nd // CW
    wsl = p.alloc(CW)
    scn = p.alloc(CW)
    inv = p.alloc(CW)
    aa = p.alloc(CW)
    t1 = p.alloc(CW, I32)
    low = p.alloc(CW)
    lowb = p.alloc(CW)
    hmk = p.alloc(CW, I32)
    sgn = p.alloc(CW)
    obf = p.alloc(CW, BF16)
    last_b = None
    for kk in range(nkc):
        for cc in range(nch_col):
            if last_b:
                p.wait_t("sync", last_b)
            g_w = p.ldgroup()
            p.dma_g("sync", wsl, w[kk * 128:(kk + 1) * 128, cc * CW:(cc + 1) * CW], g_w)
            g_s = p.ldgroup()
            p.dma_g("sync", lowb[0:4, 0:CW],
                    schbm[kk * 4:(kk + 1) * 4, cc * CW:(cc + 1) * CW], g_s)
            p.dma_g("sync", hmk.bitcast(F32)[0:4, 0:CW],
                    sihbm[kk * 4:(kk + 1) * 4, cc * CW:(cc + 1) * CW], g_s)
            p.wait_t("tensor", g_s)
            nslice = CW // 512
            cdx = 0
            for src4, dstfull in ((lowb[0:4, 0:CW], scn), (hmk.bitcast(F32)[0:4, 0:CW], inv)):
                for jj in range(nslice):
                    bidx = 3 + (jj % 2)
                    bank = pb[bidx]
                    p.use_bank("tensor", bidx)
                    cm = p.op("tensor", "pe", lambda h, o=bank[0:128, 0:512],
                              a=e32[0:4, :], b=src4[:, jj * 512:(jj + 1) * 512]:
                              h.matmul(o, a, b, start=True, stop=True))
                    p.wait("vector", "pe", cm)
                    cdx = p.op("vector", "dve", lambda h, o=dstfull[:, jj * 512:(jj + 1) * 512],
                               a=bank[0:128, 0:512]: h.tensor_copy(o, a))
                    p.set_bank_gate(bidx, "dve", cdx)
            p.wait("vector", "dve", cdx)
            p.wait_t("vector", g_w)
            c2 = cdx
            p.wait_t("scalar", g_w)
            c3 = p.op("scalar", "act", lambda h, o=aa, a=wsl: h.activation(o, a, AF.Abs))
            c3b = p.op("scalar", "act", lambda h, o=sgn, a=wsl: h.activation(o, a, AF.Sign))
            p.wait("vector", "act", c3b)
            c4 = p.op("vector", "dve", lambda h, o=aa, a=aa, b=inv: h.tensor_mul(o, a, b))
            p.wait("vector", "dve", c4)
            cch = p.op("vector", "dve", lambda h, o=low, a=aa: h.tensor_scalar(
                o, a, 0.25, None, ALU.is_ge))
            for thr in (0.75, 1.25, 1.75, 2.5, 2.5, 3.5, 3.5, 5.0, 5.0, 5.0, 5.0):
                p.wait("vector", "dve", cch)
                cch = p.op("vector", "dve", lambda h, o=low, a=aa, b=low, t=thr:
                           h.scalar_tensor_tensor(o, a, t, b, ALU.is_ge, ALU.add))
            p.wait("vector", "dve", cch)
            c13 = p.op("vector", "dve", lambda h, o=aa, a=low, b=sgn: h.tensor_mul(o, a, b))
            p.wait("vector", "dve", c13)
            c14 = p.op("vector", "dve", lambda h, o=obf, a=aa, b=scn:
                       h.scalar_tensor_tensor(o, a, 0.5, b, ALU.mult, ALU.mult))
            p.wait("scalar", "dve", c14)
            last_b = p.dma("scalar", wout[kk, :, cc * CW:(cc + 1) * CW], obf, "st")
            p.wait_t("scalar", last_b)


def ph_deq(p):
    p.phase_begin()
    _deq_weight(p, "L0_ffn_up_kernel", D, FF, "wup_bf", "sc_up")
    p.fence()
    p.phase_end()
    p.phase_begin()
    _deq_weight(p, "L0_ffn_down_kernel", FF, D, "wdn_bf", "sc_dn")
    p.fence()
    p.phase_end()


def ph_ffn(p):
    tb = 512
    p.phase_begin()
    xn = p.scratch("xn0", [NCH, 128, T], role="r")
    wup = p.scratch("wup_bf", [NCH, 128, FF], BF16, role="r")
    wdn = p.scratch("wdn_bf", [FF // 128, 128, D], BF16, role="r")
    bup = p.inp("L0_ffn_up_bias", [FF])
    bdn = p.inp("L0_ffn_down_bias", [D])
    out = p.scratch("ffn_out", [NCH, 128, T], role="w")

    NQ = 4
    QH = FF // NQ
    nmq = QH // 128

    wbuf = p.alloc(16 * QH, BF16)
    xhi = p.alloc(16 * tb, BF16)
    xlo = p.alloc(16 * tb, BF16)
    hhi = p.alloc(nmq * tb, BF16)
    hlo = p.alloc(nmq * tb, BF16)
    acc = p.alloc(16 * tb)
    ght = p.alloc(tb)
    ev = p.alloc(tb)
    bupt = p.alloc(FF // 128)
    bdnt = p.alloc(16)
    pb = p.psum_banks()

    g_b = p.ldgroup()
    for m in range(FF // 128):
        p.dma_g("sync", bupt[:, m:m + 1], bup[m * 128:(m + 1) * 128].unsqueeze(1), g_b)
    for m in range(16):
        p.dma_g("sync", bdnt[:, m:m + 1], bdn[m * 128:(m + 1) * 128].unsqueeze(1), g_b)
    p.wait_t("scalar", g_b)
    p.wait_t("vector", g_b)

    last_up_mm = 0
    last_dn_mm = 0
    for it in range(T // tb):
        if last_up_mm:
            p.wait("sync", "pe", last_up_mm)
        cspl = 0
        for k in range(NCH):
            if cspl:
                p.wait("sync", "dve", cspl)
            g_k = p.dma("sync", ght, xn[k, :, it * tb:(it + 1) * tb], "ld")
            p.wait_t("vector", g_k)
            ch = p.op("vector", "dve", lambda h, o=xhi[:, k * tb:(k + 1) * tb], a=ght:
                      h.tensor_copy(o, a))
            p.wait("vector", "dve", ch)
            cspl = p.op("vector", "dve", lambda h, o=xlo[:, k * tb:(k + 1) * tb], a=ght,
                        b=xhi[:, k * tb:(k + 1) * tb]: h.tensor_sub(o, a, b))
        p.wait("tensor", "dve", cspl)
        for q in range(NQ):
            if last_up_mm:
                p.wait("sync", "pe", max(last_up_mm, last_dn_mm))
            g_w = p.ldgroup()
            for k in range(NCH):
                p.dma_g("sync", wbuf[:, k * QH:(k + 1) * QH],
                        wup[k, :, q * QH:(q + 1) * QH], g_w)
            p.wait_t("tensor", g_w)
            chl = 0
            for m in range(nmq):
                bidx = m % 2
                bank = pb[bidx]
                p.use_bank("tensor", bidx)
                for k in range(NCH):
                    cm = p.op("tensor", "pe", lambda h, o=bank[0:128, 0:tb],
                              a=wbuf[:, k * QH + m * 128:k * QH + m * 128 + 128],
                              b=xhi[:, k * tb:(k + 1) * tb], s=(k == 0):
                              h.matmul(o, a, b, start=s, stop=False))
                    cm = p.op("tensor", "pe", lambda h, o=bank[0:128, 0:tb],
                              a=wbuf[:, k * QH + m * 128:k * QH + m * 128 + 128],
                              b=xlo[:, k * tb:(k + 1) * tb], e=(k == NCH - 1):
                              h.matmul(o, a, b, start=False, stop=e))
                last_up_mm = cm
                p.wait("scalar", "pe", cm)
                cg = p.op("scalar", "act", lambda h, o=ght, a=bank[0:128, 0:tb],
                          bb=bupt[:, q * nmq + m:q * nmq + m + 1]:
                          h.activation(o, a, AF.Gelu_apprx_tanh, bias=bb))
                p.set_bank_gate(bidx, "act", cg)
                p.wait("vector", "act", cg)
                ch = p.op("vector", "dve", lambda h, o=hhi[:, m * tb:(m + 1) * tb], a=ght:
                          h.tensor_copy(o, a))
                p.wait("vector", "dve", ch)
                chl = p.op("vector", "dve", lambda h, o=hlo[:, m * tb:(m + 1) * tb], a=ght,
                           b=hhi[:, m * tb:(m + 1) * tb]: h.tensor_sub(o, a, b))
            p.wait("sync", "pe", last_up_mm)
            g_w2 = p.ldgroup()
            for k in range(nmq):
                p.dma_g("sync", wbuf[:, k * D:(k + 1) * D],
                        wdn[q * nmq + k, :, :], g_w2)
            p.wait_t("tensor", g_w2)
            p.wait("tensor", "dve", chl)
            for md in range(16):
                bidx = 2 + (md % 2)
                bank = pb[bidx]
                p.use_bank("tensor", bidx)
                for k in range(nmq):
                    cm = p.op("tensor", "pe", lambda h, o=bank[0:128, 0:tb],
                              a=wbuf[:, k * D + md * 128:k * D + md * 128 + 128],
                              b=hhi[:, k * tb:(k + 1) * tb], s=(k == 0):
                              h.matmul(o, a, b, start=s, stop=False))
                    cm = p.op("tensor", "pe", lambda h, o=bank[0:128, 0:tb],
                              a=wbuf[:, k * D + md * 128:k * D + md * 128 + 128],
                              b=hlo[:, k * tb:(k + 1) * tb], e=(k == nmq - 1):
                              h.matmul(o, a, b, start=False, stop=e))
                last_dn_mm = cm
                p.wait("vector", "pe", cm)
                if q == 0:
                    cacc = p.op("vector", "dve", lambda h, o=acc[:, md * tb:(md + 1) * tb],
                                a=bank[0:128, 0:tb]: h.tensor_copy(o, a))
                else:
                    cacc = p.op("vector", "dve", lambda h, o=acc[:, md * tb:(md + 1) * tb],
                                a=bank[0:128, 0:tb], b=acc[:, md * tb:(md + 1) * tb]:
                                h.tensor_add(o, a, b))
                p.set_bank_gate(bidx, "dve", cacc)
        for md in range(16):
            cf = p.op("vector", "dve", lambda h, o=ev, a=acc[:, md * tb:(md + 1) * tb],
                      bb=bdnt[:, md:md + 1]: h.tensor_scalar(o, a, bb, None, ALU.add))
            p.wait("scalar", "dve", cf)
            gs = p.dma("scalar", out[md, :, it * tb:(it + 1) * tb], ev, "st")
            p.wait_t("scalar", gs)
            p.wait_t("vector", gs)
    p.fence()
    p.phase_end()





def ph_conv(p):
    p.phase_begin()
    xz = p.scratch("xz_fm", [24, 128, T], role="r")
    cw = p.inp("L1_attn_conv_w", [DI, 1, 4])
    cb = p.inp("L1_attn_conv_b", [DI])
    out = p.scratch("xconv", [12, 128, T], role="w")

    xt = p.alloc(T + 4)
    acc = p.alloc(T)
    wc = p.alloc(48)
    bc = p.alloc(12)
    g_c = p.ldgroup()
    for c in range(12):
        p.dma_g("sync", wc[:, c * 4:(c + 1) * 4],
                cw[c * 128:(c + 1) * 128, 0, :], g_c)
        p.dma_g("sync", bc[:, c:c + 1], cb[c * 128:(c + 1) * 128].unsqueeze(1), g_c)
    p.wait_t("vector", g_c)
    p.wait_t("scalar", g_c)

    last_gs = None
    for c in range(12):
        if last_gs:
            p.wait_t("sync", last_gs)
        cz = p.op("vector", "dve", lambda h, o=xt[:, 0:3]: h.memset(o, 0.0))
        g_x = p.dma("sync", xt[:, 3:T + 3], xz[c, :, :], "ld")
        p.wait_t("vector", g_x)
        p.wait("vector", "dve", cz)
        c0 = p.op("vector", "dve", lambda h, o=acc, a=xt[:, 0:T], m=wc[:, c * 4:c * 4 + 1]:
                  h.tensor_scalar(o, a, m, None, ALU.mult))
        for j in range(1, 4):
            p.wait("vector", "dve", c0)
            c0 = p.op("vector", "dve", lambda h, o=acc, a=xt[:, j:T + j],
                      m=wc[:, c * 4 + j:c * 4 + j + 1], b=acc:
                      h.scalar_tensor_tensor(o, a, m, b, ALU.mult, ALU.add))
        p.wait("scalar", "dve", c0)
        cs = p.op("scalar", "act", lambda h, o=acc, a=acc, bb=bc[:, c:c + 1]:
                  h.activation(o, a, AF.Silu, bias=bb))
        last_gs = p.dma("scalar", out[c, :, :], acc, "st")
        p.wait_t("scalar", last_gs)
    p.fence()
    p.phase_end()


def ph_scanprep1(p):
    p.phase_begin()
    xp = p.scratch("xp_fm", [25, 128, T], role="r").rearrange("c p t -> (c p) t")
    xpn = p.scratch("xpn", [25 * 128, T], role="w")
    xconv = p.scratch("xconv", [12, 128, T], role="r")
    snbh = p.scratch("snb_hbm", [H1, T], role="w")
    mallh = p.scratch("mall_hbm", [H1, T], role="w")

    W = 512
    bt = p.alloc(W)
    sq = p.alloc(W)
    row = p.alloc(4 * W)
    ones = p.alloc(128)
    pb = p.psum_banks()
    cz = p.op("vector", "dve", lambda h, o=ones: h.memset(o, 1.0))
    p.wait("tensor", "dve", cz)

    last_gs = None
    last_cm = 0
    for hd in range(H1):
        for part in range(2):
            r0 = hd * 129 + 1 + part * S1
            for it in range(T // W):
                if last_gs:
                    p.wait_t("sync", last_gs)
                g_b = p.dma("sync", bt[0:S1, :], xp[r0:r0 + S1, it * W:(it + 1) * W], "ld")
                p.wait_t("scalar", g_b)
                if last_cm:
                    p.wait("scalar", "pe", last_cm)
                c1 = p.op("scalar", "act", lambda h, o=sq[0:S1, :], a=bt[0:S1, :]:
                          h.activation(o, a, AF.Square))
                p.wait("tensor", "act", c1)
                p.use_bank("tensor", 0)
                cm = p.op("tensor", "pe", lambda h, o=pb[0][0:1, 0:W], a=ones[0:S1, 0:1],
                          b=sq[0:S1, :]: h.matmul(o, a, b, start=True, stop=True))
                last_cm = cm
                p.wait("vector", "pe", cm)
                c2 = p.op("vector", "dve", lambda h, o=row[0:1, 0:W], a=pb[0][0:1, 0:W]:
                          h.tensor_copy(o, a))
                p.set_bank_gate(0, "dve", c2)
                p.wait("scalar", "dve", c2)
                c3 = p.op("scalar", "act", lambda h, o=row[0:1, W:2 * W], a=row[0:1, 0:W]:
                          h.activation(o, a, AF.Sqrt))
                p.wait("vector", "act", c3)
                c4 = p.op("vector", "dve", lambda h, o=row[0:1, W:2 * W], a=row[0:1, W:2 * W]:
                          h.tensor_scalar(o, a, 1e-12, None, ALU.max))
                p.wait("vector", "dve", c4)
                c5 = p.op("vector", "dve", lambda h, o=row[0:1, 2 * W:3 * W],
                          a=row[0:1, W:2 * W]: h.reciprocal(o, a))
                p.wait("vector", "dve", c5)
                if part == 0:
                    c6 = p.op("vector", "dve", lambda h, o=row[0:1, 3 * W:4 * W],
                              a=row[0:1, 2 * W:3 * W], b=row[0:1, 2 * W:3 * W]:
                              h.tensor_mul(o, a, b))
                    p.wait("vector", "dve", c6)
                    c7 = p.op("vector", "dve", lambda h, o=row[0:1, 3 * W:4 * W],
                              a=row[0:1, 3 * W:4 * W], b=row[0:1, 0:W]:
                              h.tensor_mul(o, a, b))
                    p.wait("scalar", "dve", c7)
                    gs0 = p.dma("scalar", snbh[hd:hd + 1, it * W:(it + 1) * W],
                                row[0:1, 3 * W:4 * W], "st")
                    p.wait_t("scalar", gs0)
                p.use_bank("tensor", 1)
                p.wait("tensor", "dve", c5)
                cmb = p.op("tensor", "pe", lambda h, o=pb[1][0:S1, 0:W], a=ones[0:1, 0:S1],
                           b=row[0:1, 2 * W:3 * W]: h.matmul(o, a, b, start=True, stop=True))
                p.wait("vector", "pe", cmb)
                c8 = p.op("vector", "dve", lambda h, o=bt[0:S1, :], a=bt[0:S1, :],
                          b=pb[1][0:S1, 0:W]: h.tensor_mul(o, a, b))
                p.set_bank_gate(1, "dve", c8)
                p.wait("scalar", "dve", c8)
                last_gs = p.dma("scalar", xpn[r0:r0 + S1, it * W:(it + 1) * W],
                                bt[0:S1, :], "st")
                p.wait_t("scalar", last_gs)
    for hd in range(H1):
        hp, off = hd // 2, 64 * (hd % 2)
        for it in range(T // W):
            if last_gs:
                p.wait_t("sync", last_gs)
            g_b = p.dma("sync", bt[0:S1, :], xconv[hp, off:off + 64, it * W:(it + 1) * W], "ld")
            p.wait_t("tensor", g_b)
            p.use_bank("tensor", 0)
            cm = p.op("tensor", "pe", lambda h, o=pb[0][0:1, 0:W], a=ones[0:64, 0:1],
                      b=bt[0:64, :]: h.matmul(o, a, b, start=True, stop=True))
            p.wait("vector", "pe", cm)
            c2 = p.op("vector", "dve", lambda h, o=row[0:1, 0:W], a=pb[0][0:1, 0:W]:
                      h.tensor_scalar(o, a, 1.0 / 64.0, None, ALU.mult))
            p.set_bank_gate(0, "dve", c2)
            p.wait("scalar", "dve", c2)
            last_gs = p.dma("scalar", mallh[hd:hd + 1, it * W:(it + 1) * W],
                            row[0:1, 0:W], "st")
            p.wait_t("scalar", last_gs)
    p.fence()
    p.phase_end()


def ph_scanprep2(p):
    p.phase_begin()
    xp = p.scratch("xp_fm", [25, 128, T], role="r").rearrange("c p t -> (c p) t")
    snbh = p.scratch("snb_hbm", [H1, T], role="r")
    mallh = p.scratch("mall_hbm", [H1, T], role="r")
    alog = p.inp("L1_attn_A_log", [H1])
    l2ab = p.inp("L1_attn_log2_alpha_base", [H1])
    l2b = p.inp("L1_attn_log2_beta", [H1])
    sema = p.inp("L1_attn_surprise_ema", [H1])
    ident = p.inp("IDENT", [128, 128])
    nck = T // 128
    eT = p.scratch("e_T", [nck, 128, H1], role="w")
    omT = p.scratch("om_T", [nck, 128, H1], role="w")
    exT = p.scratch("ex_T", [nck, 128, H1], role="w")

    HH = H1
    LN2 = 0.6931471805599453
    dt = p.alloc(T)
    mm = p.alloc(T)
    sb = p.alloc(T)
    w1 = p.alloc(T)
    w2 = p.alloc(T)
    al = p.alloc(T)
    om = p.alloc(T)
    ec = p.alloc(T)
    colv = p.alloc(16)
    onesd = p.alloc(128)
    idt = p.alloc(128)
    ev = p.alloc(3 * 32)
    pb = p.psum_banks()

    g_c = p.ldgroup()
    p.dma_g("sync", idt, ident[:, :], g_c)
    p.dma_g("sync", colv[0:HH, 0:1], alog[:].unsqueeze(1), g_c)
    p.dma_g("sync", colv[0:HH, 1:2], l2ab[:].unsqueeze(1), g_c)
    p.dma_g("sync", colv[0:HH, 2:3], l2b[:].unsqueeze(1), g_c)
    p.dma_g("sync", colv[0:HH, 3:4], sema[:].unsqueeze(1), g_c)
    g_d = p.ldgroup()
    p.dma_g("sync", dt[0:HH, :], xp[0:24 * 129:129, :], g_d)
    p.dma_g("sync", mm[0:HH, :], mallh[:, :], g_d)
    p.dma_g("sync", sb[0:HH, :], snbh[:, :], g_d)
    for e in ("vector", "scalar", "tensor"):
        p.wait_t(e, g_c)
        p.wait_t(e, g_d)

    def vop(fn):
        c = p.op("vector", "dve", fn)
        p.wait("vector", "dve", c)
        return c

    def sop(fn):
        c = p.op("scalar", "act", fn)
        p.wait("scalar", "act", c)
        return c

    c = vop(lambda h, o=colv[0:HH, 4:5], a=colv[0:HH, 0:1]: h.tensor_scalar(
        o, a, -2.3, -0.01, ALU.max, ALU.min))
    c = vop(lambda h, o=colv[0:HH, 5:6], a=colv[0:HH, 1:2]: h.tensor_scalar(
        o, a, -3.32, -0.015, ALU.max, ALU.min))
    p.wait("scalar", "dve", c)
    c = sop(lambda h, o=colv[0:HH, 5:6], a=colv[0:HH, 5:6]: h.activation(
        o, a, AF.Exp, scale=LN2))
    p.wait("vector", "act", c)
    c = vop(lambda h, o=colv[0:HH, 5:6], a=colv[0:HH, 5:6]: h.tensor_scalar(
        o, a, -1.0, 1.0, ALU.mult, ALU.add))
    c = vop(lambda h, o=colv[0:HH, 6:7], a=colv[0:HH, 5:6]: h.tensor_scalar(
        o, a, -1.0, 1.0, ALU.mult, ALU.add))
    c = vop(lambda h, o=colv[0:HH, 7:8], a=colv[0:HH, 2:3]: h.tensor_scalar(
        o, a, -2.0, 2.0, ALU.max, ALU.min))
    p.wait("scalar", "dve", c)
    c = sop(lambda h, o=colv[0:HH, 7:8], a=colv[0:HH, 7:8]: h.activation(
        o, a, AF.Exp, scale=LN2))
    p.wait("vector", "act", c)
    c = vop(lambda h, o=colv[0:HH, 9:10], a=colv[0:HH, 3:4]: h.tensor_scalar(
        o, a, 1e-6, None, ALU.add))
    c = vop(lambda h, o=colv[0:HH, 9:10], a=colv[0:HH, 9:10]: h.reciprocal(o, a))
    cz = vop(lambda h, o=onesd[0:HH, :]: h.memset(o, 1.0))
    p.wait("scalar", "dve", cz)

    c = sop(lambda h, o=w1[0:HH, :], a=dt[0:HH, :]: h.activation(o, a, AF.Abs))
    c = sop(lambda h, o=w1[0:HH, :], a=w1[0:HH, :]: h.activation(o, a, AF.Exp, scale=-1.0))
    c = sop(lambda h, o=w1[0:HH, :], a=w1[0:HH, :], bb=onesd[0:HH, 0:1]:
            h.activation(o, a, AF.Ln, bias=bb))
    p.wait("vector", "act", c)
    c = vop(lambda h, o=w2[0:HH, :], a=dt[0:HH, :]: h.tensor_scalar(o, a, 0.0, None, ALU.max))
    c = vop(lambda h, o=w1[0:HH, :], a=w1[0:HH, :], b=w2[0:HH, :]: h.tensor_add(o, a, b))
    c = vop(lambda h, o=w1[0:HH, :], a=w1[0:HH, :], m=colv[0:HH, 4:5]:
            h.tensor_scalar(o, a, m, None, ALU.mult))
    p.wait("scalar", "dve", c)
    c = sop(lambda h, o=w2[0:HH, :], a=w1[0:HH, :]: h.activation(o, a, AF.Exp))
    p.wait("vector", "act", c)
    c = vop(lambda h, o=w2[0:HH, :], a=w2[0:HH, :]: h.tensor_scalar(
        o, a, -1.0, 1.0, ALU.mult, ALU.add))
    c = vop(lambda h, o=w2[0:HH, :], a=w2[0:HH, :], b=mm[0:HH, :]: h.tensor_mul(o, a, b))
    p.wait("scalar", "dve", c)
    c = sop(lambda h, o=w2[0:HH, :], a=w2[0:HH, :]: h.activation(o, a, AF.Square))
    p.wait("vector", "act", c)
    c = vop(lambda h, o=w2[0:HH, :], a=w2[0:HH, :], b=sb[0:HH, :]: h.tensor_mul(o, a, b))
    c = vop(lambda h, o=w2[0:HH, :], a=w2[0:HH, :], m=colv[0:HH, 9:10]:
            h.tensor_scalar(o, a, m, 1.0 / S1, ALU.mult, ALU.mult))
    c = vop(lambda h, o=w2[0:HH, :], a=w2[0:HH, :], m=colv[0:HH, 7:8]:
            h.tensor_scalar(o, a, m, None, ALU.mult))
    p.wait("scalar", "dve", c)
    c = sop(lambda h, o=w2[0:HH, :], a=w2[0:HH, :]: h.activation(o, a, AF.Tanh))
    p.wait("vector", "act", c)
    c = vop(lambda h, o=w2[0:HH, :], a=w2[0:HH, :]: h.tensor_scalar(o, a, 0.0, None, ALU.max))
    c = vop(lambda h, o=al[0:HH, :], a=w2[0:HH, :], m=colv[0:HH, 6:7]:
            h.tensor_scalar(o, a, m, None, ALU.mult))
    c = vop(lambda h, o=al[0:HH, :], a=al[0:HH, :], m=colv[0:HH, 5:6]:
            h.tensor_scalar(o, a, m, None, ALU.add))
    c = vop(lambda h, o=al[0:HH, :], a=al[0:HH, :]: h.tensor_scalar(
        o, a, 0.01, 0.999, ALU.max, ALU.min))
    c = vop(lambda h, o=om[0:HH, :], a=al[0:HH, :]: h.tensor_scalar(
        o, a, -1.0, 1.0, ALU.mult, ALU.add))
    p.wait("scalar", "dve", c)
    c = sop(lambda h, o=w1[0:HH, :], a=al[0:HH, :]: h.activation(o, a, AF.Ln))
    p.wait("vector", "act", c)
    for cc in range(nck):
        c = vop(lambda h, o=ec[0:HH, cc * 128:(cc + 1) * 128], d0=onesd[0:HH, :],
                d1=w1[0:HH, cc * 128:(cc + 1) * 128]:
                h.tensor_tensor_scan(o, d0, d1, 0.0, ALU.mult, ALU.add))
    p.wait("scalar", "dve", c)
    c = sop(lambda h, o=w2[0:HH, :], a=ec[0:HH, :]: h.activation(o, a, AF.Exp))
    p.wait("tensor", "act", c)
    p.wait("tensor", "dve", c)
    last_gs = None
    for cc in range(nck):
        for src, dst, ei in ((ec, eT, 0), (om, omT, 1), (w2, exT, 2)):
            bidx = ei % 2
            bank = pb[bidx]
            p.use_bank("tensor", bidx)
            cm = p.op("tensor", "pe", lambda h, o=bank[0:128, 0:HH],
                      a=src[0:HH, cc * 128:(cc + 1) * 128], i=idt:
                      h.transpose(o, a, i[0:HH, 0:HH]))
            p.wait("scalar", "pe", cm)
            if last_gs:
                p.wait_t("scalar", last_gs)
            ce = p.op("scalar", "act", lambda h, o=ev[:, ei * 32:ei * 32 + HH],
                      a=bank[0:128, 0:HH]: h.copy(o, a))
            p.set_bank_gate(bidx, "act", ce)
            last_gs = p.dma("scalar", dst[cc, :, :], ev[:, ei * 32:ei * 32 + HH], "st")
            p.wait_t("scalar", last_gs)
    p.fence()
    p.phase_end()




def ph_scan(p):
    nck = T // 128
    p.phase_begin()
    xpn = p.scratch("xpn", [25 * 128, T], role="r")
    xconv = p.scratch("xconv", [12, 128, T], role="r")
    eT = p.scratch("e_T", [nck, 128, H1], role="r")
    omT = p.scratch("om_T", [nck, 128, H1], role="r")
    exT = p.scratch("ex_T", [nck, 128, H1], role="r")
    ltneg = p.inp("LTNEG", [128, 128])
    ident = p.inp("IDENT", [128, 128])
    yfm = p.scratch("y_fm", [12, 128, T], role="w")

    bfm = [p.alloc(128), p.alloc(128)]
    cfm = [p.alloc(128), p.alloc(128)]
    xfm = [p.alloc(128), p.alloc(128)]
    xtok = p.alloc(64)
    btok = p.alloc(64)
    dsb = p.alloc(128)
    wsb = p.alloc(128)
    y2 = p.alloc(64)
    ysb = p.alloc(64)
    yev = p.alloc(128)
    bw = p.alloc(64)
    erow = p.alloc(128)
    de = p.alloc(4)
    decol = p.alloc(4)
    eyedec = p.alloc(64)
    eTt = p.alloc(32)
    omTt = p.alloc(32)
    exTt = p.alloc(32)
    mneg = p.alloc(128)
    idt = p.alloc(128)
    onesr = p.alloc(128)
    hsA = p.alloc(H1 * 64)
    hsB = p.alloc(H1 * 64)
    pb = p.psum_banks()

    g_c = p.ldgroup()
    p.dma_g("sync", mneg, ltneg[:, :], g_c)
    p.dma_g("sync", idt, ident[:, :], g_c)
    p.wait_t("tensor", g_c)
    p.wait_t("vector", g_c)
    czo = p.op("vector", "dve", lambda h, o=onesr[0:1, :]: h.memset(o, 1.0))
    czA = p.op("vector", "dve", lambda h, o=hsA[0:64, :]: h.memset(o, 0.0))
    p.wait("tensor", "dve", czA)

    hprev = [czA] * H1
    cur, nxt = hsA, hsB
    last_chunk_gate = 0
    seq = [(cc, hd) for cc in range(nck) for hd in range(H1)]

    def issue_bh(idx):
        cc, hd = seq[idx]
        b = idx % 2
        hp, off = hd // 2, 64 * (hd % 2)
        r0 = hd * 129 + 1
        g1 = p.ldgroup()
        p.dma_g("sync", bfm[b][0:S1, :], xpn[r0:r0 + S1, cc * 128:(cc + 1) * 128], g1)
        p.dma_g("sync", cfm[b][0:S1, :], xpn[r0 + S1:r0 + 2 * S1, cc * 128:(cc + 1) * 128], g1)
        p.dma_g("sync", xfm[b][0:64, :], xconv[hp, off:off + 64, cc * 128:(cc + 1) * 128], g1)
        return g1

    g1_of = {0: issue_bh(0)}
    m4_of = {}
    for cc in range(nck):
        if last_chunk_gate:
            p.wait("sync", "dve", last_chunk_gate)
        g_e = p.ldgroup()
        p.dma_g("sync", eTt[:, 0:H1], eT[cc, :, :], g_e)
        p.dma_g("sync", omTt[:, 0:H1], omT[cc, :, :], g_e)
        p.dma_g("sync", exTt[:, 0:H1], exT[cc, :, :], g_e)
        p.wait_t("tensor", g_e)
        p.wait_t("vector", g_e)
        for hd in range(H1):
            hp, off = hd // 2, 64 * (hd % 2)
            r0 = hd * 129 + 1
            idx = cc * H1 + hd
            bb = idx % 2
            bfmb, cfmb, xfmb = bfm[bb], cfm[bb], xfm[bb]
            if idx + 1 < len(seq):
                if idx >= 1:
                    p.wait("sync", "pe", m4_of[idx - 1])
                g1_of[idx + 1] = issue_bh(idx + 1)
            g1 = g1_of[idx]
            p.wait_t("tensor", g1)
            p.use_bank("tensor", 7)
            cm_er = p.op("tensor", "pe", lambda h, o=pb[7][0:1, 0:128],
                         a=eTt[:, hd:hd + 1], i=idt: h.transpose(o, a, i))
            p.wait("scalar", "pe", cm_er)
            ce_er = p.op("scalar", "act", lambda h, o=erow[0:1, :], a=pb[7][0:1, 0:128]:
                         h.copy(o, a))
            p.set_bank_gate(7, "act", ce_er)
            c_de = p.op("scalar", "act", lambda h, o=de[0:1, 0:1], a=erow[0:1, 127:128]:
                        h.activation(o, a, AF.Exp))
            p.use_bank("tensor", 0)
            p.wait("tensor", "act", ce_er)
            p.wait("tensor", "dve", czo)
            m1 = p.op("tensor", "pe", lambda h, o=pb[0][0:128, 0:128], a=onesr[0:1, :],
                      b=erow[0:1, :]: h.matmul(o, a, b, start=True, stop=True))
            p.wait("vector", "pe", m1)
            s1 = p.op("vector", "dve", lambda h, o=dsb, a=pb[0][0:128, 0:128],
                      m=eTt[:, hd:hd + 1], b=mneg:
                      h.scalar_tensor_tensor(o, a, m, b, ALU.subtract, ALU.add))
            p.set_bank_gate(0, "dve", s1)
            p.wait("scalar", "dve", s1)
            e1 = p.op("scalar", "act", lambda h, o=dsb, a=dsb: h.activation(o, a, AF.Exp))
            p.wait("vector", "act", e1)
            s2 = p.op("vector", "dve", lambda h, o=dsb, a=dsb, m=omTt[:, hd:hd + 1]:
                      h.tensor_scalar(o, a, m, None, ALU.mult))
            p.use_bank("tensor", 1)
            m2 = p.op("tensor", "pe", lambda h, o=pb[1][0:128, 0:128], a=bfmb[0:S1, :],
                      b=cfmb[0:S1, :]: h.matmul(o, a, b, start=True, stop=True))
            p.wait("vector", "pe", m2)
            p.wait("vector", "dve", s2)
            s3 = p.op("vector", "dve", lambda h, o=wsb, a=pb[1][0:128, 0:128], b=dsb:
                      h.tensor_mul(o, a, b))
            p.set_bank_gate(1, "dve", s3)
            p.use_bank("tensor", 5)
            mtX = p.op("tensor", "pe", lambda h, o=pb[5][0:128, 0:64], a=xfmb[0:64, :],
                       i=idt: h.transpose(o, a, i[0:64, 0:64]))
            p.wait("vector", "pe", mtX)
            seX = p.op("vector", "dve", lambda h, o=xtok[:, 0:64], a=pb[5][0:128, 0:64]:
                       h.tensor_copy(o, a))
            p.set_bank_gate(5, "dve", seX)
            p.use_bank("tensor", 6)
            mtB = p.op("tensor", "pe", lambda h, o=pb[6][0:128, 0:64], a=bfmb[0:S1, :],
                       i=idt: h.transpose(o, a, i[0:64, 0:64]))
            p.wait("vector", "pe", mtB)
            seB = p.op("vector", "dve", lambda h, o=btok[:, 0:64], a=pb[6][0:128, 0:64]:
                       h.tensor_copy(o, a))
            p.set_bank_gate(6, "dve", seB)
            p.use_bank("tensor", 2)
            p.wait("tensor", "dve", s3)
            p.wait("tensor", "dve", seX)
            m3 = p.op("tensor", "pe", lambda h, o=pb[2][0:128, 0:64], a=wsb,
                      b=xtok[:, 0:64]: h.matmul(o, a, b, start=True, stop=True))
            p.use_bank("tensor", 3)
            p.wait("tensor", "dve", hprev[hd])
            m4 = p.op("tensor", "pe", lambda h, o=pb[3][0:128, 0:64], a=cfmb[0:S1, :],
                      b=cur[0:64, hd * 64:(hd + 1) * 64]: h.matmul(o, a, b, start=True, stop=True))
            m4_of[idx] = m4
            p.wait("vector", "pe", m4)
            s4 = p.op("vector", "dve", lambda h, o=y2[:, 0:64], a=pb[3][0:128, 0:64],
                      m=exTt[:, hd:hd + 1]: h.tensor_scalar(o, a, m, None, ALU.mult))
            p.set_bank_gate(3, "dve", s4)
            p.wait("vector", "pe", m3)
            p.wait("vector", "dve", s4)
            s5 = p.op("vector", "dve", lambda h, o=ysb[:, 0:64], a=pb[2][0:128, 0:64],
                      b=y2[:, 0:64]: h.tensor_add(o, a, b))
            p.set_bank_gate(2, "dve", s5)
            p.use_bank("tensor", 5)
            p.wait("tensor", "dve", s5)
            m5 = p.op("tensor", "pe", lambda h, o=pb[5][0:64, 0:128], a=ysb[:, 0:64],
                      i=idt: h.transpose(o, a, i))
            p.wait("scalar", "pe", m5)
            s6 = p.op("scalar", "act", lambda h, o=yev[0:64, :], a=pb[5][0:64, 0:128]:
                      h.copy(o, a))
            p.set_bank_gate(5, "act", s6)
            gy = p.dma("scalar", yfm[hp, off:off + 64, cc * 128:(cc + 1) * 128],
                       yev[0:64, :], "st")
            p.wait_t("scalar", gy)
            p.wait("vector", "dve", seB)
            s7 = p.op("vector", "dve", lambda h, o=bw[:, 0:64], a=btok[:, 0:64],
                      m=dsb[:, 127:128]: h.tensor_scalar(o, a, m, None, ALU.mult))
            p.use_bank("tensor", 4)
            p.wait("tensor", "dve", s7)
            m6 = p.op("tensor", "pe", lambda h, o=pb[4][0:64, 0:64], a=bw[:, 0:64],
                      b=xtok[:, 0:64]: h.matmul(o, a, b, start=True, stop=False))
            p.use_bank("tensor", 7)
            p.wait("tensor", "act", c_de)
            m7 = p.op("tensor", "pe", lambda h, o=pb[7][0:64, 0:1], a=onesr[0:1, 0:64],
                      b=de[0:1, 0:1]: h.matmul(o, a, b, start=True, stop=True))
            p.wait("vector", "pe", m7)
            s9 = p.op("vector", "dve", lambda h, o=decol[0:64, 0:1], a=pb[7][0:64, 0:1]:
                      h.tensor_copy(o, a))
            p.set_bank_gate(7, "dve", s9)
            p.wait("vector", "dve", s9)
            s10 = p.op("vector", "dve", lambda h, o=eyedec[0:64, 0:64], a=idt[0:64, 0:64],
                       m=decol[0:64, 0:1]: h.tensor_scalar(o, a, m, None, ALU.mult))
            p.wait("tensor", "dve", s10)
            m8 = p.op("tensor", "pe", lambda h, o=pb[4][0:64, 0:64], a=eyedec[0:64, 0:64],
                      b=cur[0:64, hd * 64:(hd + 1) * 64]: h.matmul(o, a, b, start=False, stop=True))
            p.wait("vector", "pe", m8)
            s11 = p.op("vector", "dve", lambda h, o=nxt[0:64, hd * 64:(hd + 1) * 64],
                       a=pb[4][0:64, 0:64]: h.tensor_copy(o, a))
            p.set_bank_gate(4, "dve", s11)
            hprev[hd] = s11
            p.wait("sync", "dve", s11)
        cur, nxt = nxt, cur
        last_chunk_gate = s11
    p.fence()
    p.phase_end()


def ph_gate(p):
    p.phase_begin()
    yf = p.scratch("y_fm", [12, 128, T], role="r")
    xz = p.scratch("xz_fm", [24, 128, T], role="r")
    out = p.scratch("yg_fm", [12, 128, T], role="w")
    W = 512
    a = p.alloc(W)
    b = p.alloc(W)
    last_gs = None
    for c in range(12):
        for it in range(T // W):
            if last_gs:
                p.wait_t("sync", last_gs)
            g1 = p.ldgroup()
            p.dma_g("sync", a, yf[c, :, it * W:(it + 1) * W], g1)
            p.dma_g("sync", b, xz[12 + c, :, it * W:(it + 1) * W], g1)
            p.wait_t("scalar", g1)
            c1 = p.op("scalar", "act", lambda h, o=b, i=b: h.activation(o, i, AF.Silu))
            p.wait("vector", "act", c1)
            c2 = p.op("vector", "dve", lambda h, o=a, x=a, y=b: h.tensor_mul(o, x, y))
            p.wait("scalar", "dve", c2)
            last_gs = p.dma("scalar", out[c, :, it * W:(it + 1) * W], a, "st")
            p.wait_t("scalar", last_gs)
    p.fence()
    p.phase_end()


def ph_router(p):
    p.phase_begin()
    xn = p.scratch("xn1", [NCH, 128, T], role="r")
    rw = p.inp("L1_ffn_router_w", [D, NE])
    iota8 = p.inp("IOTA8", [128, NE])
    ident = p.inp("IDENT", [128, 128])
    cwfm = p.scratch("cw_fm", [NE, T], role="w")

    wt = p.alloc(NCH * NE)
    xt = p.alloc(NCH * 128)
    sc8 = p.alloc(128)
    ptok = p.alloc(NE)
    io8 = p.alloc(NE)
    msk = p.alloc(3 * NE, I32)
    m1c = p.alloc(16)
    cw = p.alloc(NE)
    cwT = p.alloc(128)
    idt = p.alloc(128)
    pb = p.psum_banks()

    g_w = p.ldgroup()
    for k in range(NCH):
        p.dma_g("sync", wt[:, k * NE:(k + 1) * NE], rw[k * 128:(k + 1) * 128, :], g_w)
    p.dma_g("sync", io8[:, 0:NE], iota8[:, :], g_w)
    p.dma_g("sync", idt, ident[:, :], g_w)
    p.wait_t("tensor", g_w)
    p.wait_t("vector", g_w)

    last_gs = None
    ntile = T // 128
    for i in range(ntile):
        if last_gs:
            p.wait_t("sync", last_gs)
        g_x = p.ldgroup()
        for k in range(NCH):
            p.dma_g("sync", xt[:, k * 128:(k + 1) * 128],
                    xn[k, :, i * 128:(i + 1) * 128], g_x)
        p.wait_t("tensor", g_x)
        p.use_bank("tensor", 0)
        for k in range(NCH):
            cm = p.op("tensor", "pe", lambda h, o=pb[0][0:NE, 0:128],
                      a=wt[:, k * NE:(k + 1) * NE], b=xt[:, k * 128:(k + 1) * 128],
                      s=(k == 0), e=(k == NCH - 1): h.matmul(o, a, b, start=s, stop=e))
        p.wait("vector", "pe", cm)
        cv = p.op("vector", "dve", lambda h, o=sc8[0:NE, 0:128], a=pb[0][0:NE, 0:128]:
                  h.tensor_copy(o, a))
        p.set_bank_gate(0, "dve", cv)
        p.use_bank("tensor", 1)
        p.wait("tensor", "dve", cv)
        cm2 = p.op("tensor", "pe", lambda h, o=pb[1][0:128, 0:NE], a=sc8[0:NE, 0:128],
                   i=idt: h.transpose(o, a, i[0:NE, 0:NE]))
        p.wait("vector", "pe", cm2)
        cv2 = p.op("vector", "dve", lambda h, o=ptok[:, 0:NE], a=pb[1][0:128, 0:NE]:
                   h.tensor_copy(o, a))
        p.set_bank_gate(1, "dve", cv2)
        p.wait("vector", "dve", cv2)
        c1 = p.op("vector", "dve", lambda h, o=m1c[:, 0:1], a=ptok[:, 0:NE]:
                  h.reduce_max(o, a, axis=AX.X, negate=True))
        p.wait("scalar", "dve", c1)
        c2 = p.op("scalar", "act", lambda h, o=ptok[:, 0:NE], a=ptok[:, 0:NE],
                  bb=m1c[:, 0:1]: h.activation(o, a, AF.Exp, bias=bb))
        p.wait("vector", "act", c2)
        c3 = p.op("vector", "dve", lambda h, o=m1c[:, 1:2], a=ptok[:, 0:NE]:
                  h.reduce_sum(o, a, axis=AX.X))
        p.wait("vector", "dve", c3)
        c4 = p.op("vector", "dve", lambda h, o=m1c[:, 2:3], a=m1c[:, 1:2]: h.reciprocal(o, a))
        p.wait("vector", "dve", c4)
        c5 = p.op("vector", "dve", lambda h, o=ptok[:, 0:NE], a=ptok[:, 0:NE],
                  m=m1c[:, 2:3]: h.tensor_scalar(o, a, m, None, ALU.mult))
        p.wait("vector", "dve", c5)
        c6 = p.op("vector", "dve", lambda h, o=m1c[:, 3:4], a=ptok[:, 0:NE]:
                  h.reduce_max(o, a, axis=AX.X))
        p.wait("vector", "dve", c6)
        c7 = p.op("vector", "dve", lambda h, o=msk.bitcast(I32)[:, 0:NE], a=ptok[:, 0:NE],
                  m=m1c[:, 3:4]: h.tensor_scalar(o, a, m, None, ALU.is_ge))
        p.wait("vector", "dve", c7)
        c8 = p.op("vector", "dve", lambda h, o=cw[:, 0:NE], a=msk.bitcast(I32)[:, 0:NE],
                  x=io8[:, 0:NE], y=io8[:, 0:NE]: h.select(o, a, x, y)) if False else None
        c8a = p.op("vector", "dve", lambda h, o=cwT[:, 0:NE]: h.memset(o, 8.0))
        p.wait("vector", "dve", c8a)
        c8b = p.op("vector", "dve", lambda h, o=cw[:, 0:NE], a=msk.bitcast(I32)[:, 0:NE],
                   x=io8[:, 0:NE], y=cwT[:, 0:NE]: h.select(o, a, x, y))
        p.wait("vector", "dve", c8b)
        c9 = p.op("vector", "dve", lambda h, o=m1c[:, 4:5], a=cw[:, 0:NE]:
                  h.tensor_reduce(o, a, axis=AX.X, op=ALU.min))
        p.wait("vector", "dve", c9)
        c10 = p.op("vector", "dve", lambda h, o=msk.bitcast(I32)[:, 0:NE], a=io8[:, 0:NE],
                   m=m1c[:, 4:5]: h.tensor_scalar(o, a, m, None, ALU.is_equal))
        p.wait("vector", "dve", c10)
        c11 = p.op("vector", "dve", lambda h, o=cwT[:, 0:NE]: h.memset(o, -1.0))
        p.wait("vector", "dve", c11)
        c12 = p.op("vector", "dve", lambda h, o=cw[:, 0:NE], a=msk.bitcast(I32)[:, 0:NE],
                   x=cwT[:, 0:NE], y=ptok[:, 0:NE]: h.select(o, a, x, y))
        p.wait("vector", "dve", c12)
        c13 = p.op("vector", "dve", lambda h, o=m1c[:, 5:6], a=cw[:, 0:NE]:
                   h.reduce_max(o, a, axis=AX.X))
        p.wait("vector", "dve", c13)
        c14 = p.op("vector", "dve", lambda h, o=msk.bitcast(I32)[:, NE:2 * NE], a=cw[:, 0:NE],
                   m=m1c[:, 5:6]: h.tensor_scalar(o, a, m, None, ALU.is_ge))
        p.wait("vector", "dve", c14)
        c15 = p.op("vector", "dve", lambda h, o=cw[:, 0:NE], a=msk.bitcast(I32)[:, NE:2 * NE],
                   x=io8[:, 0:NE], y=cwT2 if False else cwT[:, 0:NE]: h.select(o, a, x, y)) if False else None
        c15a = p.op("vector", "dve", lambda h, o=cwT[:, 0:NE]: h.memset(o, 8.0))
        p.wait("vector", "dve", c15a)
        c15b = p.op("vector", "dve", lambda h, o=cw[:, 0:NE], a=msk.bitcast(I32)[:, NE:2 * NE],
                    x=io8[:, 0:NE], y=cwT[:, 0:NE]: h.select(o, a, x, y))
        p.wait("vector", "dve", c15b)
        c16 = p.op("vector", "dve", lambda h, o=m1c[:, 6:7], a=cw[:, 0:NE]:
                   h.tensor_reduce(o, a, axis=AX.X, op=ALU.min))
        p.wait("vector", "dve", c16)
        c17 = p.op("vector", "dve", lambda h, o=msk.bitcast(I32)[:, NE:2 * NE], a=io8[:, 0:NE],
                   m=m1c[:, 6:7]: h.tensor_scalar(o, a, m, None, ALU.is_equal))
        p.wait("vector", "dve", c17)
        c18 = p.op("vector", "dve", lambda h, o=m1c[:, 7:8], a=m1c[:, 3:4], b=m1c[:, 5:6]:
                   h.tensor_add(o, a, b))
        p.wait("vector", "dve", c18)
        c19 = p.op("vector", "dve", lambda h, o=m1c[:, 8:9], a=m1c[:, 7:8]: h.reciprocal(o, a))
        p.wait("vector", "dve", c19)
        c20 = p.op("vector", "dve", lambda h, o=m1c[:, 9:10], a=m1c[:, 3:4], b=m1c[:, 8:9]:
                   h.tensor_mul(o, a, b))
        p.wait("vector", "dve", c20)
        c21 = p.op("vector", "dve", lambda h, o=m1c[:, 10:11], a=m1c[:, 5:6], b=m1c[:, 8:9]:
                   h.tensor_mul(o, a, b))
        p.wait("vector", "dve", c21)
        c22 = p.op("vector", "dve", lambda h, o=cw[:, 0:NE], a=msk.bitcast(I32)[:, 0:NE]:
                   h.tensor_copy(o, a))
        p.wait("vector", "dve", c22)
        c23 = p.op("vector", "dve", lambda h, o=cw[:, 0:NE], a=cw[:, 0:NE],
                   m=m1c[:, 9:10]: h.tensor_scalar(o, a, m, None, ALU.mult))
        p.wait("vector", "dve", c23)
        c24 = p.op("vector", "dve", lambda h, o=cwT[:, 0:NE], a=msk.bitcast(I32)[:, NE:2 * NE]:
                   h.tensor_copy(o, a))
        p.wait("vector", "dve", c24)
        c25 = p.op("vector", "dve", lambda h, o=cwT[:, 0:NE], a=cwT[:, 0:NE],
                   m=m1c[:, 10:11]: h.tensor_scalar(o, a, m, None, ALU.mult))
        p.wait("vector", "dve", c25)
        c26 = p.op("vector", "dve", lambda h, o=cw[:, 0:NE], a=cw[:, 0:NE],
                   b=cwT[:, 0:NE]: h.tensor_add(o, a, b))
        p.use_bank("tensor", 2)
        p.wait("tensor", "dve", c26)
        cm3 = p.op("tensor", "pe", lambda h, o=pb[2][0:NE, 0:128], a=cw[:, 0:NE],
                   i=idt: h.transpose(o, a, i))
        p.wait("vector", "pe", cm3)
        cv3 = p.op("vector", "dve", lambda h, o=cwT[0:NE, 0:128], a=pb[2][0:NE, 0:128]:
                   h.tensor_copy(o, a))
        p.set_bank_gate(2, "dve", cv3)
        p.wait("scalar", "dve", cv3)
        last_gs = p.dma("scalar", cwfm[:, i * 128:(i + 1) * 128], cwT[0:NE, 0:128], "st")
        p.wait_t("scalar", last_gs)
    p.fence()
    p.phase_end()


def ph_moe_expert(p, e):
    p.phase_begin()
    xn = p.scratch("xn1", [NCH, 128, T], role="r")
    wg = p.inp("L1_ffn_gate_proj", [NE, D, FE])
    wu = p.inp("L1_ffn_up_proj", [NE, D, FE])
    wd = p.inp("L1_ffn_down_proj", [NE, FE, D])
    cwfm = p.scratch("cw_fm", [NE, T], role="r")
    hh = p.scratch("moe_h", [FE // 128, 128, T], role="w")
    macc = p.scratch("moe_acc", [NCH, 128, T], role="w")

    nk = NCH
    nm = FE // 128
    tb = 256
    wgt = p.alloc(nk * FE)
    wut = p.alloc(nk * FE)
    xt = [p.alloc(nk * tb), p.alloc(nk * tb)]
    sg = p.alloc(tb)
    ev = p.alloc(4 * tb)
    pb = p.psum_banks()

    g_w = p.ldgroup()
    for k in range(nk):
        p.dma_g("sync", wgt[:, k * FE:(k + 1) * FE], wg[e, k * 128:(k + 1) * 128, :], g_w)
        p.dma_g("sync", wut[:, k * FE:(k + 1) * FE], wu[e, k * 128:(k + 1) * 128, :], g_w)
    p.wait_t("tensor", g_w)

    def issue_loads(it):
        g_x = p.ldgroup()
        for k in range(nk):
            p.dma_g("sync", xt[it % 2][:, k * tb:(k + 1) * tb],
                    xn[k, :, it * tb:(it + 1) * tb], g_x)
        return g_x

    ntb = T // tb
    mm_of = {}
    g_of = {0: issue_loads(0)}
    gs_slot = [None] * 4
    for it in range(ntb):
        if it + 1 < ntb:
            if it >= 1:
                p.wait("sync", "pe", mm_of[it - 1])
            g_of[it + 1] = issue_loads(it + 1)
        xb = xt[it % 2]
        p.wait_t("tensor", g_of[it])
        for m in range(nm):
            bg = m % 2
            bu = 2 + m % 2
            p.use_bank("tensor", bg)
            for k in range(nk):
                cm = p.op("tensor", "pe", lambda h, o=pb[bg][0:128, 0:tb],
                          a=wgt[:, k * FE + m * 128:k * FE + m * 128 + 128],
                          b=xb[:, k * tb:(k + 1) * tb], s=(k == 0), e2=(k == nk - 1):
                          h.matmul(o, a, b, start=s, stop=e2))
            p.use_bank("tensor", bu)
            for k in range(nk):
                cm = p.op("tensor", "pe", lambda h, o=pb[bu][0:128, 0:tb],
                          a=wut[:, k * FE + m * 128:k * FE + m * 128 + 128],
                          b=xb[:, k * tb:(k + 1) * tb], s=(k == 0), e2=(k == nk - 1):
                          h.matmul(o, a, b, start=s, stop=e2))
            mm_of[it] = cm
            p.wait("scalar", "pe", cm)
            c1 = p.op("scalar", "act", lambda h, o=sg, a=pb[bg][0:128, 0:tb]:
                      h.activation(o, a, AF.Silu))
            p.set_bank_gate(bg, "act", c1)
            p.wait("vector", "act", c1)
            slot = m % 4
            if gs_slot[slot]:
                p.wait_t("vector", gs_slot[slot])
            c2 = p.op("vector", "dve", lambda h, o=ev[:, slot * tb:(slot + 1) * tb],
                      a=pb[bu][0:128, 0:tb], b=sg: h.tensor_mul(o, a, b))
            p.set_bank_gate(bu, "dve", c2)
            p.wait("scalar", "dve", c2)
            gs_slot[slot] = p.dma("scalar", hh[m, :, it * tb:(it + 1) * tb],
                                  ev[:, slot * tb:(slot + 1) * tb], "st")
    for gslt in gs_slot:
        if gslt:
            p.wait_t("scalar", gslt)
    p.fence()
    p.phase_end()
    p.phase_begin()
    tb = 512
    wdt = p.alloc(nm * D)
    ht = [p.alloc(nm * tb), p.alloc(nm * tb)]
    cwr = p.alloc(2 * tb)
    cwb = p.alloc(tb)
    at = p.alloc(4 * tb)
    onesr2 = p.alloc(128)
    g_w2 = p.ldgroup()
    for k in range(nm):
        p.dma_g("sync", wdt[:, k * D:(k + 1) * D], wd[e, k * 128:(k + 1) * 128, :], g_w2)
    p.wait_t("tensor", g_w2)
    czz = p.op("vector", "dve", lambda h, o=onesr2[0:1, :]: h.memset(o, 1.0))
    p.wait("tensor", "dve", czz)

    def issue_h(it):
        g_h = p.ldgroup()
        for k in range(nm):
            p.dma_g("sync", ht[it % 2][:, k * tb:(k + 1) * tb],
                    hh[k, :, it * tb:(it + 1) * tb], g_h)
        p.dma_g("sync", cwr[0:1, (it % 2) * tb:(it % 2 + 1) * tb],
                cwfm[e:e + 1, it * tb:(it + 1) * tb], g_h)
        return g_h

    ntb = T // tb
    mm_of = {}
    cw_of = {}
    g_of = {0: issue_h(0)}
    gs_slot = [None] * 4
    ga_slot = {}
    last_cvb = 0
    for it in range(ntb):
        if it + 1 < ntb:
            if it >= 1:
                p.wait("sync", "pe", mm_of[it - 1])
                p.wait("sync", "dve", cw_of[it - 1])
            g_of[it + 1] = issue_h(it + 1)
        hb = ht[it % 2]
        p.use_bank("tensor", 4)
        p.wait_t("tensor", g_of[it])
        cmb = p.op("tensor", "pe", lambda h, o=pb[4][0:128, 0:512], a=onesr2[0:1, :],
                   b=cwr[0:1, (it % 2) * tb:(it % 2 + 1) * tb]:
                   h.matmul(o, a, b, start=True, stop=True))
        p.wait("vector", "pe", cmb)
        cvb = p.op("vector", "dve", lambda h, o=cwb, a=pb[4][0:128, 0:512]:
                   h.tensor_copy(o, a))
        p.set_bank_gate(4, "dve", cvb)
        p.wait("vector", "dve", cvb)
        ch = 0
        for k in range(nm):
            ch = p.op("vector", "dve", lambda h, o=hb[:, k * tb:(k + 1) * tb],
                      a=hb[:, k * tb:(k + 1) * tb], b=cwb: h.tensor_mul(o, a, b))
        cw_of[it] = ch
        p.wait("tensor", "dve", ch)
        for md in range(16):
            bidx = md % 2
            p.use_bank("tensor", bidx)
            for k in range(nm):
                cm = p.op("tensor", "pe", lambda h, o=pb[bidx][0:128, 0:tb],
                          a=wdt[:, k * D + md * 128:k * D + md * 128 + 128],
                          b=hb[:, k * tb:(k + 1) * tb], s=(k == 0), e2=(k == nm - 1):
                          h.matmul(o, a, b, start=s, stop=e2))
            mm_of[it] = cm
            slot = md % 4
            aslot = at[:, slot * tb:(slot + 1) * tb]
            if e == 0:
                p.wait("scalar", "pe", cm)
                if gs_slot[slot]:
                    p.wait_t("scalar", gs_slot[slot])
                ce = p.op("scalar", "act", lambda h, o=aslot, a=pb[bidx][0:128, 0:tb]:
                          h.copy(o, a))
                p.set_bank_gate(bidx, "act", ce)
                p.wait("scalar", "act", ce)
            else:
                if gs_slot[slot]:
                    p.wait_t("sync", gs_slot[slot])
                g_a = p.dma("sync", aslot, macc[md, :, it * tb:(it + 1) * tb], "ld")
                p.wait_t("vector", g_a)
                p.wait("vector", "pe", cm)
                ce = p.op("vector", "dve", lambda h, o=aslot, a=pb[bidx][0:128, 0:tb],
                          b=aslot: h.tensor_add(o, a, b))
                p.set_bank_gate(bidx, "dve", ce)
                p.wait("scalar", "dve", ce)
            gs_slot[slot] = p.dma("scalar", macc[md, :, it * tb:(it + 1) * tb], aslot, "st")
    for gslt in gs_slot:
        if gslt:
            p.wait_t("scalar", gslt)
    p.fence()
    p.phase_end()


def run_phases(phases, inputs_by_core, fused=False, name="prog", n_cores=8):
    p = Prog(name, fused_scratch=fused)
    for ph in phases:
        ph(p)
    nc = p.finalize()
    in_names = [n for n, k in p.hbm_kind.items() if k == "in"]
    in_maps = []
    for core in range(n_cores):
        m = {n: inputs_by_core[core][n] for n in in_names}
        in_maps.append(m)
    res = run_bass_kernel_spmd(nc, in_maps, core_ids=list(range(n_cores)))
    return res.results


def l0_phases():
    return [
        lambda p: ph_ln(p, "x", "xn0"),
        lambda p: ph_matmul_fm(p, "L0_attn_Wq", [D, 2048], "xn0", "q_fm", 2048),
        lambda p: ph_matmul_fm(p, "L0_attn_Wk", [D, 512], "xn0", "k_fm", 512),
        lambda p: ph_matmul_fm(p, "L0_attn_Wv", [D, 512], "xn0", "v_fm", 512),
        lambda p: ph_attn(p),
        lambda p: ph_matmul_fm(p, "L0_attn_Wo", [2048, D], "attn_o", "wo_out", D),
        lambda p: ph_deq(p),
        lambda p: ph_ffn(p),
        lambda p: ph_merge(p, "x1_tok", "x", [("wo_out", None), ("ffn_out", None)],
                           base_is_input=True),
    ]


def l1_phases():
    return [
        lambda p: ph_ln(p, "x1_tok", "xn1", want_rms="xr1"),
        lambda p: ph_matmul_fm(p, "L1_attn_in_proj_w", [2 * DI, D], "xr1", "xz_fm",
                               2 * DI, transpose_w=True, bias_name="L1_attn_in_proj_b",
                               rhs_scale_name="L1_attn_norm_w", mslice=(0, DI)),
        lambda p: ph_matmul_fm(p, "L1_attn_in_proj_w", [2 * DI, D], "xr1", "xz_fm",
                               2 * DI, transpose_w=True, bias_name="L1_attn_in_proj_b",
                               rhs_scale_name="L1_attn_norm_w", mslice=(DI, DI)),
        lambda p: ph_conv(p),
        lambda p: ph_matmul_fm(p, "L1_attn_x_proj_w", [PD, DI], "xconv", "xp_fm",
                               PD, transpose_w=True, bias_name="L1_attn_x_proj_b"),
        lambda p: ph_scanprep1(p),
        lambda p: ph_scanprep2(p),
        lambda p: ph_scan(p),
        lambda p: ph_gate(p),
        lambda p: ph_matmul_fm(p, "L1_attn_out_proj_w", [D, DI], "yg_fm", "op_out",
                               D, transpose_w=True, bias_name="L1_attn_out_proj_b"),
        lambda p: ph_router(p),
    ] + [(lambda e: lambda p: ph_moe_expert(p, e))(e) for e in range(NE)] + [
        lambda p: ph_merge(p, "x2_tok", "x1_tok",
                           [("xn1", None), ("op_out", "rg"), ("moe_acc", None)],
                           base_is_input=False),
    ]


FUSED = True


def kernel(**inputs):
    consts = host_constants()
    base = dict(consts)
    for k, v in inputs.items():
        if k != "L0_attn_x":
            base[k] = np.ascontiguousarray(np.asarray(v))
    xfull = np.asarray(inputs["L0_attn_x"])
    percore = []
    for b in range(8):
        m = dict(base)
        m["x"] = np.ascontiguousarray(xfull[b])
        percore.append(m)

    if FUSED:
        res = run_phases(l0_phases() + l1_phases(), percore, fused=True,
                         name="mk_full", n_cores=8)
        out = np.stack([res[b]["x2_tok"] for b in range(8)], axis=0)
        return out.astype(np.float32)
    res_a = run_phases(l0_phases(), percore, name="mk_l0", n_cores=8)
    for b in range(8):
        percore[b]["x1_tok"] = np.ascontiguousarray(res_a[b]["x1_tok"])
    res_b = run_phases(l1_phases(), percore, name="mk_l1", n_cores=8)
    out = np.stack([res_b[b]["x2_tok"] for b in range(8)], axis=0)
    return out.astype(np.float32)


# version 06
# speedup vs baseline: 1.0745x; 1.0114x over previous
import math
from contextlib import ExitStack

import numpy as np
import concourse.bass as bass
import concourse.mybir as mybir
from concourse.bass_utils import run_bass_kernel_spmd
from concourse.bass import IndirectOffsetOnAxis

F32 = mybir.dt.float32
BF16 = mybir.dt.bfloat16
I32 = mybir.dt.int32
AF = mybir.ActivationFunctionType
ALU = mybir.AluOpType
AX = mybir.AxisListType

B, T, D = 8, 4096, 2048
NCH = D // 128
H0, HK0, HD0, WIN = 16, 4, 128, 512
NWIN = T // WIN
FF = 8192
H1, S1, DH1 = 24, 64, 64
DI = H1 * DH1
PD = H1 * (1 + 2 * S1)
NE, FE = 8, 1024
CAP = 2048

ENGINES = ("sync", "tensor", "vector", "scalar", "gpsimd")
ARENA_F32 = 48896
NEG = -1e30
F32EPS = 1.1920928955078125e-07


class Prog:

    def __init__(self, name, fused_scratch=False):
        self.nc = bass.Bass("TRN2", target_bir_lowering=False,
                            detect_race_conditions=False)
        self.name = name
        self.fused_scratch = fused_scratch
        self.external_outs = {"x2_tok"}
        self.tape = {e: [] for e in ENGINES}
        self.cnt = {}
        self.sem_names = (["pe", "dve", "act", "pool", "fence", "gth", "sct"]
                          + [f"ld{i}" for i in range(8)]
                          + [f"st{i}" for i in range(8)]
                          + [f"gst{i}" for i in range(4)])
        self._ld_rr = 0
        self._st_rr = 0
        self.sems = {}
        self.ctx = ExitStack()
        self.hbm = {}
        self.hbm_kind = {}
        self._arena = None
        self._arena_off = 0
        self._phase_mark = 0
        self.psum = []
        self.bank_gate = {}
        for s in self.sem_names:
            self.cnt[s] = 0

    def inp(self, name, shape, dt=F32):
        if name not in self.hbm:
            self.hbm[name] = self.nc.dram_tensor(name, list(shape), dt,
                                                 kind="ExternalInput").ap()
            self.hbm_kind[name] = "in"
        return self.hbm[name]

    def out(self, name, shape, dt=F32):
        if name not in self.hbm:
            self.hbm[name] = self.nc.dram_tensor(name, list(shape), dt,
                                                 kind="ExternalOutput").ap()
            self.hbm_kind[name] = "out"
        return self.hbm[name]

    def scratch(self, name, shape, dt=F32, role="rw"):
        if name in self.hbm:
            return self.hbm[name]
        if self.fused_scratch and name not in self.external_outs:
            ap = self.nc.dram_tensor(name, list(shape), dt, kind="Internal").ap()
            self.hbm_kind[name] = "internal"
        elif role == "r":
            ap = self.nc.dram_tensor(name, list(shape), dt, kind="ExternalInput").ap()
            self.hbm_kind[name] = "in"
        else:
            ap = self.nc.dram_tensor(name, list(shape), dt, kind="ExternalOutput").ap()
            self.hbm_kind[name] = "out"
        self.hbm[name] = ap
        return ap

    def _ensure_arena(self):
        if self._arena is None:
            self._arena = self.ctx.enter_context(
                self.nc.sbuf_tensor("arena", [128, ARENA_F32], F32))

    def alloc(self, n, dt=F32):
        self._ensure_arena()
        nf = n if dt in (F32, I32) else (n + 1) // 2
        off = (self._arena_off + 127) & ~127
        self._arena_off = off + nf
        assert self._arena_off <= ARENA_F32, (
            f"{self.name}: arena overflow {self._arena_off} > {ARENA_F32}")
        sl = self._arena[:, off:off + nf]
        if dt == F32:
            return sl
        if dt == I32:
            return sl.bitcast(I32)
        return sl.bitcast(BF16)[:, 0:n]

    def phase_begin(self):
        self._phase_mark = self._arena_off

    def phase_end(self):
        self._arena_off = self._phase_mark

    def psum_banks(self):
        if not self.psum:
            for i in range(8):
                self.psum.append(self.ctx.enter_context(
                    self.nc.psum_tensor(f"pb{i}", [128, 512], F32)))
        return self.psum

    def use_bank(self, eng, idx):
        g = self.bank_gate.get(idx)
        if g:
            self.wait(eng, g[0], g[1])

    def set_bank_gate(self, idx, sem, count):
        self.bank_gate[idx] = (sem, count)

    def emit(self, eng, fn):
        self.tape[eng].append(fn)

    def bump(self, sem, k=1):
        self.cnt[sem] += k
        return self.cnt[sem]

    def ldgroup(self):
        self._ld_rr = (self._ld_rr + 1) % 8
        return [f"ld{self._ld_rr}", self.cnt[f"ld{self._ld_rr}"]]

    def stgroup(self):
        self._st_rr = (self._st_rr + 1) % 8
        return [f"st{self._st_rr}", self.cnt[f"st{self._st_rr}"]]

    def gstgroup(self):
        self._gst_rr = (getattr(self, "_gst_rr", 0) + 1) % 4
        return [f"gst{self._gst_rr}", self.cnt[f"gst{self._gst_rr}"]]

    def dma_g(self, eng, out_ap, in_ap, grp):
        sem = grp[0]
        c = self.bump(sem, 16)
        grp[1] = c
        self.emit(eng, lambda h, o=out_ap, i=in_ap, s=sem: h.dma_start(
            out=o, in_=i).then_inc(self.sems[s], 16))
        return c

    def wait_t(self, eng, grp):
        self.wait(eng, grp[0], grp[1])

    def dma(self, eng, out_ap, in_ap, grp):
        if eng == "gpsimd":
            g = self.gstgroup()
        elif grp in ("ld", "ld2"):
            g = self.ldgroup()
        else:
            g = self.stgroup()
        self.dma_g(eng, out_ap, in_ap, g)
        return g

    def wait(self, eng, sem, count):
        if count <= 0:
            return
        self.emit(eng, lambda h, s=sem, c=count: h.wait_ge(self.sems[s], c))

    def op(self, eng, sem, fn):
        c = self.bump(sem, 1)
        self.emit(eng, lambda h, f=fn, s=sem: f(h).then_inc(self.sems[s], 1))
        return c

    def opq(self, eng, fn):
        self.emit(eng, lambda h, f=fn: f(h))

    def fence(self):
        for e in ENGINES:
            self.bump("fence", 1)
            self.emit(e, lambda h: h.nop().then_inc(self.sems["fence"], 1))
        tgt = self.cnt["fence"]
        for e in ENGINES:
            self.wait(e, "fence", tgt)

    def finalize(self):
        nc = self.nc
        with self.ctx:
            for s in self.sem_names:
                self.sems[s] = self.ctx.enter_context(nc.semaphore(s))
            self.psum_banks()
            with nc.Block() as block:
                @block.sync
                def _(h):
                    for fn in self.tape["sync"]:
                        fn(h)

                @block.tensor
                def _(h):
                    for fn in self.tape["tensor"]:
                        fn(h)

                @block.vector
                def _(h):
                    for fn in self.tape["vector"]:
                        fn(h)

                @block.scalar
                def _(h):
                    for fn in self.tape["scalar"]:
                        fn(h)

                @block.gpsimd
                def _(h):
                    for fn in self.tape["gpsimd"]:
                        fn(h)
        return nc



def host_constants():
    c = {}
    c["IDENT"] = np.eye(128, dtype=np.float32)
    r = np.arange(WIN)[:, None]
    col = np.arange(2 * WIN)[None, :]
    allowed = np.where(col < WIN, col > r, col - WIN <= r)
    c["MASKG"] = np.where(allowed, 0.0, NEG).astype(np.float32)
    c["MASK0"] = np.where((col >= WIN) & (col - WIN <= r), 0.0, NEG).astype(np.float32)
    tau = np.arange(128)[:, None]
    tt = np.arange(128)[None, :]
    c["LTNEG"] = np.where(tau <= tt, 0.0, NEG).astype(np.float32)
    c["EXP32"] = (np.arange(128)[None, :] // 32 == np.arange(4)[:, None]).astype(np.float32)
    c["IOTA8"] = np.tile(np.arange(8, dtype=np.float32)[None, :], (128, 1))
    c["LSTRICT"] = (np.arange(128)[:, None] < np.arange(128)[None, :]).astype(np.float32)
    c["IOTACOL"] = np.arange(128, dtype=np.float32)[:, None]
    return c



def ph_ln(p, x_name, xn_name, want_rms=None, also_tok=None):
    eps = 1e-5
    p.phase_begin()
    x = p.inp("x", [T, D]) if x_name == "x" else p.scratch(x_name, [T, D], role="r")
    xn = p.scratch(xn_name, [NCH, 128, T], role="w")
    xr = p.scratch(want_rms, [NCH, 128, T], role="w") if want_rms else None
    xtok = p.scratch(also_tok, [T, D], role="w") if also_tok else None
    ident = p.inp("IDENT", [128, 128])

    xt = p.alloc(D)
    sq = p.alloc(D)
    stats = p.alloc(16)
    xnt = p.alloc(D)
    xrt = p.alloc(D) if want_rms else None
    idt = p.alloc(128)
    evac = p.alloc(256)
    pb = p.psum_banks()

    g_id = p.dma("sync", idt, ident[:, :], "ld")
    p.wait_t("tensor", g_id)

    g_store = None
    ntile = T // 128
    for i in range(ntile):
        if g_store:
            p.wait_t("sync", g_store)
        g_x = p.dma("sync", xt, x[i * 128:(i + 1) * 128, :], "ld")
        p.wait_t("vector", g_x)
        c0 = p.op("vector", "dve", lambda h, o=stats[:, 0:1], a=xt: h.reduce_sum(o, a, axis=AX.X))
        p.wait("vector", "dve", c0)
        c0b = p.op("vector", "dve", lambda h, o=stats[:, 1:2], a=stats[:, 0:1]: h.tensor_scalar(
            o, a, 1.0 / D, None, ALU.mult))
        p.wait("vector", "dve", c0b)
        c1 = p.op("vector", "dve", lambda h, o=xnt, a=xt, m=stats[:, 1:2]: h.tensor_scalar(
            o, a, m, None, ALU.subtract))
        p.wait("scalar", "dve", c1)
        c2 = p.op("scalar", "act", lambda h, o=sq, a=xnt: h.activation(o, a, AF.Square))
        p.wait("vector", "act", c2)
        c2b = p.op("vector", "dve", lambda h, o=stats[:, 2:3], a=sq: h.reduce_sum(o, a, axis=AX.X))
        p.wait("vector", "dve", c2b)
        c2c = p.op("vector", "dve", lambda h, o=stats[:, 3:4], a=stats[:, 2:3]: h.tensor_scalar(
            o, a, 1.0 / D, eps, ALU.mult, ALU.add))
        p.wait("vector", "dve", c2c)
        c3 = p.op("vector", "dve", lambda h, o=stats[:, 4:5], a=stats[:, 3:4]: h.reciprocal(o, a))
        p.wait("scalar", "dve", c3)
        c4 = p.op("scalar", "act", lambda h, o=stats[:, 5:6], a=stats[:, 4:5]: h.activation(
            o, a, AF.Sqrt))
        p.wait("vector", "act", c4)
        cln = p.op("vector", "dve", lambda h, o=xnt, a=xnt, m=stats[:, 5:6]: h.tensor_scalar(
            o, a, m, None, ALU.mult))
        clast = cln
        if want_rms:
            p.wait("scalar", "dve", cln)
            c6 = p.op("scalar", "act", lambda h, o=sq, a=xnt: h.activation(o, a, AF.Square))
            p.wait("vector", "act", c6)
            c6b = p.op("vector", "dve", lambda h, o=stats[:, 6:7], a=sq: h.reduce_sum(o, a, axis=AX.X))
            p.wait("vector", "dve", c6b)
            c6c = p.op("vector", "dve", lambda h, o=stats[:, 7:8], a=stats[:, 6:7]: h.tensor_scalar(
                o, a, 1.0 / D, F32EPS, ALU.mult, ALU.add))
            p.wait("vector", "dve", c6c)
            c7 = p.op("vector", "dve", lambda h, o=stats[:, 8:9], a=stats[:, 7:8]: h.reciprocal(o, a))
            p.wait("scalar", "dve", c7)
            c8 = p.op("scalar", "act", lambda h, o=stats[:, 9:10], a=stats[:, 8:9]: h.activation(
                o, a, AF.Sqrt))
            p.wait("vector", "act", c8)
            clast = p.op("vector", "dve", lambda h, o=xrt, a=xnt, m=stats[:, 9:10]:
                         h.tensor_scalar(o, a, m, None, ALU.mult))
        p.wait("tensor", "dve", clast)
        if also_tok:
            p.wait("gpsimd", "dve", clast)
            gt = p.dma("gpsimd", xtok[i * 128:(i + 1) * 128, :], xnt, "st")
            p.wait_t("gpsimd", gt)
        g_store = p.stgroup()
        for c in range(NCH):
            bank = pb[c % 2]
            p.use_bank("tensor", c % 2)
            cm = p.op("tensor", "pe", lambda h, o=bank[0:128, 0:128],
                      a=xnt[:, c * 128:(c + 1) * 128], i=idt: h.transpose(o, a, i))
            p.wait("scalar", "pe", cm)
            ce = p.op("scalar", "act", lambda h, o=evac[:, 0:128], a=bank[0:128, 0:128]:
                      h.copy(o, a))
            if not want_rms:
                p.set_bank_gate(c % 2, "act", ce)
            p.wait("scalar", "act", ce)
            gs = p.dma("scalar", xn[c, :, i * 128:(i + 1) * 128], evac[:, 0:128], "st")
            p.wait_t("scalar", gs)
            if want_rms:
                cm2 = p.op("tensor", "pe", lambda h, o=bank[0:128, 128:256],
                           a=xrt[:, c * 128:(c + 1) * 128], i=idt: h.transpose(o, a, i))
                p.wait("scalar", "pe", cm2)
                ce2 = p.op("scalar", "act", lambda h, o=evac[:, 128:256], a=bank[0:128, 128:256]:
                           h.copy(o, a))
                p.set_bank_gate(c % 2, "act", ce2)
                p.wait("scalar", "act", ce2)
                gs = p.dma("scalar", xr[c, :, i * 128:(i + 1) * 128], evac[:, 128:256], "st")
                p.wait_t("scalar", gs)
        g_store = gs
    p.fence()
    p.phase_end()


def ph_matmul_fm(p, w_name, w_shape, in_name, out_name, n_out,
                 transpose_w=False, bias_name=None, tb=None, out_scale=None,
                 rhs_scale_name=None, mslice=None):
    K = w_shape[0] if not transpose_w else w_shape[1]
    nm_all = (n_out + 127) // 128
    m0, M = (0, n_out) if mslice is None else mslice
    assert m0 % 128 == 0
    nk = K // 128
    nm = (M + 127) // 128
    cm0 = m0 // 128
    if tb is None:
        tb = 256 if nk * M > 24576 else 512
    p.phase_begin()
    w = p.inp(w_name, list(w_shape))
    xin = p.scratch(in_name, [nk, 128, T], role="r")
    xout = p.scratch(out_name, [nm_all, 128, T], role="w")
    bias = p.inp(bias_name, [n_out]) if bias_name else None
    rsv = p.inp(rhs_scale_name, [K]) if rhs_scale_name else None

    wt = p.alloc(nk * M)
    xt = [p.alloc(nk * tb), p.alloc(nk * tb)]
    ev = p.alloc(4 * tb)
    bt = p.alloc(nm) if bias_name else None
    rst = p.alloc(nk) if rhs_scale_name else None
    pb = p.psum_banks()

    c_dve_w = 0
    g_w = p.ldgroup()
    if not transpose_w:
        for k in range(nk):
            p.dma_g("sync", wt[:, k * M:(k + 1) * M],
                    w[k * 128:(k + 1) * 128, m0:m0 + M], g_w)
    else:
        ident = p.inp("IDENT", [128, 128])
        idt = p.alloc(128)
        wtmp = p.alloc(K)
        g_id = p.dma("sync", idt, ident[:, :], "ld")
        p.wait_t("tensor", g_id)
        for m in range(nm):
            mrows = min(128, M - m * 128)
            g_wt = p.dma("sync", wtmp[0:mrows, :],
                         w[m0 + m * 128:m0 + m * 128 + mrows, :], "ld")
            p.wait_t("tensor", g_wt)
            for k in range(nk):
                bidx = 6 + (k % 2)
                bank = pb[bidx]
                p.use_bank("tensor", bidx)
                cm = p.op("tensor", "pe", lambda h, o=bank[0:128, 0:128],
                          a=wtmp[0:mrows, k * 128:(k + 1) * 128], i=idt, mr=mrows:
                          h.transpose(o[0:128, 0:mr], a, i[0:mr, 0:mr]))
                p.wait("vector", "pe", cm)
                c_dve_w = p.op("vector", "dve",
                               lambda h, o=wt[:, k * M + m * 128:k * M + m * 128 + mrows],
                               a=bank[0:128, 0:mrows]: h.tensor_copy(o, a))
                p.set_bank_gate(bidx, "dve", c_dve_w)
            p.wait("sync", "dve", c_dve_w)
    if bias_name:
        for m in range(nm):
            mrows = min(128, M - m * 128)
            p.dma_g("sync", bt[0:mrows, m:m + 1],
                    bias[m0 + m * 128:m0 + m * 128 + mrows].unsqueeze(1), g_w)
    if rhs_scale_name:
        for k in range(nk):
            p.dma_g("sync", rst[0:128, k:k + 1],
                    rsv[k * 128:(k + 1) * 128].unsqueeze(1), g_w)

    ntb = T // tb
    p.wait_t("scalar", g_w)
    p.wait_t("vector", g_w)
    p.wait_t("tensor", g_w)

    def issue_loads(it):
        buf = it % 2
        g_x = p.ldgroup()
        for k in range(nk):
            p.dma_g("sync", xt[buf][:, k * tb:(k + 1) * tb],
                    xin[k, :, it * tb:(it + 1) * tb], g_x)
        return g_x

    gs_slot = [None] * 4
    mm_of = {}
    g_of = {}
    g_of[0] = issue_loads(0)
    for it in range(ntb):
        buf = it % 2
        if it + 1 < ntb:
            if it >= 1:
                p.wait("sync", "pe", mm_of[it - 1])
            g_of[it + 1] = issue_loads(it + 1)
        g_x = g_of[it]
        xb = xt[buf]
        if rhs_scale_name:
            p.wait_t("vector", g_x)
            cvs = 0
            for k in range(nk):
                cvs = p.op("vector", "dve", lambda h, o=xb[:, k * tb:(k + 1) * tb],
                           a=xb[:, k * tb:(k + 1) * tb], m=rst[0:128, k:k + 1]:
                           h.tensor_scalar(o, a, m, None, ALU.mult))
            p.wait("tensor", "dve", cvs)
        p.wait_t("tensor", g_x)
        if transpose_w:
            p.wait("tensor", "dve", c_dve_w)
        for m in range(nm):
            mrows = min(128, M - m * 128)
            bank = pb[m % 2]
            p.use_bank("tensor", m % 2)
            for k in range(nk):
                cm = p.op("tensor", "pe", lambda h, o=bank[0:mrows, 0:tb],
                          a=wt[:, k * M + m * 128:k * M + m * 128 + mrows],
                          b=xb[:, k * tb:(k + 1) * tb], s=(k == 0), e=(k == nk - 1):
                          h.matmul(o, a, b, start=s, stop=e))
            mm_of[it] = cm
            slot = m % 4
            eslot = ev[:, slot * tb:slot * tb + tb]
            if bias_name:
                p.wait("vector", "pe", cm)
                if gs_slot[slot]:
                    p.wait_t("vector", gs_slot[slot])
                ce = p.op("vector", "dve", lambda h, o=eslot, a=bank[0:mrows, 0:tb],
                          bb=bt[0:mrows, m:m + 1], mr=mrows:
                          h.tensor_scalar(o[0:mr, :], a, bb, None, ALU.add))
                p.set_bank_gate(m % 2, "dve", ce)
                p.wait("scalar", "dve", ce)
            else:
                p.wait("scalar", "pe", cm)
                if gs_slot[slot]:
                    p.wait_t("scalar", gs_slot[slot])
                if out_scale is not None:
                    ce = p.op("scalar", "act", lambda h, o=eslot, a=bank[0:mrows, 0:tb],
                              mr=mrows: h.activation(o[0:mr, :], a, AF.Copy, scale=out_scale))
                else:
                    ce = p.op("scalar", "act", lambda h, o=eslot, a=bank[0:mrows, 0:tb],
                              mr=mrows: h.copy(o[0:mr, :], a))
                p.set_bank_gate(m % 2, "act", ce)
                p.wait("scalar", "act", ce)
            gs_slot[slot] = p.dma("scalar", xout[cm0 + m, 0:mrows, it * tb:(it + 1) * tb],
                                  eslot[0:mrows, :], "st")
    for gslt in gs_slot:
        if gslt:
            p.wait_t("scalar", gslt)
    p.fence()
    p.phase_end()


def ph_attn(p):
    nwin = T // WIN
    scale = 1.0 / math.sqrt(HD0)
    p.phase_begin()
    q_fm = p.scratch("q_fm", [H0, 128, T], role="r")
    k_fm = p.scratch("k_fm", [HK0, 128, T], role="r")
    v_fm = p.scratch("v_fm", [HK0, 128, T], role="r")
    attn_o = p.scratch("attn_o", [H0, 128, T], role="w")
    maskg = p.inp("MASKG", [WIN, 2 * WIN])
    mask0 = p.inp("MASK0", [WIN, 2 * WIN])
    ident = p.inp("IDENT", [128, 128])

    kc = [p.alloc(1024) for _ in range(HK0)]
    vt = [p.alloc(1024) for _ in range(HK0)]
    vstg = p.alloc(512)
    qh = p.alloc(512)
    sc = p.alloc(1024)
    pp = p.alloc(1024)
    pt = p.alloc(8 * 512)
    oe = p.alloc(512)
    mskg = p.alloc(4 * 1024)
    msk0 = p.alloc(4 * 1024)
    stats = p.alloc(8)
    idt = p.alloc(128)
    pb = p.psum_banks()

    g_const = p.ldgroup()
    p.dma_g("sync", idt, ident[:, :], g_const)
    for sub in range(4):
        p.dma_g("sync", mskg[:, sub * 1024:(sub + 1) * 1024],
                maskg[sub * 128:(sub + 1) * 128, :], g_const)
        p.dma_g("sync", msk0[:, sub * 1024:(sub + 1) * 1024],
                mask0[sub * 128:(sub + 1) * 128, :], g_const)
    p.wait_t("tensor", g_const)
    p.wait_t("vector", g_const)
    cz = 0
    for g in range(HK0):
        p.op("vector", "dve", lambda h, o=kc[g][:, 0:512]: h.memset(o, 0.0))
        cz = p.op("vector", "dve", lambda h, o=vt[g][:, 0:512]: h.memset(o, 0.0))
    p.wait("tensor", "dve", cz)

    last_pv = 0
    for n in range(nwin):
        msk = msk0 if n == 0 else mskg
        for g in range(HK0):
            if last_pv:
                p.wait("sync", "pe", last_pv)
            g_k = p.ldgroup()
            if n > 0:
                p.dma_g("sync", kc[g][:, 0:512], k_fm[g, :, (n - 1) * WIN:n * WIN], g_k)
            p.dma_g("sync", kc[g][:, 512:1024], k_fm[g, :, n * WIN:(n + 1) * WIN], g_k)
            halves = ([1] if n == 0 else [0, 1])
            c_vt = 0
            last_tr = 0
            for hv in halves:
                wv = n - 1 + hv
                if last_tr:
                    p.wait("sync", "pe", last_tr)
                g_v = p.dma("sync", vstg, v_fm[g, :, wv * WIN:(wv + 1) * WIN], "ld")
                p.wait_t("tensor", g_v)
                for j in range(4):
                    bidx = 5 + (j % 2)
                    bank = pb[bidx]
                    p.use_bank("tensor", bidx)
                    cm = p.op("tensor", "pe", lambda h, o=bank[0:128, 0:128],
                              a=vstg[:, j * 128:(j + 1) * 128], i=idt: h.transpose(o, a, i))
                    p.wait("scalar", "pe", cm)
                    c_vt = p.op("scalar", "act", lambda h,
                                o=vt[g][:, (hv * 4 + j) * 128:(hv * 4 + j + 1) * 128],
                                a=bank[0:128, 0:128]: h.copy(o, a))
                    p.set_bank_gate(bidx, "act", c_vt)
                last_tr = cm
            p.wait_t("tensor", g_k)
            p.wait("tensor", "act", c_vt)
            for r in range(4):
                hq = g * 4 + r
                if last_pv:
                    p.wait("sync", "pe", last_pv)
                g_q = p.dma("sync", qh, q_fm[hq, :, n * WIN:(n + 1) * WIN], "ld")
                p.wait_t("tensor", g_q)
                cpt = 0
                for sub in range(4):
                    p.use_bank("tensor", 0)
                    p.use_bank("tensor", 1)
                    p.op("tensor", "pe", lambda h, o=pb[0][0:128, 0:512],
                         a=qh[:, sub * 128:(sub + 1) * 128], b=kc[g][:, 0:512]:
                         h.matmul(o, a, b, start=True, stop=True))
                    cm1 = p.op("tensor", "pe", lambda h, o=pb[1][0:128, 0:512],
                               a=qh[:, sub * 128:(sub + 1) * 128], b=kc[g][:, 512:1024]:
                               h.matmul(o, a, b, start=True, stop=True))
                    p.wait("vector", "pe", cm1)
                    cd0 = p.op("vector", "dve", lambda h, o=sc[:, 0:512],
                               a=pb[0][0:128, 0:512], m=msk[:, sub * 1024:sub * 1024 + 512]:
                               h.scalar_tensor_tensor(o, a, scale, m, ALU.mult, ALU.add))
                    p.set_bank_gate(0, "dve", cd0)
                    cd1 = p.op("vector", "dve", lambda h, o=sc[:, 512:1024],
                               a=pb[1][0:128, 0:512], m=msk[:, sub * 1024 + 512:sub * 1024 + 1024]:
                               h.scalar_tensor_tensor(o, a, scale, m, ALU.mult, ALU.add))
                    p.set_bank_gate(1, "dve", cd1)
                    cd2 = p.op("vector", "dve", lambda h, o=stats[:, 0:1], a=sc:
                               h.reduce_max(o, a, axis=AX.X, negate=True))
                    p.wait("scalar", "dve", cd2)
                    ce = p.op("scalar", "act", lambda h, o=pp, a=sc, b=stats[:, 0:1]:
                              h.activation(o, a, AF.Exp, bias=b))
                    p.wait("vector", "act", ce)
                    cd3 = p.op("vector", "dve", lambda h, o=stats[:, 1:2], a=pp:
                               h.reduce_sum(o, a, axis=AX.X))
                    p.wait("vector", "dve", cd3)
                    cd4 = p.op("vector", "dve", lambda h, o=stats[:, 2:3], a=stats[:, 1:2]:
                               h.reciprocal(o, a))
                    p.wait("vector", "dve", cd4)
                    cd5 = p.op("vector", "dve", lambda h, o=pp, a=pp, m=stats[:, 2:3]:
                               h.tensor_scalar(o, a, m, None, ALU.mult))
                    p.wait("tensor", "dve", cd5)
                    for j in range(8):
                        bidx = 2 + (j % 2)
                        bank = pb[bidx]
                        p.use_bank("tensor", bidx)
                        cm = p.op("tensor", "pe", lambda h, o=bank[0:128, 0:128],
                                  a=pp[:, j * 128:(j + 1) * 128], i=idt: h.transpose(o, a, i))
                        p.wait("scalar", "pe", cm)
                        cpt = p.op("scalar", "act", lambda h,
                                   o=pt[:, j * 512 + sub * 128:j * 512 + sub * 128 + 128],
                                   a=bank[0:128, 0:128]: h.copy(o, a))
                        p.set_bank_gate(bidx, "act", cpt)
                p.use_bank("tensor", 4)
                p.wait("tensor", "act", cpt)
                for j in range(8):
                    cm = p.op("tensor", "pe", lambda h, o=pb[4][0:128, 0:512],
                              a=vt[g][:, j * 128:(j + 1) * 128], b=pt[:, j * 512:(j + 1) * 512],
                              s=(j == 0), e=(j == 7): h.matmul(o, a, b, start=s, stop=e))
                last_pv = cm
                p.wait("scalar", "pe", cm)
                co = p.op("scalar", "act", lambda h, o=oe, a=pb[4][0:128, 0:512]: h.copy(o, a))
                p.set_bank_gate(4, "act", co)
                p.wait("scalar", "act", co)
                gs = p.dma("scalar", attn_o[hq, :, n * WIN:(n + 1) * WIN], oe, "st")
                p.wait_t("scalar", gs)
    p.fence()
    p.phase_end()




def ph_merge(p, out_name, base_name, terms, base_is_input=False):
    p.phase_begin()
    base = (p.inp(base_name, [T, D]) if base_is_input
            else p.scratch(base_name, [T, D], role="r"))
    outt = p.scratch(out_name, [T, D], role="w")
    fms = [p.scratch(nm, [T, D] if kd == "tok" else [NCH, 128, T], role="r")
           for nm, kd in terms]
    ident = p.inp("IDENT", [128, 128])
    need_rg = any(k == "rg" for _, k in terms)

    xb = p.alloc(D)
    xb2 = p.alloc(D)
    ft = [p.alloc(D) for _ in terms]
    idt = p.alloc(128)
    rgc = p.alloc(4) if need_rg else None
    onec = p.alloc(128) if need_rg else None
    pb = p.psum_banks()

    g_c = p.ldgroup()
    p.dma_g("sync", idt, ident[:, :], g_c)
    if need_rg:
        rg = p.inp("L1_attn_res_gate", [1])
        p.dma_g("sync", rgc[0:1, 0:1], rg[0:1].unsqueeze(1), g_c)
    p.wait_t("tensor", g_c)
    p.wait_t("vector", g_c)
    if need_rg:
        cz = p.op("vector", "dve", lambda h, o=onec[0:1, 0:128]: h.memset(o, 1.0))
        p.wait("tensor", "dve", cz)
        cmb = p.op("tensor", "pe", lambda h, o=pb[3][0:128, 0:1], a=onec[0:1, 0:128],
                   b=rgc[0:1, 0:1]: h.matmul(o, a, b, start=True, stop=True))
        p.wait("vector", "pe", cmb)
        crg = p.op("vector", "dve", lambda h, o=rgc[:, 1:2], a=pb[3][0:128, 0:1]:
                   h.tensor_copy(o, a))
        p.set_bank_gate(3, "dve", crg)

    ntile = T // 128
    last_gs = None
    for i in range(ntile):
        if last_gs:
            p.wait_t("sync", last_gs)
        g_x = p.ldgroup()
        p.dma_g("sync", xb, base[i * 128:(i + 1) * 128, :], g_x)
        for t_i, (nm, kd) in enumerate(terms):
            if kd == "tok":
                p.dma_g("sync", ft[t_i], fms[t_i][i * 128:(i + 1) * 128, :], g_x)
            else:
                p.dma_g("sync", ft[t_i].rearrange("p (c t) -> p c t", t=128),
                        fms[t_i][:, :, i * 128:(i + 1) * 128].rearrange("c p t -> p c t"), g_x)
        p.wait_t("tensor", g_x)
        p.wait_t("vector", g_x)
        for c in range(NCH):
            for t_i, (nm, kind) in enumerate(terms):
                if kind == "tok":
                    if c == 0:
                        cd = p.op("vector", "dve", lambda h, o=xb, a=ft[t_i], x2=xb:
                                  h.tensor_add(o, a, x2))
                        p.wait("vector", "dve", cd)
                    continue
                bidx = t_i % 2
                bank = pb[bidx]
                p.use_bank("tensor", bidx)
                cm = p.op("tensor", "pe", lambda h, o=bank[0:128, 0:128],
                          a=ft[t_i][:, c * 128:(c + 1) * 128], i=idt: h.transpose(o, a, i))
                p.wait("vector", "pe", cm)
                if kind == "rg":
                    cd = p.op("vector", "dve", lambda h, o=xb[:, c * 128:(c + 1) * 128],
                              a=bank[0:128, 0:128], m=rgc[:, 1:2], x2=xb[:, c * 128:(c + 1) * 128]:
                              h.scalar_tensor_tensor(o, a, m, x2, ALU.mult, ALU.add))
                else:
                    cd = p.op("vector", "dve", lambda h, o=xb[:, c * 128:(c + 1) * 128],
                              a=bank[0:128, 0:128], x2=xb[:, c * 128:(c + 1) * 128]:
                              h.tensor_tensor(o, a, x2, ALU.add))
                p.set_bank_gate(bidx, "dve", cd)
        p.wait("scalar", "dve", cd)
        gs = p.dma("scalar", outt[i * 128:(i + 1) * 128, :], xb, "st")
        p.wait_t("scalar", gs)
        last_gs = gs
    p.fence()
    p.phase_end()


def _deq_weight(p, w_name, Kd, Nd, out_name, sc_name):
    nkc = Kd // 128
    nkb = Kd // 32
    nnc = Nd // 128
    w = p.inp(w_name, [Kd, Nd])
    wout = p.scratch(out_name, [nkc, 128, Nd], BF16, role="w")
    schbm = p.scratch(sc_name, [nkb, Nd], role="w")
    sihbm = p.scratch(sc_name + "_inv", [nkb, Nd], role="w")
    ident = p.inp("IDENT", [128, 128])
    exp32 = p.inp("EXP32", [4, 128])
    pb = p.psum_banks()

    wblk = p.alloc(128)
    wT = p.alloc(128)
    scT = p.alloc(nkc * 4)
    scTi = p.alloc(nkc * 4, I32)
    msk_m = p.alloc(nkc * 4, I32)
    idt = p.alloc(128)
    e32 = p.alloc(128)
    ev = p.alloc(128)

    g_c = p.ldgroup()
    p.dma_g("sync", idt, ident[:, :], g_c)
    p.dma_g("sync", e32[0:4, :], exp32[:, :], g_c)
    p.wait_t("tensor", g_c)
    p.wait_t("vector", g_c)
    cmsk = p.op("vector", "dve", lambda h, o=msk_m: h.memset(o, -8388608))

    last_sc_store = None
    last_ce = 0
    for nn in range(nnc):
        if last_sc_store:
            p.wait_t("sync", last_sc_store)
        if last_ce:
            p.wait("vector", "act", last_ce)
        prev_evt = 0
        for kk in range(nkc):
            if prev_evt:
                p.wait("sync", "dve", prev_evt)
            g_wb = p.dma("sync", wblk, w[kk * 128:(kk + 1) * 128,
                                         nn * 128:(nn + 1) * 128], "ld")
            p.wait_t("tensor", g_wb)
            bidx = 0 if kk % 2 == 0 else 1
            bank = pb[bidx]
            p.use_bank("tensor", bidx)
            cm = p.op("tensor", "pe", lambda h, o=bank[0:128, 0:128], a=wblk, i=idt:
                      h.transpose(o, a, i))
            p.wait("vector", "pe", cm)
            cd = p.op("vector", "dve", lambda h, o=wT, a=bank[0:128, 0:128]:
                      h.tensor_copy(o, a))
            p.set_bank_gate(bidx, "dve", cd)
            p.wait("vector", "dve", cd)
            prev_evt = p.op("vector", "dve", lambda h, o=scT[:, kk * 4:(kk + 1) * 4],
                            a=wT.rearrange("p (g k) -> p g k", k=32):
                            h.reduce_max(o, a, axis=AX.X, apply_absolute_value=True))
        p.wait("vector", "dve", prev_evt)
        ci1 = p.op("vector", "dve", lambda h, o=scTi, a=scT.bitcast(I32), m=msk_m:
                   h.tensor_tensor(o, a, m, ALU.bitwise_and))
        p.wait("vector", "dve", ci1)
        ci2 = p.op("vector", "dve", lambda h, o=scT, a=scTi.bitcast(F32):
                   h.tensor_scalar(o, a, 0.25, None, ALU.mult))
        p.wait("vector", "dve", ci2)
        ci3 = p.op("vector", "dve", lambda h, o=scTi, a=scT.bitcast(I32):
                   h.tensor_scalar(o, a, -1.0, 2130706432.0, ALU.mult, ALU.add))
        p.wait("tensor", "dve", ci3)
        nkb_cols = nkc * 4
        for src, dsthbm in ((scT, schbm), (scTi.bitcast(F32), sihbm)):
            for q0 in range(0, nkb_cols, 128):
                qn = min(128, nkb_cols - q0)
                bank = pb[2]
                p.use_bank("tensor", 2)
                cm = p.op("tensor", "pe", lambda h, o=bank[0:128, 0:128], a=src[:, q0:q0 + qn],
                          i=idt, q=qn: h.transpose(o[0:q, 0:128], a, i))
                p.wait("scalar", "pe", cm)
                ce = p.op("scalar", "act", lambda h, o=ev[0:128, 0:128],
                          a=bank[0:128, 0:128], q=qn: h.copy(o[0:q, :], a[0:q, :]))
                p.set_bank_gate(2, "act", ce)
                last_ce = ce
                p.wait("scalar", "act", ce)
                last_sc_store = p.dma("scalar", dsthbm[q0:q0 + qn, nn * 128:(nn + 1) * 128],
                                      ev[0:qn, 0:128], "st")
                p.wait_t("scalar", last_sc_store)

    p.wait_t("sync", last_sc_store)
    CW = min(Nd, 4096)
    nch_col = Nd // CW
    wsl = p.alloc(CW)
    scn = p.alloc(CW)
    inv = p.alloc(CW)
    aa = p.alloc(CW)
    t1 = p.alloc(CW, I32)
    low = p.alloc(CW)
    lowb = p.alloc(CW)
    hmk = p.alloc(CW, I32)
    sgn = p.alloc(CW)
    obf = p.alloc(CW, BF16)
    last_b = None
    for kk in range(nkc):
        for cc in range(nch_col):
            if last_b:
                p.wait_t("sync", last_b)
            g_w = p.ldgroup()
            p.dma_g("sync", wsl, w[kk * 128:(kk + 1) * 128, cc * CW:(cc + 1) * CW], g_w)
            g_s = p.ldgroup()
            p.dma_g("sync", lowb[0:4, 0:CW],
                    schbm[kk * 4:(kk + 1) * 4, cc * CW:(cc + 1) * CW], g_s)
            p.dma_g("sync", hmk.bitcast(F32)[0:4, 0:CW],
                    sihbm[kk * 4:(kk + 1) * 4, cc * CW:(cc + 1) * CW], g_s)
            p.wait_t("tensor", g_s)
            nslice = CW // 512
            cdx = 0
            for src4, dstfull in ((lowb[0:4, 0:CW], scn), (hmk.bitcast(F32)[0:4, 0:CW], inv)):
                for jj in range(nslice):
                    bidx = 3 + (jj % 2)
                    bank = pb[bidx]
                    p.use_bank("tensor", bidx)
                    cm = p.op("tensor", "pe", lambda h, o=bank[0:128, 0:512],
                              a=e32[0:4, :], b=src4[:, jj * 512:(jj + 1) * 512]:
                              h.matmul(o, a, b, start=True, stop=True))
                    p.wait("vector", "pe", cm)
                    cdx = p.op("vector", "dve", lambda h, o=dstfull[:, jj * 512:(jj + 1) * 512],
                               a=bank[0:128, 0:512]: h.tensor_copy(o, a))
                    p.set_bank_gate(bidx, "dve", cdx)
            p.wait("vector", "dve", cdx)
            p.wait_t("vector", g_w)
            c2 = cdx
            p.wait_t("scalar", g_w)
            c3 = p.op("scalar", "act", lambda h, o=aa, a=wsl: h.activation(o, a, AF.Abs))
            c3b = p.op("scalar", "act", lambda h, o=sgn, a=wsl: h.activation(o, a, AF.Sign))
            p.wait("vector", "act", c3b)
            c4 = p.op("vector", "dve", lambda h, o=t1.bitcast(F32), a=aa, b=inv:
                      h.tensor_mul(o, a, b))
            p.wait("vector", "dve", c4)
            c4b = p.op("vector", "dve", lambda h, o=aa, a=t1.bitcast(F32): h.tensor_copy(o, a))
            p.wait("vector", "dve", c4b)
            chain_bufs = [low, lowb]
            cch = p.op("vector", "dve", lambda h, o=low, a=aa: h.tensor_scalar(
                o, a, 0.25, None, ALU.is_ge))
            for jj, thr in enumerate((0.75, 1.25, 1.75, 2.5, 2.5, 3.5, 3.5, 5.0, 5.0, 5.0, 5.0)):
                p.wait("vector", "dve", cch)
                srcb = chain_bufs[jj % 2]
                dstb = chain_bufs[(jj + 1) % 2]
                cch = p.op("vector", "dve", lambda h, o=dstb, a=aa, b=srcb, t=thr:
                           h.scalar_tensor_tensor(o, a, t, b, ALU.is_ge, ALU.add))
            p.wait("vector", "dve", cch)
            qfin = chain_bufs[11 % 2]
            c13 = p.op("vector", "dve", lambda h, o=low2 if False else hmk.bitcast(F32),
                       a=qfin, b=sgn: h.tensor_mul(o, a, b))
            p.wait("vector", "dve", c13)
            c14 = p.op("vector", "dve", lambda h, o=obf, a=hmk.bitcast(F32), b=scn:
                       h.scalar_tensor_tensor(o, a, 0.5, b, ALU.mult, ALU.mult))
            p.wait("gpsimd", "dve", c14)
            last_b = p.dma("gpsimd", wout[kk, :, cc * CW:(cc + 1) * CW], obf, "st")
            p.wait_t("gpsimd", last_b)


def ph_deq(p):
    p.phase_begin()
    _deq_weight(p, "L0_ffn_up_kernel", D, FF, "wup_bf", "sc_up")
    p.fence()
    p.phase_end()
    p.phase_begin()
    _deq_weight(p, "L0_ffn_down_kernel", FF, D, "wdn_bf", "sc_dn")
    p.fence()
    p.phase_end()


def ph_ffn(p):
    tb = 512
    p.phase_begin()
    xn = p.scratch("xn0", [NCH, 128, T], role="r")
    wup = p.scratch("wup_bf", [NCH, 128, FF], BF16, role="r")
    wdn = p.scratch("wdn_bf", [FF // 128, 128, D], BF16, role="r")
    bup = p.inp("L0_ffn_up_bias", [FF])
    bdn = p.inp("L0_ffn_down_bias", [D])
    out = p.scratch("ffn_out", [NCH, 128, T], role="w")

    NQ = 4
    QH = FF // NQ
    nmq = QH // 128

    wbuf = p.alloc(16 * QH, BF16)
    xhi = p.alloc(16 * tb, BF16)
    xlo = p.alloc(16 * tb, BF16)
    hhi = p.alloc(nmq * tb, BF16)
    hlo = p.alloc(nmq * tb, BF16)
    acc = p.alloc(16 * tb)
    ght = p.alloc(tb)
    ght2 = p.alloc(tb)
    ev = p.alloc(tb)
    bupt = p.alloc(FF // 128)
    bdnt = p.alloc(16)
    pb = p.psum_banks()

    g_b = p.ldgroup()
    for m in range(FF // 128):
        p.dma_g("sync", bupt[:, m:m + 1], bup[m * 128:(m + 1) * 128].unsqueeze(1), g_b)
    for m in range(16):
        p.dma_g("sync", bdnt[:, m:m + 1], bdn[m * 128:(m + 1) * 128].unsqueeze(1), g_b)
    p.wait_t("scalar", g_b)
    p.wait_t("vector", g_b)

    last_up_mm = 0
    last_dn_mm = 0
    for it in range(T // tb):
        if last_up_mm:
            p.wait("sync", "pe", last_up_mm)
        cspl = 0
        for k in range(NCH):
            if cspl:
                p.wait("sync", "dve", cspl)
            g_k = p.dma("sync", ght, xn[k, :, it * tb:(it + 1) * tb], "ld")
            p.wait_t("vector", g_k)
            ch = p.op("vector", "dve", lambda h, o=xhi[:, k * tb:(k + 1) * tb], a=ght:
                      h.tensor_copy(o, a))
            p.wait("vector", "dve", ch)
            cspl = p.op("vector", "dve", lambda h, o=xlo[:, k * tb:(k + 1) * tb], a=ght,
                        b=xhi[:, k * tb:(k + 1) * tb]: h.tensor_sub(o, a, b))
        p.wait("tensor", "dve", cspl)
        for q in range(NQ):
            if last_up_mm:
                p.wait("sync", "pe", max(last_up_mm, last_dn_mm))
            g_w = p.ldgroup()
            for k in range(NCH):
                p.dma_g("sync", wbuf[:, k * QH:(k + 1) * QH],
                        wup[k, :, q * QH:(q + 1) * QH], g_w)
            p.wait_t("tensor", g_w)
            chl = 0
            for m in range(nmq):
                bidx = m % 2
                bank = pb[bidx]
                p.use_bank("tensor", bidx)
                for k in range(NCH):
                    cm = p.op("tensor", "pe", lambda h, o=bank[0:128, 0:tb],
                              a=wbuf[:, k * QH + m * 128:k * QH + m * 128 + 128],
                              b=xhi[:, k * tb:(k + 1) * tb], s=(k == 0):
                              h.matmul(o, a, b, start=s, stop=False))
                    cm = p.op("tensor", "pe", lambda h, o=bank[0:128, 0:tb],
                              a=wbuf[:, k * QH + m * 128:k * QH + m * 128 + 128],
                              b=xlo[:, k * tb:(k + 1) * tb], e=(k == NCH - 1):
                              h.matmul(o, a, b, start=False, stop=e))
                last_up_mm = cm
                p.wait("scalar", "pe", cm)
                cg = p.op("scalar", "act", lambda h, o=ght, a=bank[0:128, 0:tb],
                          bb=bupt[:, q * nmq + m:q * nmq + m + 1]:
                          h.activation(o, a, AF.Gelu_apprx_tanh, bias=bb))
                p.set_bank_gate(bidx, "act", cg)
                p.wait("vector", "act", cg)
                ch = p.op("vector", "dve", lambda h, o=hhi[:, m * tb:(m + 1) * tb], a=ght:
                          h.tensor_copy(o, a))
                p.wait("vector", "dve", ch)
                chl = p.op("vector", "dve", lambda h, o=hlo[:, m * tb:(m + 1) * tb], a=ght,
                           b=hhi[:, m * tb:(m + 1) * tb]: h.tensor_sub(o, a, b))
            p.wait("sync", "pe", last_up_mm)
            g_w2 = p.ldgroup()
            for k in range(nmq):
                p.dma_g("sync", wbuf[:, k * D:(k + 1) * D],
                        wdn[q * nmq + k, :, :], g_w2)
            p.wait_t("tensor", g_w2)
            p.wait("tensor", "dve", chl)
            for md in range(16):
                bidx = 2 + (md % 2)
                bank = pb[bidx]
                p.use_bank("tensor", bidx)
                for k in range(nmq):
                    cm = p.op("tensor", "pe", lambda h, o=bank[0:128, 0:tb],
                              a=wbuf[:, k * D + md * 128:k * D + md * 128 + 128],
                              b=hhi[:, k * tb:(k + 1) * tb], s=(k == 0):
                              h.matmul(o, a, b, start=s, stop=False))
                    cm = p.op("tensor", "pe", lambda h, o=bank[0:128, 0:tb],
                              a=wbuf[:, k * D + md * 128:k * D + md * 128 + 128],
                              b=hlo[:, k * tb:(k + 1) * tb], e=(k == nmq - 1):
                              h.matmul(o, a, b, start=False, stop=e))
                last_dn_mm = cm
                p.wait("vector", "pe", cm)
                if q == 0:
                    cacc = p.op("vector", "dve", lambda h, o=acc[:, md * tb:(md + 1) * tb],
                                a=bank[0:128, 0:tb]: h.tensor_copy(o, a))
                else:
                    ct = p.op("vector", "dve", lambda h, o=ght2, a=bank[0:128, 0:tb],
                              b=acc[:, md * tb:(md + 1) * tb]: h.tensor_add(o, a, b))
                    p.wait("vector", "dve", ct)
                    cacc = p.op("vector", "dve", lambda h, o=acc[:, md * tb:(md + 1) * tb],
                                a=ght2: h.tensor_copy(o, a))
                p.set_bank_gate(bidx, "dve", cacc)
        for md in range(16):
            cf = p.op("vector", "dve", lambda h, o=ev, a=acc[:, md * tb:(md + 1) * tb],
                      bb=bdnt[:, md:md + 1]: h.tensor_scalar(o, a, bb, None, ALU.add))
            p.wait("scalar", "dve", cf)
            gs = p.dma("scalar", out[md, :, it * tb:(it + 1) * tb], ev, "st")
            p.wait_t("scalar", gs)
            p.wait_t("vector", gs)
    p.fence()
    p.phase_end()





def ph_conv(p):
    p.phase_begin()
    xz = p.scratch("xz_fm", [24, 128, T], role="r")
    cw = p.inp("L1_attn_conv_w", [DI, 1, 4])
    cb = p.inp("L1_attn_conv_b", [DI])
    out = p.scratch("xconv", [12, 128, T], role="w")

    xt = p.alloc(T + 4)
    acc = p.alloc(T)
    acc2 = p.alloc(T)
    wc = p.alloc(48)
    bc = p.alloc(12)
    g_c = p.ldgroup()
    for c in range(12):
        p.dma_g("sync", wc[:, c * 4:(c + 1) * 4],
                cw[c * 128:(c + 1) * 128, 0, :], g_c)
        p.dma_g("sync", bc[:, c:c + 1], cb[c * 128:(c + 1) * 128].unsqueeze(1), g_c)
    p.wait_t("vector", g_c)
    p.wait_t("scalar", g_c)

    last_gs = None
    for c in range(12):
        if last_gs:
            p.wait_t("sync", last_gs)
        cz = p.op("vector", "dve", lambda h, o=xt[:, 0:3]: h.memset(o, 0.0))
        g_x = p.dma("sync", xt[:, 3:T + 3], xz[c, :, :], "ld")
        p.wait_t("vector", g_x)
        p.wait("vector", "dve", cz)
        c0 = p.op("vector", "dve", lambda h, o=acc, a=xt[:, 0:T], m=wc[:, c * 4:c * 4 + 1]:
                  h.tensor_scalar(o, a, m, None, ALU.mult))
        bufs = [acc, acc2]
        for j in range(1, 4):
            p.wait("vector", "dve", c0)
            c0 = p.op("vector", "dve", lambda h, o=bufs[j % 2], a=xt[:, j:T + j],
                      m=wc[:, c * 4 + j:c * 4 + j + 1], b=bufs[(j - 1) % 2]:
                      h.scalar_tensor_tensor(o, a, m, b, ALU.mult, ALU.add))
        p.wait("scalar", "dve", c0)
        for q4 in range(4):
            sl4 = slice(q4 * (T // 4), (q4 + 1) * (T // 4))
            cs = p.op("scalar", "act", lambda h, o=acc[:, sl4], a=acc2[:, sl4],
                      bb=bc[:, c:c + 1]: h.activation(o, a, AF.Silu, bias=bb))
        p.wait("scalar", "act", cs)
        last_gs = p.dma("scalar", out[c, :, :], acc, "st")
        p.wait_t("scalar", last_gs)
    p.fence()
    p.phase_end()


def ph_scanprep1(p):
    p.phase_begin()
    xp = p.scratch("xp_fm", [25, 128, T], role="r").rearrange("c p t -> (c p) t")
    xpn = p.scratch("xpn", [25 * 128, T], role="w")
    xconv = p.scratch("xconv", [12, 128, T], role="r")
    snbh = p.scratch("snb_hbm", [H1, T], role="w")
    mallh = p.scratch("mall_hbm", [H1, T], role="w")

    W = 512
    bt = p.alloc(W)
    btn = p.alloc(W)
    sq = p.alloc(W)
    row = p.alloc(5 * W)
    ones = p.alloc(128)
    pb = p.psum_banks()
    cz = p.op("vector", "dve", lambda h, o=ones: h.memset(o, 1.0))
    p.wait("tensor", "dve", cz)

    last_gs = None
    last_cm = 0
    for hd in range(H1):
        for part in range(2):
            r0 = hd * 129 + 1 + part * S1
            for it in range(T // W):
                if last_gs:
                    p.wait_t("sync", last_gs)
                g_b = p.dma("sync", bt[0:S1, :], xp[r0:r0 + S1, it * W:(it + 1) * W], "ld")
                p.wait_t("scalar", g_b)
                if last_cm:
                    p.wait("scalar", "pe", last_cm)
                c1 = p.op("scalar", "act", lambda h, o=sq[0:S1, :], a=bt[0:S1, :]:
                          h.activation(o, a, AF.Square))
                p.wait("tensor", "act", c1)
                p.use_bank("tensor", 0)
                cm = p.op("tensor", "pe", lambda h, o=pb[0][0:1, 0:W], a=ones[0:S1, 0:1],
                          b=sq[0:S1, :]: h.matmul(o, a, b, start=True, stop=True))
                last_cm = cm
                p.wait("vector", "pe", cm)
                c2 = p.op("vector", "dve", lambda h, o=row[0:1, 0:W], a=pb[0][0:1, 0:W]:
                          h.tensor_copy(o, a))
                p.set_bank_gate(0, "dve", c2)
                p.wait("scalar", "dve", c2)
                c3 = p.op("scalar", "act", lambda h, o=row[0:1, W:2 * W], a=row[0:1, 0:W]:
                          h.activation(o, a, AF.Sqrt))
                p.wait("vector", "act", c3)
                c4 = p.op("vector", "dve", lambda h, o=row[0:1, W:2 * W], a=row[0:1, W:2 * W]:
                          h.tensor_scalar(o, a, 1e-12, None, ALU.max))
                p.wait("vector", "dve", c4)
                c5 = p.op("vector", "dve", lambda h, o=row[0:1, 2 * W:3 * W],
                          a=row[0:1, W:2 * W]: h.reciprocal(o, a))
                p.wait("vector", "dve", c5)
                if part == 0:
                    c6 = p.op("vector", "dve", lambda h, o=row[0:1, 3 * W:4 * W],
                              a=row[0:1, 2 * W:3 * W], b=row[0:1, 2 * W:3 * W]:
                              h.tensor_mul(o, a, b))
                    p.wait("vector", "dve", c6)
                    c7 = p.op("vector", "dve", lambda h, o=row[0:1, 4 * W:5 * W],
                              a=row[0:1, 3 * W:4 * W], b=row[0:1, 0:W]:
                              h.tensor_mul(o, a, b))
                    p.wait("scalar", "dve", c7)
                    gs0 = p.dma("scalar", snbh[hd:hd + 1, it * W:(it + 1) * W],
                                row[0:1, 4 * W:5 * W], "st")
                    p.wait_t("scalar", gs0)
                p.use_bank("tensor", 1)
                p.wait("tensor", "dve", c5)
                cmb = p.op("tensor", "pe", lambda h, o=pb[1][0:S1, 0:W], a=ones[0:1, 0:S1],
                           b=row[0:1, 2 * W:3 * W]: h.matmul(o, a, b, start=True, stop=True))
                p.wait("vector", "pe", cmb)
                c8 = p.op("vector", "dve", lambda h, o=btn[0:S1, :], a=bt[0:S1, :],
                          b=pb[1][0:S1, 0:W]: h.tensor_mul(o, a, b))
                p.set_bank_gate(1, "dve", c8)
                p.wait("scalar", "dve", c8)
                last_gs = p.dma("scalar", xpn[r0:r0 + S1, it * W:(it + 1) * W],
                                btn[0:S1, :], "st")
                p.wait_t("scalar", last_gs)
    for hd in range(H1):
        hp, off = hd // 2, 64 * (hd % 2)
        for it in range(T // W):
            if last_gs:
                p.wait_t("sync", last_gs)
            g_b = p.dma("sync", bt[0:S1, :], xconv[hp, off:off + 64, it * W:(it + 1) * W], "ld")
            p.wait_t("tensor", g_b)
            p.use_bank("tensor", 0)
            cm = p.op("tensor", "pe", lambda h, o=pb[0][0:1, 0:W], a=ones[0:64, 0:1],
                      b=bt[0:64, :]: h.matmul(o, a, b, start=True, stop=True))
            p.wait("vector", "pe", cm)
            c2 = p.op("vector", "dve", lambda h, o=row[0:1, 0:W], a=pb[0][0:1, 0:W]:
                      h.tensor_scalar(o, a, 1.0 / 64.0, None, ALU.mult))
            p.set_bank_gate(0, "dve", c2)
            p.wait("scalar", "dve", c2)
            last_gs = p.dma("scalar", mallh[hd:hd + 1, it * W:(it + 1) * W],
                            row[0:1, 0:W], "st")
            p.wait_t("scalar", last_gs)
    p.fence()
    p.phase_end()


def ph_scanprep2(p):
    p.phase_begin()
    xp = p.scratch("xp_fm", [25, 128, T], role="r").rearrange("c p t -> (c p) t")
    snbh = p.scratch("snb_hbm", [H1, T], role="r")
    mallh = p.scratch("mall_hbm", [H1, T], role="r")
    alog = p.inp("L1_attn_A_log", [H1])
    l2ab = p.inp("L1_attn_log2_alpha_base", [H1])
    l2b = p.inp("L1_attn_log2_beta", [H1])
    sema = p.inp("L1_attn_surprise_ema", [H1])
    ident = p.inp("IDENT", [128, 128])
    nck = T // 128
    eT = p.scratch("e_T", [nck, 128, H1], role="w")
    omT = p.scratch("om_T", [nck, 128, H1], role="w")
    exT = p.scratch("ex_T", [nck, 128, H1], role="w")

    HH = H1
    LN2 = 0.6931471805599453
    dt = p.alloc(T)
    mm = p.alloc(T)
    sb = p.alloc(T)
    w1 = p.alloc(T)
    w2 = p.alloc(T)
    w3 = p.alloc(T)
    al = p.alloc(T)
    om = p.alloc(T)
    ec = p.alloc(T)
    colv = p.alloc(16)
    onesd = p.alloc(128)
    idt = p.alloc(128)
    ev = p.alloc(3 * 32)
    pb = p.psum_banks()

    g_c = p.ldgroup()
    p.dma_g("sync", idt, ident[:, :], g_c)
    p.dma_g("sync", colv[0:HH, 0:1], alog[:].unsqueeze(1), g_c)
    p.dma_g("sync", colv[0:HH, 1:2], l2ab[:].unsqueeze(1), g_c)
    p.dma_g("sync", colv[0:HH, 2:3], l2b[:].unsqueeze(1), g_c)
    p.dma_g("sync", colv[0:HH, 3:4], sema[:].unsqueeze(1), g_c)
    g_d = p.ldgroup()
    p.dma_g("sync", dt[0:HH, :], xp[0:24 * 129:129, :], g_d)
    p.dma_g("sync", mm[0:HH, :], mallh[:, :], g_d)
    p.dma_g("sync", sb[0:HH, :], snbh[:, :], g_d)
    for e in ("vector", "scalar", "tensor"):
        p.wait_t(e, g_c)
        p.wait_t(e, g_d)

    def vop(fn):
        c = p.op("vector", "dve", fn)
        p.wait("vector", "dve", c)
        return c

    def sop(fn):
        c = p.op("scalar", "act", fn)
        p.wait("scalar", "act", c)
        return c

    c = vop(lambda h, o=colv[0:HH, 4:5], a=colv[0:HH, 0:1]: h.tensor_scalar(
        o, a, -2.3, -0.01, ALU.max, ALU.min))
    c = vop(lambda h, o=colv[0:HH, 5:6], a=colv[0:HH, 1:2]: h.tensor_scalar(
        o, a, -3.32, -0.015, ALU.max, ALU.min))
    p.wait("scalar", "dve", c)
    c = sop(lambda h, o=colv[0:HH, 5:6], a=colv[0:HH, 5:6]: h.activation(
        o, a, AF.Exp, scale=LN2))
    p.wait("vector", "act", c)
    c = vop(lambda h, o=colv[0:HH, 5:6], a=colv[0:HH, 5:6]: h.tensor_scalar(
        o, a, -1.0, 1.0, ALU.mult, ALU.add))
    c = vop(lambda h, o=colv[0:HH, 6:7], a=colv[0:HH, 5:6]: h.tensor_scalar(
        o, a, -1.0, 1.0, ALU.mult, ALU.add))
    c = vop(lambda h, o=colv[0:HH, 7:8], a=colv[0:HH, 2:3]: h.tensor_scalar(
        o, a, -2.0, 2.0, ALU.max, ALU.min))
    p.wait("scalar", "dve", c)
    c = sop(lambda h, o=colv[0:HH, 7:8], a=colv[0:HH, 7:8]: h.activation(
        o, a, AF.Exp, scale=LN2))
    p.wait("vector", "act", c)
    c = vop(lambda h, o=colv[0:HH, 9:10], a=colv[0:HH, 3:4]: h.tensor_scalar(
        o, a, 1e-6, None, ALU.add))
    c = vop(lambda h, o=colv[0:HH, 9:10], a=colv[0:HH, 9:10]: h.reciprocal(o, a))
    cz = vop(lambda h, o=onesd[0:HH, :]: h.memset(o, 1.0))
    p.wait("scalar", "dve", cz)

    c = sop(lambda h, o=w1[0:HH, :], a=dt[0:HH, :]: h.activation(o, a, AF.Abs))
    c = sop(lambda h, o=w1[0:HH, :], a=w1[0:HH, :]: h.activation(o, a, AF.Exp, scale=-1.0))
    c = sop(lambda h, o=w1[0:HH, :], a=w1[0:HH, :], bb=onesd[0:HH, 0:1]:
            h.activation(o, a, AF.Ln, bias=bb))
    p.wait("vector", "act", c)
    c = vop(lambda h, o=w2[0:HH, :], a=dt[0:HH, :]: h.tensor_scalar(o, a, 0.0, None, ALU.max))
    c = vop(lambda h, o=w3[0:HH, :], a=w1[0:HH, :], b=w2[0:HH, :]: h.tensor_add(o, a, b))
    c = vop(lambda h, o=w1[0:HH, :], a=w3[0:HH, :], m=colv[0:HH, 4:5]:
            h.tensor_scalar(o, a, m, None, ALU.mult))
    p.wait("scalar", "dve", c)
    c = sop(lambda h, o=w2[0:HH, :], a=w1[0:HH, :]: h.activation(o, a, AF.Exp))
    p.wait("vector", "act", c)
    c = vop(lambda h, o=w2[0:HH, :], a=w2[0:HH, :]: h.tensor_scalar(
        o, a, -1.0, 1.0, ALU.mult, ALU.add))
    c = vop(lambda h, o=w3[0:HH, :], a=w2[0:HH, :], b=mm[0:HH, :]: h.tensor_mul(o, a, b))
    p.wait("scalar", "dve", c)
    c = sop(lambda h, o=w2[0:HH, :], a=w3[0:HH, :]: h.activation(o, a, AF.Square))
    p.wait("vector", "act", c)
    c = vop(lambda h, o=w3[0:HH, :], a=w2[0:HH, :], b=sb[0:HH, :]: h.tensor_mul(o, a, b))
    c = vop(lambda h, o=w2[0:HH, :], a=w3[0:HH, :], m=colv[0:HH, 9:10]:
            h.tensor_scalar(o, a, m, 1.0 / S1, ALU.mult, ALU.mult))
    c = vop(lambda h, o=w2[0:HH, :], a=w2[0:HH, :], m=colv[0:HH, 7:8]:
            h.tensor_scalar(o, a, m, None, ALU.mult))
    p.wait("scalar", "dve", c)
    c = sop(lambda h, o=w2[0:HH, :], a=w2[0:HH, :]: h.activation(o, a, AF.Tanh))
    p.wait("vector", "act", c)
    c = vop(lambda h, o=w2[0:HH, :], a=w2[0:HH, :]: h.tensor_scalar(o, a, 0.0, None, ALU.max))
    c = vop(lambda h, o=al[0:HH, :], a=w2[0:HH, :], m=colv[0:HH, 6:7]:
            h.tensor_scalar(o, a, m, None, ALU.mult))
    c = vop(lambda h, o=al[0:HH, :], a=al[0:HH, :], m=colv[0:HH, 5:6]:
            h.tensor_scalar(o, a, m, None, ALU.add))
    c = vop(lambda h, o=al[0:HH, :], a=al[0:HH, :]: h.tensor_scalar(
        o, a, 0.01, 0.999, ALU.max, ALU.min))
    c = vop(lambda h, o=om[0:HH, :], a=al[0:HH, :]: h.tensor_scalar(
        o, a, -1.0, 1.0, ALU.mult, ALU.add))
    p.wait("scalar", "dve", c)
    c = sop(lambda h, o=w1[0:HH, :], a=al[0:HH, :]: h.activation(o, a, AF.Ln))
    p.wait("vector", "act", c)
    for cc in range(nck):
        c = vop(lambda h, o=ec[0:HH, cc * 128:(cc + 1) * 128], d0=onesd[0:HH, :],
                d1=w1[0:HH, cc * 128:(cc + 1) * 128]:
                h.tensor_tensor_scan(o, d0, d1, 0.0, ALU.mult, ALU.add))
    p.wait("scalar", "dve", c)
    c = sop(lambda h, o=w2[0:HH, :], a=ec[0:HH, :]: h.activation(o, a, AF.Exp))
    p.wait("tensor", "act", c)
    p.wait("tensor", "dve", c)
    last_gs = None
    for cc in range(nck):
        for src, dst, ei in ((ec, eT, 0), (om, omT, 1), (w2, exT, 2)):
            bidx = ei % 2
            bank = pb[bidx]
            p.use_bank("tensor", bidx)
            cm = p.op("tensor", "pe", lambda h, o=bank[0:128, 0:HH],
                      a=src[0:HH, cc * 128:(cc + 1) * 128], i=idt:
                      h.transpose(o, a, i[0:HH, 0:HH]))
            p.wait("scalar", "pe", cm)
            if last_gs:
                p.wait_t("scalar", last_gs)
            ce = p.op("scalar", "act", lambda h, o=ev[:, ei * 32:ei * 32 + HH],
                      a=bank[0:128, 0:HH]: h.copy(o, a))
            p.set_bank_gate(bidx, "act", ce)
            p.wait("scalar", "act", ce)
            last_gs = p.dma("scalar", dst[cc, :, :], ev[:, ei * 32:ei * 32 + HH], "st")
            p.wait_t("scalar", last_gs)
    p.fence()
    p.phase_end()




def ph_scan(p):
    nck = T // 128
    p.phase_begin()
    xpn = p.scratch("xpn", [25 * 128, T], role="r")
    xconv = p.scratch("xconv", [12, 128, T], role="r")
    eT = p.scratch("e_T", [nck, 128, H1], role="r")
    omT = p.scratch("om_T", [nck, 128, H1], role="r")
    exT = p.scratch("ex_T", [nck, 128, H1], role="r")
    ltneg = p.inp("LTNEG", [128, 128])
    ident = p.inp("IDENT", [128, 128])
    yfm = p.scratch("y_fm", [12, 128, T], role="w")

    bfm = [p.alloc(128), p.alloc(128)]
    cfm = [p.alloc(128), p.alloc(128)]
    xfm = [p.alloc(128), p.alloc(128)]
    xtok = p.alloc(64)
    btok = p.alloc(64)
    dsb = p.alloc(128)
    wsb = p.alloc(128)
    y2 = p.alloc(64)
    ysb = p.alloc(64)
    yev = p.alloc(128)
    bw = p.alloc(64)
    erow = p.alloc(128)
    de = p.alloc(4)
    decol = p.alloc(4)
    eyedec = p.alloc(64)
    eTt = p.alloc(32)
    omTt = p.alloc(32)
    exTt = p.alloc(32)
    mneg = p.alloc(128)
    idt = p.alloc(128)
    onesr = p.alloc(128)
    hsA = p.alloc(H1 * 64)
    hsB = p.alloc(H1 * 64)
    pb = p.psum_banks()

    g_c = p.ldgroup()
    p.dma_g("sync", mneg, ltneg[:, :], g_c)
    p.dma_g("sync", idt, ident[:, :], g_c)
    p.wait_t("tensor", g_c)
    p.wait_t("vector", g_c)
    czo = p.op("vector", "dve", lambda h, o=onesr[0:1, :]: h.memset(o, 1.0))
    czA = p.op("vector", "dve", lambda h, o=hsA[0:64, :]: h.memset(o, 0.0))
    p.wait("tensor", "dve", czA)

    hprev = [czA] * H1
    cur, nxt = hsA, hsB
    last_chunk_gate = 0
    seq = [(cc, hd) for cc in range(nck) for hd in range(H1)]

    def issue_bh(idx):
        cc, hd = seq[idx]
        b = idx % 2
        hp, off = hd // 2, 64 * (hd % 2)
        r0 = hd * 129 + 1
        g1 = p.ldgroup()
        p.dma_g("sync", bfm[b][0:S1, :], xpn[r0:r0 + S1, cc * 128:(cc + 1) * 128], g1)
        p.dma_g("sync", cfm[b][0:S1, :], xpn[r0 + S1:r0 + 2 * S1, cc * 128:(cc + 1) * 128], g1)
        p.dma_g("sync", xfm[b][0:64, :], xconv[hp, off:off + 64, cc * 128:(cc + 1) * 128], g1)
        return g1

    g1_of = {0: issue_bh(0)}
    m4_of = {}
    for cc in range(nck):
        if last_chunk_gate:
            p.wait("sync", "dve", last_chunk_gate)
        g_e = p.ldgroup()
        p.dma_g("sync", eTt[:, 0:H1], eT[cc, :, :], g_e)
        p.dma_g("sync", omTt[:, 0:H1], omT[cc, :, :], g_e)
        p.dma_g("sync", exTt[:, 0:H1], exT[cc, :, :], g_e)
        p.wait_t("tensor", g_e)
        p.wait_t("vector", g_e)
        for hd in range(H1):
            hp, off = hd // 2, 64 * (hd % 2)
            r0 = hd * 129 + 1
            idx = cc * H1 + hd
            bb = idx % 2
            bfmb, cfmb, xfmb = bfm[bb], cfm[bb], xfm[bb]
            if idx + 1 < len(seq):
                if idx >= 1:
                    p.wait("sync", "pe", m4_of[idx - 1])
                g1_of[idx + 1] = issue_bh(idx + 1)
            g1 = g1_of[idx]
            p.wait_t("tensor", g1)
            p.use_bank("tensor", 7)
            cm_er = p.op("tensor", "pe", lambda h, o=pb[7][0:1, 0:128],
                         a=eTt[:, hd:hd + 1], i=idt: h.transpose(o, a, i))
            p.wait("scalar", "pe", cm_er)
            ce_er = p.op("scalar", "act", lambda h, o=erow[0:1, :], a=pb[7][0:1, 0:128]:
                         h.copy(o, a))
            p.set_bank_gate(7, "act", ce_er)
            c_de = p.op("scalar", "act", lambda h, o=de[0:1, 0:1], a=erow[0:1, 127:128]:
                        h.activation(o, a, AF.Exp))
            p.use_bank("tensor", 0)
            p.wait("tensor", "act", ce_er)
            p.wait("tensor", "dve", czo)
            m1 = p.op("tensor", "pe", lambda h, o=pb[0][0:128, 0:128], a=onesr[0:1, :],
                      b=erow[0:1, :]: h.matmul(o, a, b, start=True, stop=True))
            p.wait("vector", "pe", m1)
            s1 = p.op("vector", "dve", lambda h, o=dsb, a=pb[0][0:128, 0:128],
                      m=eTt[:, hd:hd + 1], b=mneg:
                      h.scalar_tensor_tensor(o, a, m, b, ALU.subtract, ALU.add))
            p.set_bank_gate(0, "dve", s1)
            p.wait("scalar", "dve", s1)
            e1 = p.op("scalar", "act", lambda h, o=dsb, a=dsb: h.activation(o, a, AF.Exp))
            p.wait("vector", "act", e1)
            s2 = p.op("vector", "dve", lambda h, o=dsb, a=dsb, m=omTt[:, hd:hd + 1]:
                      h.tensor_scalar(o, a, m, None, ALU.mult))
            p.use_bank("tensor", 1)
            m2 = p.op("tensor", "pe", lambda h, o=pb[1][0:128, 0:128], a=bfmb[0:S1, :],
                      b=cfmb[0:S1, :]: h.matmul(o, a, b, start=True, stop=True))
            p.wait("vector", "pe", m2)
            p.wait("vector", "dve", s2)
            s3 = p.op("vector", "dve", lambda h, o=wsb, a=pb[1][0:128, 0:128], b=dsb:
                      h.tensor_mul(o, a, b))
            p.set_bank_gate(1, "dve", s3)
            p.use_bank("tensor", 5)
            mtX = p.op("tensor", "pe", lambda h, o=pb[5][0:128, 0:64], a=xfmb[0:64, :],
                       i=idt: h.transpose(o, a, i[0:64, 0:64]))
            p.wait("vector", "pe", mtX)
            seX = p.op("vector", "dve", lambda h, o=xtok[:, 0:64], a=pb[5][0:128, 0:64]:
                       h.tensor_copy(o, a))
            p.set_bank_gate(5, "dve", seX)
            p.use_bank("tensor", 6)
            mtB = p.op("tensor", "pe", lambda h, o=pb[6][0:128, 0:64], a=bfmb[0:S1, :],
                       i=idt: h.transpose(o, a, i[0:64, 0:64]))
            p.wait("vector", "pe", mtB)
            seB = p.op("vector", "dve", lambda h, o=btok[:, 0:64], a=pb[6][0:128, 0:64]:
                       h.tensor_copy(o, a))
            p.set_bank_gate(6, "dve", seB)
            p.use_bank("tensor", 2)
            p.wait("tensor", "dve", s3)
            p.wait("tensor", "dve", seX)
            m3 = p.op("tensor", "pe", lambda h, o=pb[2][0:128, 0:64], a=wsb,
                      b=xtok[:, 0:64]: h.matmul(o, a, b, start=True, stop=True))
            p.use_bank("tensor", 3)
            p.wait("tensor", "dve", hprev[hd])
            m4 = p.op("tensor", "pe", lambda h, o=pb[3][0:128, 0:64], a=cfmb[0:S1, :],
                      b=cur[0:64, hd * 64:(hd + 1) * 64]: h.matmul(o, a, b, start=True, stop=True))
            m4_of[idx] = m4
            p.wait("vector", "pe", m4)
            s4 = p.op("vector", "dve", lambda h, o=y2[:, 0:64], a=pb[3][0:128, 0:64],
                      m=exTt[:, hd:hd + 1]: h.tensor_scalar(o, a, m, None, ALU.mult))
            p.set_bank_gate(3, "dve", s4)
            p.wait("vector", "pe", m3)
            p.wait("vector", "dve", s4)
            s5 = p.op("vector", "dve", lambda h, o=ysb[:, 0:64], a=pb[2][0:128, 0:64],
                      b=y2[:, 0:64]: h.tensor_add(o, a, b))
            p.set_bank_gate(2, "dve", s5)
            p.use_bank("tensor", 5)
            p.wait("tensor", "dve", s5)
            m5 = p.op("tensor", "pe", lambda h, o=pb[5][0:64, 0:128], a=ysb[:, 0:64],
                      i=idt: h.transpose(o, a, i))
            p.wait("scalar", "pe", m5)
            s6 = p.op("scalar", "act", lambda h, o=yev[0:64, :], a=pb[5][0:64, 0:128]:
                      h.copy(o, a))
            p.set_bank_gate(5, "act", s6)
            p.wait("scalar", "act", s6)
            gy = p.dma("scalar", yfm[hp, off:off + 64, cc * 128:(cc + 1) * 128],
                       yev[0:64, :], "st")
            p.wait_t("scalar", gy)
            p.wait("vector", "dve", seB)
            s7 = p.op("vector", "dve", lambda h, o=bw[:, 0:64], a=btok[:, 0:64],
                      m=dsb[:, 127:128]: h.tensor_scalar(o, a, m, None, ALU.mult))
            p.use_bank("tensor", 4)
            p.wait("tensor", "dve", s7)
            m6 = p.op("tensor", "pe", lambda h, o=pb[4][0:64, 0:64], a=bw[:, 0:64],
                      b=xtok[:, 0:64]: h.matmul(o, a, b, start=True, stop=False))
            p.use_bank("tensor", 7)
            p.wait("tensor", "act", c_de)
            m7 = p.op("tensor", "pe", lambda h, o=pb[7][0:64, 0:1], a=onesr[0:1, 0:64],
                      b=de[0:1, 0:1]: h.matmul(o, a, b, start=True, stop=True))
            p.wait("vector", "pe", m7)
            s9 = p.op("vector", "dve", lambda h, o=decol[0:64, 0:1], a=pb[7][0:64, 0:1]:
                      h.tensor_copy(o, a))
            p.set_bank_gate(7, "dve", s9)
            p.wait("vector", "dve", s9)
            s10 = p.op("vector", "dve", lambda h, o=eyedec[0:64, 0:64], a=idt[0:64, 0:64],
                       m=decol[0:64, 0:1]: h.tensor_scalar(o, a, m, None, ALU.mult))
            p.wait("tensor", "dve", s10)
            m8 = p.op("tensor", "pe", lambda h, o=pb[4][0:64, 0:64], a=eyedec[0:64, 0:64],
                      b=cur[0:64, hd * 64:(hd + 1) * 64]: h.matmul(o, a, b, start=False, stop=True))
            p.wait("vector", "pe", m8)
            s11 = p.op("vector", "dve", lambda h, o=nxt[0:64, hd * 64:(hd + 1) * 64],
                       a=pb[4][0:64, 0:64]: h.tensor_copy(o, a))
            p.set_bank_gate(4, "dve", s11)
            hprev[hd] = s11
            p.wait("sync", "dve", s11)
        cur, nxt = nxt, cur
        last_chunk_gate = s11
    p.fence()
    p.phase_end()


def ph_gate(p):
    p.phase_begin()
    yf = p.scratch("y_fm", [12, 128, T], role="r")
    xz = p.scratch("xz_fm", [24, 128, T], role="r")
    out = p.scratch("yg_fm", [12, 128, T], role="w")
    W = 512
    a = p.alloc(W)
    b = p.alloc(W)
    g2t = p.alloc(W)
    last_gs = None
    for c in range(12):
        for it in range(T // W):
            if last_gs:
                p.wait_t("sync", last_gs)
            g1 = p.ldgroup()
            p.dma_g("sync", a, yf[c, :, it * W:(it + 1) * W], g1)
            p.dma_g("sync", b, xz[12 + c, :, it * W:(it + 1) * W], g1)
            p.wait_t("scalar", g1)
            c1 = p.op("scalar", "act", lambda h, o=b, i=b: h.activation(o, i, AF.Silu))
            p.wait("vector", "act", c1)
            c2 = p.op("vector", "dve", lambda h, o=g2t, x=a, y=b: h.tensor_mul(o, x, y))
            p.wait("scalar", "dve", c2)
            last_gs = p.dma("scalar", out[c, :, it * W:(it + 1) * W], g2t, "st")
            p.wait_t("scalar", last_gs)
    p.fence()
    p.phase_end()


def ph_router(p):
    p.phase_begin()
    xn = p.scratch("xn1", [NCH, 128, T], role="r")
    rw = p.inp("L1_ffn_router_w", [D, NE])
    iota8 = p.inp("IOTA8", [128, NE])
    iotac = p.inp("IOTACOL", [128, 1])
    lstrict = p.inp("LSTRICT", [128, 128])
    ident = p.inp("IDENT", [128, 128])
    cw1h = p.scratch("cw1_hbm", [T], role="w")
    cw2h = p.scratch("cw2_hbm", [T], role="w")
    r1h = p.scratch("r1_hbm", [T], I32, role="w")
    r2h = p.scratch("r2_hbm", [T], I32, role="w")
    gidx = p.scratch("gidx", [NE * CAP], I32, role="w")

    wt = p.alloc(NCH * NE)
    xt = p.alloc(NCH * 128)
    sc8 = p.alloc(128)
    ptok = p.alloc(NE)
    io8 = p.alloc(NE)
    ioc = p.alloc(4)
    lst = p.alloc(128)
    msk = p.alloc(2 * NE, I32)
    m1c = p.alloc(16)
    cw = p.alloc(NE)
    cwT = p.alloc(128)
    ohf = p.alloc(NE)
    gpos = p.alloc(2 * NE)
    gm = p.alloc(NE)
    base = p.alloc(3 * NE)
    cur_base = 0
    zt = p.alloc(128)
    tokid = p.alloc(8)
    addr = p.alloc(8, I32)
    pb = p.psum_banks()

    g_w = p.ldgroup()
    for k in range(NCH):
        p.dma_g("sync", wt[:, k * NE:(k + 1) * NE], rw[k * 128:(k + 1) * 128, :], g_w)
    p.dma_g("sync", io8[:, 0:NE], iota8[:, :], g_w)
    p.dma_g("sync", ioc[:, 0:1], iotac[:, :], g_w)
    p.dma_g("sync", lst, lstrict[:, :], g_w)
    idt = p.alloc(128)
    p.dma_g("sync", idt, ident[:, :], g_w)
    p.wait_t("tensor", g_w)
    p.wait_t("vector", g_w)
    czp = p.op("vector", "dve", lambda h, o=zt.bitcast(I32): h.memset(o, 0))
    czb = p.op("vector", "dve", lambda h, o=base[0:1, 0:2 * NE]: h.memset(o, 0.0))
    p.wait("scalar", "dve", czp)
    g_pre = p.dma("scalar", gidx.rearrange("(a b) -> a b", b=128).bitcast(I32),
                  zt.bitcast(I32)[0:NE * CAP // 128, 0:128], "st")
    p.wait_t("gpsimd", g_pre)

    def vop(fn):
        c = p.op("vector", "dve", fn)
        p.wait("vector", "dve", c)
        return c

    last_gs = None
    last_sc = 0
    last_sct = 0
    ntile = T // 128
    for i in range(ntile):
        if last_gs:
            p.wait_t("sync", last_gs)
        if last_sct:
            p.wait("sync", "sct", last_sct)
        g_x = p.ldgroup()
        for k in range(NCH):
            p.dma_g("sync", xt[:, k * 128:(k + 1) * 128],
                    xn[k, :, i * 128:(i + 1) * 128], g_x)
        p.wait_t("tensor", g_x)
        p.use_bank("tensor", 0)
        for k in range(NCH):
            cm = p.op("tensor", "pe", lambda h, o=pb[0][0:NE, 0:128],
                      a=wt[:, k * NE:(k + 1) * NE], b=xt[:, k * 128:(k + 1) * 128],
                      s=(k == 0), e=(k == NCH - 1): h.matmul(o, a, b, start=s, stop=e))
        p.wait("vector", "pe", cm)
        cv = p.op("vector", "dve", lambda h, o=sc8[0:NE, 0:128], a=pb[0][0:NE, 0:128]:
                  h.tensor_copy(o, a))
        p.set_bank_gate(0, "dve", cv)
        p.use_bank("tensor", 1)
        p.wait("tensor", "dve", cv)
        cm2 = p.op("tensor", "pe", lambda h, o=pb[1][0:128, 0:NE], a=sc8[0:NE, 0:128],
                   i=idt: h.transpose(o, a, i[0:NE, 0:NE]))
        p.wait("vector", "pe", cm2)
        cv2 = p.op("vector", "dve", lambda h, o=ptok[:, 0:NE], a=pb[1][0:128, 0:NE]:
                   h.tensor_copy(o, a))
        p.set_bank_gate(1, "dve", cv2)
        p.wait("vector", "dve", cv2)
        c1 = vop(lambda h, o=m1c[:, 0:1], a=ptok[:, 0:NE]:
                 h.reduce_max(o, a, axis=AX.X, negate=True))
        p.wait("scalar", "dve", c1)
        c2 = p.op("scalar", "act", lambda h, o=ptok[:, 0:NE], a=ptok[:, 0:NE],
                  bb=m1c[:, 0:1]: h.activation(o, a, AF.Exp, bias=bb))
        p.wait("vector", "act", c2)
        c3 = vop(lambda h, o=m1c[:, 1:2], a=ptok[:, 0:NE]: h.reduce_sum(o, a, axis=AX.X))
        c4 = vop(lambda h, o=m1c[:, 2:3], a=m1c[:, 1:2]: h.reciprocal(o, a))
        c5 = vop(lambda h, o=ptok[:, 0:NE], a=ptok[:, 0:NE], m=m1c[:, 2:3]:
                 h.tensor_scalar(o, a, m, None, ALU.mult))
        c6 = vop(lambda h, o=m1c[:, 3:4], a=ptok[:, 0:NE]: h.reduce_max(o, a, axis=AX.X))
        c7 = vop(lambda h, o=msk.bitcast(I32)[:, 0:NE], a=ptok[:, 0:NE],
                 m=m1c[:, 3:4]: h.tensor_scalar(o, a, m, None, ALU.is_ge))
        c8a = vop(lambda h, o=cwT[:, 0:NE]: h.memset(o, 8.0))
        c8b = vop(lambda h, o=cw[:, 0:NE], a=msk.bitcast(I32)[:, 0:NE],
                  x=io8[:, 0:NE], y=cwT[:, 0:NE]: h.select(o, a, x, y))
        c9 = vop(lambda h, o=m1c[:, 4:5], a=cw[:, 0:NE]:
                 h.tensor_reduce(o, a, axis=AX.X, op=ALU.min))
        c10 = vop(lambda h, o=msk.bitcast(I32)[:, 0:NE], a=io8[:, 0:NE],
                  m=m1c[:, 4:5]: h.tensor_scalar(o, a, m, None, ALU.is_equal))
        c11 = vop(lambda h, o=cwT[:, 0:NE]: h.memset(o, -1.0))
        c12 = vop(lambda h, o=cw[:, 0:NE], a=msk.bitcast(I32)[:, 0:NE],
                  x=cwT[:, 0:NE], y=ptok[:, 0:NE]: h.select(o, a, x, y))
        c13 = vop(lambda h, o=m1c[:, 5:6], a=cw[:, 0:NE]: h.reduce_max(o, a, axis=AX.X))
        c14 = vop(lambda h, o=msk.bitcast(I32)[:, NE:2 * NE], a=cw[:, 0:NE],
                  m=m1c[:, 5:6]: h.tensor_scalar(o, a, m, None, ALU.is_ge))
        c15a = vop(lambda h, o=cwT[:, 0:NE]: h.memset(o, 8.0))
        c15b = vop(lambda h, o=cw[:, 0:NE], a=msk.bitcast(I32)[:, NE:2 * NE],
                   x=io8[:, 0:NE], y=cwT[:, 0:NE]: h.select(o, a, x, y))
        c16 = vop(lambda h, o=m1c[:, 6:7], a=cw[:, 0:NE]:
                  h.tensor_reduce(o, a, axis=AX.X, op=ALU.min))
        c17 = vop(lambda h, o=msk.bitcast(I32)[:, NE:2 * NE], a=io8[:, 0:NE],
                  m=m1c[:, 6:7]: h.tensor_scalar(o, a, m, None, ALU.is_equal))
        c18 = vop(lambda h, o=m1c[:, 7:8], a=m1c[:, 3:4], b=m1c[:, 5:6]:
                  h.tensor_add(o, a, b))
        c19 = vop(lambda h, o=m1c[:, 8:9], a=m1c[:, 7:8]: h.reciprocal(o, a))
        c20 = vop(lambda h, o=m1c[:, 9:10], a=m1c[:, 3:4], b=m1c[:, 8:9]:
                  h.tensor_mul(o, a, b))
        c21 = vop(lambda h, o=m1c[:, 10:11], a=m1c[:, 5:6], b=m1c[:, 8:9]:
                  h.tensor_mul(o, a, b))
        c22 = vop(lambda h, o=cw[:, 0:NE], a=msk.bitcast(I32)[:, 0:NE]: h.tensor_copy(o, a))
        c23 = vop(lambda h, o=cwT[:, 0:NE], a=msk.bitcast(I32)[:, NE:2 * NE]:
                  h.tensor_copy(o, a))
        c24 = vop(lambda h, o=ohf[:, 0:NE], a=cw[:, 0:NE], b=cwT[:, 0:NE]:
                  h.tensor_add(o, a, b))
        p.use_bank("tensor", 2)
        p.wait("tensor", "dve", c24)
        cm3 = p.op("tensor", "pe", lambda h, o=pb[2][0:128, 0:NE], a=lst,
                   b=ohf[:, 0:NE]: h.matmul(o, a, b, start=True, stop=True))
        p.use_bank("tensor", 3)
        cm4 = p.op("tensor", "pe", lambda h, o=pb[3][0:1, 0:NE], a=cwT[:, 0:1] if False
                   else lst[:, 0:1], b=ohf[:, 0:NE]: h.matmul(o, a, b, start=True, stop=True)) \
            if False else None
        co = vop(lambda h, o=cwT[:, 0:1]: h.memset(o, 1.0))
        p.wait("tensor", "dve", co)
        cm4 = p.op("tensor", "pe", lambda h, o=pb[3][0:1, 0:NE], a=cwT[:, 0:1],
                   b=ohf[:, 0:NE]: h.matmul(o, a, b, start=True, stop=True))
        p.use_bank("tensor", 4)
        p.wait("tensor", "dve", last_sc if last_sc else czb)
        cm5 = p.op("tensor", "pe", lambda h, o=pb[4][0:128, 0:NE], a=cwT[0:1, 0:128],
                   b=base[0:1, 0:NE]: h.matmul(o, a, b, start=True, stop=True)) \
            if False else None
        cor = vop(lambda h, o=cwT[0:1, 0:128]: h.memset(o, 1.0))
        p.wait("tensor", "dve", cor)
        cm5 = p.op("tensor", "pe", lambda h, o=pb[4][0:128, 0:NE], a=cwT[0:1, 0:128],
                   b=base[0:1, cur_base:cur_base + NE]: h.matmul(o, a, b, start=True, stop=True))
        p.wait("vector", "pe", cm5)
        cg1 = vop(lambda h, o=gpos[:, 0:NE], a=pb[2][0:128, 0:NE], b=pb[4][0:128, 0:NE]:
                  h.tensor_add(o, a, b)) if False else None
        cg0 = vop(lambda h, o=gpos[:, NE:2 * NE], a=pb[2][0:128, 0:NE]: h.tensor_copy(o, a))
        p.set_bank_gate(2, "dve", cg0)
        cg1 = vop(lambda h, o=gpos[:, 0:NE], a=pb[4][0:128, 0:NE], b=gpos[:, NE:2 * NE]:
                  h.tensor_add(o, a, b))
        p.set_bank_gate(4, "dve", cg1)
        p.wait("vector", "pe", cm4)
        cb = vop(lambda h, o=base[0:1, NE:2 * NE], a=pb[3][0:1, 0:NE]: h.tensor_copy(o, a))
        p.set_bank_gate(3, "dve", cb)
        nb = 2 * NE if cur_base == 0 else 0
        last_sc = vop(lambda h, o=base[0:1, nb:nb + NE], a=base[0:1, cur_base:cur_base + NE],
                      b=base[0:1, NE:2 * NE]: h.tensor_add(o, a, b))
        cur_base = nb
        c25 = vop(lambda h, o=cw[:, 0:NE], a=msk.bitcast(I32)[:, 0:NE]: h.tensor_copy(o, a))
        c26 = vop(lambda h, o=gm[:, 0:NE], a=cw[:, 0:NE], b=gpos[:, 0:NE]: h.tensor_mul(o, a, b))
        c27 = vop(lambda h, o=m1c[:, 11:12], a=gm[:, 0:NE]: h.reduce_sum(o, a, axis=AX.X))
        c28 = vop(lambda h, o=m1c[:, 11:12], a=m1c[:, 11:12]: h.tensor_scalar(
            o, a, float(CAP - 1), None, ALU.min))
        c29 = vop(lambda h, o=cw[:, 0:NE], a=msk.bitcast(I32)[:, NE:2 * NE]: h.tensor_copy(o, a))
        c30 = vop(lambda h, o=gm[:, 0:NE], a=cw[:, 0:NE], b=gpos[:, 0:NE]: h.tensor_mul(o, a, b))
        c31 = vop(lambda h, o=m1c[:, 12:13], a=gm[:, 0:NE]: h.reduce_sum(o, a, axis=AX.X))
        c32 = vop(lambda h, o=m1c[:, 12:13], a=m1c[:, 12:13]: h.tensor_scalar(
            o, a, float(CAP - 1), None, ALU.min))
        c33 = vop(lambda h, o=m1c[:, 13:14], a=m1c[:, 4:5], b=m1c[:, 11:12]:
                  h.scalar_tensor_tensor(o, a, float(CAP), b, ALU.mult, ALU.add))
        c34 = vop(lambda h, o=addr.bitcast(I32)[:, 0:1], a=m1c[:, 13:14]: h.tensor_copy(o, a))
        c35 = vop(lambda h, o=m1c[:, 14:15], a=m1c[:, 6:7], b=m1c[:, 12:13]:
                  h.scalar_tensor_tensor(o, a, float(CAP), b, ALU.mult, ALU.add))
        c36 = vop(lambda h, o=addr.bitcast(I32)[:, 1:2], a=m1c[:, 14:15]: h.tensor_copy(o, a))
        c37 = vop(lambda h, o=tokid[:, 0:1], a=ioc[:, 0:1], im=float(i * 128):
                  h.tensor_scalar(o, a, im, None, ALU.add))
        c38 = vop(lambda h, o=tokid.bitcast(I32)[:, 1:2], a=tokid[:, 0:1]: h.tensor_copy(o, a))
        p.wait("gpsimd", "dve", c38)
        p.bump("sct", 16)
        p.emit("gpsimd", lambda h, o=gidx.unsqueeze(1), oa=addr.bitcast(I32)[:, 0:1],
               iv=tokid.bitcast(I32)[:, 1:2]: h.indirect_dma_start(
                   out=o, out_offset=IndirectOffsetOnAxis(ap=oa, axis=0),
                   in_=iv, in_offset=None).then_inc(p.sems["sct"], 16))
        gsc2 = p.bump("sct", 16)
        p.emit("gpsimd", lambda h, o=gidx.unsqueeze(1), oa=addr.bitcast(I32)[:, 1:2],
               iv=tokid.bitcast(I32)[:, 1:2]: h.indirect_dma_start(
                   out=o, out_offset=IndirectOffsetOnAxis(ap=oa, axis=0),
                   in_=iv, in_offset=None).then_inc(p.sems["sct"], 16))
        p.wait("gpsimd", "sct", gsc2)
        last_sct = gsc2
        p.wait("scalar", "dve", c36)
        gs1 = p.dma("scalar", cw1h[i * 128:(i + 1) * 128].unsqueeze(1), m1c[:, 9:10], "st")
        gs2 = p.dma("scalar", cw2h[i * 128:(i + 1) * 128].unsqueeze(1), m1c[:, 10:11], "st")
        gs3 = p.dma("scalar", r1h[i * 128:(i + 1) * 128].unsqueeze(1),
                    addr.bitcast(I32)[:, 0:1], "st")
        gs4 = p.dma("scalar", r2h[i * 128:(i + 1) * 128].unsqueeze(1),
                    addr.bitcast(I32)[:, 1:2], "st")
        p.wait_t("scalar", gs1)
        p.wait_t("scalar", gs2)
        p.wait_t("scalar", gs3)
        p.wait_t("scalar", gs4)
        last_gs = gs4
    p.fence()
    p.phase_end()


def ph_moe_sparse(p, e):
    p.phase_begin()
    xtokh = p.scratch("xn1_tok", [T, D], role="r")
    wg = p.inp("L1_ffn_gate_proj", [NE, D, FE])
    wu = p.inp("L1_ffn_up_proj", [NE, D, FE])
    wd = p.inp("L1_ffn_down_proj", [NE, FE, D])
    gidx = p.scratch("gidx", [NE * CAP], I32, role="r")
    hh = p.scratch("moe_h", [FE // 128, 128, CAP], role="w")
    dtok = p.scratch("dtok", [NE * CAP, D], role="w")
    ident = p.inp("IDENT", [128, 128])

    nk = NCH
    nm = FE // 128
    tb = 128
    ngr = CAP // 128
    ntb = CAP // tb

    idxt = p.alloc(ngr, I32)
    idt = p.alloc(128)
    mark = p._arena_off

    wgt = p.alloc(nk * FE)
    wut = p.alloc(nk * FE)
    xgt = p.alloc(D)
    xgf = p.alloc(nk * tb)
    sg = p.alloc(tb)
    ev = p.alloc(2 * tb)
    pb = p.psum_banks()

    g_w = p.ldgroup()
    for k in range(nk):
        p.dma_g("sync", wgt[:, k * FE:(k + 1) * FE], wg[e, k * 128:(k + 1) * 128, :], g_w)
        p.dma_g("sync", wut[:, k * FE:(k + 1) * FE], wu[e, k * 128:(k + 1) * 128, :], g_w)
    p.dma_g("sync", idt, ident[:, :], g_w)
    g_i = p.ldgroup()
    for j in range(ngr):
        p.dma_g("sync", idxt[:, j:j + 1], gidx[e * CAP + j * 128:e * CAP + (j + 1) * 128]
                .unsqueeze(1), g_i)
    p.wait_t("gpsimd", g_i)
    p.wait_t("tensor", g_w)

    last_tr = 0
    last_mm = 0
    gs_slot = [None, None]
    for it in range(ntb):
        j = it
        if last_tr:
            p.wait("gpsimd", "pe", last_tr)
        gg = p.bump("gth", 16)
        p.emit("gpsimd", lambda h, o=xgt, ia=idxt[:, j:j + 1]:
               h.indirect_dma_start(
                   out=o, out_offset=None, in_=xtokh,
                   in_offset=IndirectOffsetOnAxis(ap=ia, axis=0)).then_inc(
                   p.sems["gth"], 16))
        p.wait("tensor", "gth", gg)
        for c in range(nk):
            bidx = 5 + (c % 2)
            bank = pb[bidx]
            p.use_bank("tensor", bidx)
            cm = p.op("tensor", "pe", lambda h, o=bank[0:128, 0:128],
                      a=xgt[:, c * 128:(c + 1) * 128], i=idt: h.transpose(o, a, i))
            p.wait("vector", "pe", cm)
            cd = p.op("vector", "dve", lambda h,
                      o=xgf[:, c * tb:(c + 1) * tb],
                      a=bank[0:128, 0:128]: h.tensor_copy(o, a))
            p.set_bank_gate(bidx, "dve", cd)
        last_tr = cm
        p.wait("tensor", "dve", cd)
        for m in range(nm):
            bg = m % 2
            bu = 2 + m % 2
            p.use_bank("tensor", bg)
            for k in range(nk):
                cm = p.op("tensor", "pe", lambda h, o=pb[bg][0:128, 0:tb],
                          a=wgt[:, k * FE + m * 128:k * FE + m * 128 + 128],
                          b=xgf[:, k * tb:(k + 1) * tb], s=(k == 0), e2=(k == nk - 1):
                          h.matmul(o, a, b, start=s, stop=e2))
            p.use_bank("tensor", bu)
            for k in range(nk):
                cm = p.op("tensor", "pe", lambda h, o=pb[bu][0:128, 0:tb],
                          a=wut[:, k * FE + m * 128:k * FE + m * 128 + 128],
                          b=xgf[:, k * tb:(k + 1) * tb], s=(k == 0), e2=(k == nk - 1):
                          h.matmul(o, a, b, start=s, stop=e2))
            last_mm = cm
            p.wait("scalar", "pe", cm)
            c1 = p.op("scalar", "act", lambda h, o=sg, a=pb[bg][0:128, 0:tb]:
                      h.activation(o, a, AF.Silu))
            p.set_bank_gate(bg, "act", c1)
            p.wait("vector", "act", c1)
            slot = m % 2
            if gs_slot[slot]:
                p.wait_t("vector", gs_slot[slot])
            c2 = p.op("vector", "dve", lambda h, o=ev[:, slot * tb:(slot + 1) * tb],
                      a=pb[bu][0:128, 0:tb], b=sg: h.tensor_mul(o, a, b))
            p.set_bank_gate(bu, "dve", c2)
            p.wait("scalar", "dve", c2)
            gs_slot[slot] = p.dma("scalar", hh[m, :, it * tb:(it + 1) * tb],
                                  ev[:, slot * tb:(slot + 1) * tb], "st")
        p.wait("sync", "pe", last_mm)
        p.wait("gpsimd", "pe", last_mm)
    for gslt in gs_slot:
        if gslt:
            p.wait_t("scalar", gslt)
    p.fence()
    p._arena_off = mark
    wdt = p.alloc(nm * D)
    ht = p.alloc(nm * 128)
    dstg = p.alloc(D)
    ev4 = p.alloc(4 * 128)
    g_w2 = p.ldgroup()
    for k in range(nm):
        p.dma_g("sync", wdt[:, k * D:(k + 1) * D], wd[e, k * 128:(k + 1) * 128, :], g_w2)
    p.wait_t("tensor", g_w2)
    gs_last = None
    ev4_gate = [0, 0, 0, 0]
    last_dn = 0
    for jt in range(ngr):
        if last_dn:
            p.wait("sync", "pe", last_dn)
        g_h = p.ldgroup()
        for k in range(nm):
            p.dma_g("sync", ht[:, k * 128:(k + 1) * 128],
                    hh[k, :, jt * 128:(jt + 1) * 128], g_h)
        p.wait_t("tensor", g_h)
        for md in range(16):
            bidx = md % 2
            bank = pb[bidx]
            p.use_bank("tensor", bidx)
            for k in range(nm):
                cm = p.op("tensor", "pe", lambda h, o=bank[0:128, 0:128],
                          a=wdt[:, k * D + md * 128:k * D + md * 128 + 128],
                          b=ht[:, k * 128:(k + 1) * 128],
                          s=(k == 0), e2=(k == nm - 1):
                          h.matmul(o, a, b, start=s, stop=e2))
            last_dn = cm
            slot = md % 4
            p.wait("vector", "pe", cm)
            if ev4_gate[slot]:
                p.wait("vector", "pe", ev4_gate[slot])
            ce = p.op("vector", "dve", lambda h, o=ev4[:, slot * 128:(slot + 1) * 128],
                      a=bank[0:128, 0:128]: h.tensor_copy(o, a))
            p.set_bank_gate(bidx, "dve", ce)
            bidx2 = 5 + (md % 2)
            bank2 = pb[bidx2]
            p.use_bank("tensor", bidx2)
            p.wait("tensor", "dve", ce)
            cm2 = p.op("tensor", "pe", lambda h, o=bank2[0:128, 0:128],
                       a=ev4[:, slot * 128:(slot + 1) * 128], i=idt: h.transpose(o, a, i))
            ev4_gate[slot] = cm2
            p.wait("scalar", "pe", cm2)
            cd2 = p.op("scalar", "act", lambda h, o=dstg[:, md * 128:(md + 1) * 128],
                       a=bank2[0:128, 0:128]: h.copy(o, a))
            p.set_bank_gate(bidx2, "act", cd2)
        p.wait("scalar", "act", cd2)
        if gs_last:
            p.wait_t("scalar", gs_last)
        gs_last = p.dma("scalar", dtok[e * CAP + jt * 128:e * CAP + (jt + 1) * 128, :],
                        dstg, "st")
        p.wait_t("scalar", gs_last)
    p.fence()
    p.phase_end()


def ph_combine(p):
    p.phase_begin()
    dtok = p.scratch("dtok", [NE * CAP, D], role="r")
    cw1h = p.scratch("cw1_hbm", [T], role="r")
    cw2h = p.scratch("cw2_hbm", [T], role="r")
    r1h = p.scratch("r1_hbm", [T], I32, role="r")
    r2h = p.scratch("r2_hbm", [T], I32, role="r")
    out = p.scratch("moe_tok", [T, D], role="w")

    a = p.alloc(D)
    b = p.alloc(D)
    ot = p.alloc(D)
    cols = p.alloc(8)
    ridx = p.alloc(4, I32)
    last_gs = None
    ntile = T // 128
    for i in range(ntile):
        if last_gs:
            p.wait_t("sync", last_gs)
            p.wait_t("gpsimd", last_gs)
        g_c = p.ldgroup()
        p.dma_g("sync", cols[:, 0:1], cw1h[i * 128:(i + 1) * 128].unsqueeze(1), g_c)
        p.dma_g("sync", cols[:, 1:2], cw2h[i * 128:(i + 1) * 128].unsqueeze(1), g_c)
        p.dma_g("sync", ridx[:, 0:1], r1h[i * 128:(i + 1) * 128].unsqueeze(1), g_c)
        p.dma_g("sync", ridx[:, 1:2], r2h[i * 128:(i + 1) * 128].unsqueeze(1), g_c)
        p.wait_t("gpsimd", g_c)
        p.bump("gth", 16)
        p.emit("gpsimd", lambda h, o=a, ia=ridx[:, 0:1]: h.indirect_dma_start(
            out=o, out_offset=None, in_=dtok,
            in_offset=IndirectOffsetOnAxis(ap=ia, axis=0)).then_inc(p.sems["gth"], 16))
        g2 = p.bump("gth", 16)
        p.emit("gpsimd", lambda h, o=b, ia=ridx[:, 1:2]: h.indirect_dma_start(
            out=o, out_offset=None, in_=dtok,
            in_offset=IndirectOffsetOnAxis(ap=ia, axis=0)).then_inc(p.sems["gth"], 16))
        p.wait("vector", "gth", g2)
        p.wait_t("vector", g_c)
        c1 = p.op("vector", "dve", lambda h, o=a, x=a, m=cols[:, 0:1]:
                  h.tensor_scalar(o, x, m, None, ALU.mult))
        p.wait("vector", "dve", c1)
        c2 = p.op("vector", "dve", lambda h, o=ot, x=b, m=cols[:, 1:2], y=a:
                  h.scalar_tensor_tensor(o, x, m, y, ALU.mult, ALU.add))
        p.wait("gpsimd", "dve", c2)
        last_gs = p.dma("gpsimd", out[i * 128:(i + 1) * 128, :], ot, "st")
        p.wait_t("gpsimd", last_gs)
    p.fence()
    p.phase_end()


def run_phases(phases, inputs_by_core, fused=False, name="prog", n_cores=8):
    p = Prog(name, fused_scratch=fused)
    for ph in phases:
        ph(p)
    nc = p.finalize()
    in_names = [n for n, k in p.hbm_kind.items() if k == "in"]
    in_maps = []
    for core in range(n_cores):
        m = {n: inputs_by_core[core][n] for n in in_names}
        in_maps.append(m)
    res = run_bass_kernel_spmd(nc, in_maps, core_ids=list(range(n_cores)))
    return res.results


def l0_phases():
    return [
        lambda p: ph_ln(p, "x", "xn0"),
        lambda p: ph_matmul_fm(p, "L0_attn_Wq", [D, 2048], "xn0", "q_fm", 2048),
        lambda p: ph_matmul_fm(p, "L0_attn_Wk", [D, 512], "xn0", "k_fm", 512),
        lambda p: ph_matmul_fm(p, "L0_attn_Wv", [D, 512], "xn0", "v_fm", 512),
        lambda p: ph_attn(p),
        lambda p: ph_matmul_fm(p, "L0_attn_Wo", [2048, D], "attn_o", "wo_out", D),
        lambda p: ph_deq(p),
        lambda p: ph_ffn(p),
        lambda p: ph_merge(p, "x1_tok", "x", [("wo_out", None), ("ffn_out", None)],
                           base_is_input=True),
    ]


def l1_phases():
    return [
        lambda p: ph_ln(p, "x1_tok", "xn1", want_rms="xr1", also_tok="xn1_tok"),
        lambda p: ph_matmul_fm(p, "L1_attn_in_proj_w", [2 * DI, D], "xr1", "xz_fm",
                               2 * DI, transpose_w=True, bias_name="L1_attn_in_proj_b",
                               rhs_scale_name="L1_attn_norm_w", mslice=(0, DI)),
        lambda p: ph_matmul_fm(p, "L1_attn_in_proj_w", [2 * DI, D], "xr1", "xz_fm",
                               2 * DI, transpose_w=True, bias_name="L1_attn_in_proj_b",
                               rhs_scale_name="L1_attn_norm_w", mslice=(DI, DI)),
        lambda p: ph_conv(p),
        lambda p: ph_matmul_fm(p, "L1_attn_x_proj_w", [PD, DI], "xconv", "xp_fm",
                               PD, transpose_w=True, bias_name="L1_attn_x_proj_b"),
        lambda p: ph_scanprep1(p),
        lambda p: ph_scanprep2(p),
        lambda p: ph_scan(p),
        lambda p: ph_gate(p),
        lambda p: ph_matmul_fm(p, "L1_attn_out_proj_w", [D, DI], "yg_fm", "op_out",
                               D, transpose_w=True, bias_name="L1_attn_out_proj_b"),
        lambda p: ph_router(p),
    ] + [(lambda e: lambda p: ph_moe_sparse(p, e))(e) for e in range(NE)] + [
        lambda p: ph_combine(p),
        lambda p: ph_merge(p, "x2_tok", "x1_tok",
                           [("xn1", None), ("op_out", "rg"), ("moe_tok", "tok")],
                           base_is_input=False),
    ]


FUSED = True


def kernel(**inputs):
    consts = host_constants()
    base = dict(consts)
    for k, v in inputs.items():
        if k != "L0_attn_x":
            base[k] = np.ascontiguousarray(np.asarray(v))
    xfull = np.asarray(inputs["L0_attn_x"])
    percore = []
    for b in range(8):
        m = dict(base)
        m["x"] = np.ascontiguousarray(xfull[b])
        percore.append(m)

    if FUSED:
        res = run_phases(l0_phases() + l1_phases(), percore, fused=True,
                         name="mk_full", n_cores=8)
        out = np.stack([res[b]["x2_tok"] for b in range(8)], axis=0)
        return out.astype(np.float32)
    res_a = run_phases(l0_phases(), percore, name="mk_l0", n_cores=8)
    for b in range(8):
        percore[b]["x1_tok"] = np.ascontiguousarray(res_a[b]["x1_tok"])
    res_b = run_phases(l1_phases(), percore, name="mk_l1", n_cores=8)
    out = np.stack([res_b[b]["x2_tok"] for b in range(8)], axis=0)
    return out.astype(np.float32)
